# version 01
# baseline (speedup 1.0000x reference)
import numpy as np
from contextlib import ExitStack

import concourse.bass as bass
import concourse.mybir as mybir
from concourse.bass_utils import run_bass_kernel_spmd

F32 = mybir.dt.float32
BF16 = mybir.dt.bfloat16
AF = mybir.ActivationFunctionType
ALU = mybir.AluOpType
AX = mybir.AxisListType

MAGIC = float(np.float32(1.5 * 2.0 ** 23))
LN_EPS = 1e-5
D = 1024
DI = 2048
DSL = 512
NDB = 4
DS = 64
KC = 4
G2 = [[0, 1, 2, 3], [4, 5, 6, 7]]
G1 = [[c] for c in range(8)]

ENGS = ("sync", "scalar", "vector", "tensor", "gpsimd")


class Plan:
    def __init__(self):
        self.seq = {e: [] for e in ENGS}
        self.cnt = {}

    def sem(self, name):
        self.cnt.setdefault(name, 0)

    SELF_SEM = {"vector": "dve", "scalar": "act"}

    def op(self, eng, build, inc=None, inck=1, wait=()):
        waits = []
        if eng in self.SELF_SEM:
            sn = self.SELF_SEM[eng]
            waits.append((sn, self.cnt[sn]))
        for w in wait:
            if isinstance(w, tuple):
                waits.append(w)
            else:
                waits.append((w, self.cnt[w]))
        incs = None
        if inc is not None:
            self.cnt[inc] += inck
            incs = (inc, inck)

        def fn(e, H, waits=tuple(waits), build=build, incs=incs):
            for s, v in waits:
                if v > 0:
                    e.wait_ge(H[s], v)
            ins = build(e)
            if incs is not None:
                ins.then_inc(H[incs[0]], incs[1])
        self.seq[eng].append(fn)


DBG = False


def build_nc(L, Q):
    NCH = L // Q
    QT = Q // 128
    NQ4 = Q // 4
    OWN = NCH * NQ4
    NB4 = OWN // 512
    assert L % Q == 0 and Q == 1024 and NQ4 % 128 == 0 and OWN % 512 == 0

    nc = bass.Bass("TRN2", target_bir_lowering=False, debug=False,
                   detect_race_conditions=False)

    def din(name, shape, dt=F32):
        return nc.dram_tensor(name, shape, dt, kind="ExternalInput").ap()

    x_b = din("x_b", [L, D])
    x_own = din("x_own", [OWN, D])
    w_in_xs = din("w_in_xs", [128, 8, DSL])
    w_in_z = din("w_in_z", [128, 8, DSL])
    w_xp = din("w_xp", [128, NDB, 129])
    conv_w4 = din("conv_w4", [128, NDB, KC])
    conv_b4 = din("conv_b4", [128, NDB])
    dtw4 = din("dtw4", [128, NDB])
    dtb4 = din("dtb4", [128, NDB])
    alog4 = din("alog4", [128, NDB, DS])
    d4 = din("d4", [128, NDB])
    w_out_sl = din("w_out_sl", [128, NDB, 8, 128])
    wg_t = din("wg_t", [128, 32, 8, 128])
    wu_t = din("wu_t", [128, 32, 8, 128])
    wd_t = din("wd_t", [128, 8, 32, 128])
    c_row = din("c_row", [1, D])
    wmod_r = din("wmod_r", [128, 8, 6 * D])
    ident_h = din("ident_h", [128, 128], BF16)

    y_out = nc.dram_tensor("y_out", [OWN, D], F32, kind="ExternalOutput").ap()
    dbg = {}
    if DBG:
        for nm, shape, dt in [
                ("dbg_hT", [128, 8 * Q], BF16), ("dbg_xsr", [128, NDB * (KC + Q)], BF16),
                ("dbg_xssi", [128, NDB * Q], BF16), ("dbg_bct", [128, Q], F32),
                ("dbg_dtr", [1, Q], F32), ("dbg_dt", [128, NDB * Q], F32),
                ("dbg_u", [128, NDB * Q], BF16), ("dbg_y", [128, NDB * Q], F32),
                ("dbg_yz", [128, NDB * Q], BF16), ("dbg_carry", [128, NDB * DS], F32),
                ("dbg_ca1", [128, D], BF16), ("dbg_sa", [128, D], BF16),
                ("dbg_ga", [128, D], BF16), ("dbg_aq", [1024, NQ4], F32),
                ("dbg_z", [128, NDB * Q], BF16), ("dbg_asl", [128, NDB * DS], F32),
                ("dbg_xt", [128, D], F32), ("dbg_cols", [128, 6], F32),
                ("dbg_hth", [128, D], BF16), ("dbg_hth2", [128, D], BF16)]:
            dbg[nm] = nc.dram_tensor(nm, shape, dt, kind="ExternalOutput").ap()

    dscr = lambda n, s, dt=F32: nc.dram_tensor(n, s, dt).ap()
    xp_part = [dscr(f"xp_part{c}", [129, Q]) for c in range(NCH)]
    xp_red = [dscr(f"xp_red{c}", [129, Q]) for c in range(NCH)]
    po_d = [dscr(f"po_d{c}", [4 * 1024, NQ4]) for c in range(NCH)]
    attn_q = [dscr(f"attn_q{c}", [1024, NQ4]) for c in range(NCH)]
    wq_g_d = dscr("wq_g_d", [128, 32, 8, 128], BF16)
    wq_u_d = dscr("wq_u_d", [128, 32, 8, 128], BF16)
    wq_d_d = dscr("wq_d_d", [128, 8, 32, 128], BF16)
    x2_d = dscr("x2_d", [OWN, D])
    q1l_in = dscr("q1l_in", [OWN, D], BF16)
    q1l_out = dscr("q1l_out", [OWN, D], BF16)
    q2l_in = [dscr(f"q2l_in{b}", [4096, 512], BF16) for b in range(NB4)]
    q2l_out = [dscr(f"q2l_out{b}", [4096, 512], BF16) for b in range(NB4)]
    swb_d = dscr("swb_d", [1, 4])

    p = Plan()
    for s in ("ld", "gp", "te", "dve", "act", "gpc"):
        p.sem(s)
    cc_names = []

    def new_cc():
        n = f"cc{len(cc_names)}"
        cc_names.append(n)
        p.sem(n)
        return n

    with ExitStack() as ctx:
        sb = lambda n, s, dt=F32: ctx.enter_context(nc.sbuf_tensor(n, s, dt))
        psum = lambda n, s, dt=F32: ctx.enter_context(nc.psum_tensor(n, s, dt))

        wxs_h = sb("wxs_h", [128, 8, DSL], BF16)
        wz_h = sb("wz_h", [128, 8, DSL], BF16)
        wxp_h = sb("wxp_h", [128, NDB, 129], BF16)
        wout_h = sb("wout_h", [128, NDB, 8, 128], BF16)
        convw = sb("convw", [128, NDB, KC])
        convb = sb("convb", [128, NDB])
        dtw = sb("dtw", [128, NDB])
        dtb = sb("dtb", [128, NDB])
        a_sl = sb("a_sl", [128, NDB, DS])
        d_col = sb("d_col", [128, NDB])
        ident = sb("ident", [128, 128], BF16)
        ident_f = sb("ident_f", [128, 128])
        ones_f = sb("ones_f", [1, 128])
        onesc_f = sb("onesc_f", [128, 1])
        reps = {}
        for nmm in ("ca1", "sa", "ga", "cb1", "sbm", "gb"):
            reps[nmm] = sb("rep_" + nmm, [128, D], BF16)
        carry = sb("carry", [128, NDB, DS])
        halo = sb("halo", [128, NDB, KC - 1], BF16)
        silc_f = sb("silc_f", [128, 8])
        sw3 = sb("sw3", [128, 4])
        swq = sb("swq", [128, 1])
        cols = [sb(f"col{i}", [128, 1]) for i in range(6)]
        rows4 = sb("rows4", [1, 16])
        cst_eps = sb("cst_eps", [128, 1])
        cst_neg1 = sb("cst_neg1", [128, 1])
        cst_one = sb("cst_one", [128, 1])

        big_a = sb("big_a", [128, NDB * Q])
        big_b = sb("big_b", [128, NDB * Q])
        big_c = sb("big_c", [128, NDB * Q])
        big_d = sb("big_d", [128, NDB * Q])
        xt = sb("xt", [128, D])
        bct = sb("bct", [128, Q])
        hT = sb("hT", [128, 8 * Q], BF16)
        NXS = NDB * (KC + Q)
        bfB = sb("bfB", [128, NXS + 4 * NDB * Q + 5 * Q], BF16)
        o = 0
        def carve(n):
            nonlocal o
            v = bfB[:, o:o + n]
            o += n
            return v
        xsr = carve(NXS).rearrange("p (a q) -> p a q", a=NDB)
        xssi = carve(NDB * Q)
        z_h = carve(NDB * Q)
        u_h = carve(NDB * Q)
        inj = carve(NDB * Q)
        bb = [carve(Q) for i in range(2)]
        cbb = [carve(Q) for i in range(2)]
        bc_h = carve(Q)
        ht_h = sb("ht_h", [128, D], BF16)
        ht_h2 = sb("ht_h2", [128, D], BF16)
        gu16 = bfB[:, 0:32 * 512]
        wstr16 = bfB[:, 32 * 512:32 * 512 + 32 * 128]
        gtl = bfB[:, 20480:20992]
        gtl2 = bfB[:, 20992:21504]
        utl = bfB[:, 21504:22016]
        ps_a = psum("ps_a", [128, 512])
        ps_b = psum("ps_b", [128, 512])
        ps_r = psum("ps_r", [128, 512])
        ps_t = psum("ps_t", [128, 128])
        ps_th = psum("ps_th", [128, 128], BF16)

        dt_t = big_a[:].rearrange("p (a q) -> p a q", a=NDB)
        tmp_a = big_b[:].rearrange("p (a q) -> p a q", a=NDB)
        dAv = big_c[:].rearrange("p (a q) -> p a q", a=NDB)
        yacc = big_d[:].rearrange("p (a q) -> p a q", a=NDB)
        xssi3 = xssi.rearrange("p (a q) -> p a q", a=NDB)
        z3 = z_h.rearrange("p (a q) -> p a q", a=NDB)
        u3 = u_h.rearrange("p (a q) -> p a q", a=NDB)
        inj3 = inj.rearrange("p (a q) -> p a q", a=NDB)
        hT3 = hT[:].rearrange("p (k q) -> p k q", k=8)
        gu3 = gu16.rearrange("p (m t) -> p m t", m=32)
        wst3 = wstr16.rearrange("p (k m) -> p k m", m=128)
        ffn8 = big_d[:].rearrange("p (m t) -> p m t", m=8)

        def bcast3(t, n):
            return t.unsqueeze(1).broadcast_to([128, n, t.shape[-1]])

        def setup():
            p.op("vector", lambda e: e.memset(ones_f[:], 1.0), inc="dve")
            p.op("vector", lambda e: e.memset(cst_eps[:], LN_EPS), inc="dve")
            p.op("vector", lambda e: e.memset(cst_neg1[:], -1.0), inc="dve")
            p.op("vector", lambda e: e.memset(cst_one[:], 1.0), inc="dve")
            p.op("vector", lambda e: e.memset(onesc_f[:], 1.0), inc="dve")
            p.op("vector", lambda e: e.memset(carry[:], 0.0), inc="dve")
            p.op("vector", lambda e: e.memset(halo[:], 0.0), inc="dve")
            for dst, src in [(convw[:], conv_w4), (convb[:], conv_b4),
                             (dtw[:], dtw4), (dtb[:], dtb4), (d_col[:], d4),
                             (ident[:], ident_h)]:
                p.op("sync", lambda e, d=dst, s=src: e.dma_start(out=d, in_=s),
                     inc="ld", inck=16)

            def stage_cast(dst_h, src, nelem):
                st32 = big_a[:, :nelem]
                p.op("sync", lambda e, s=src, dd=st32: e.dma_start(out=dd, in_=s),
                     inc="ld", inck=16, wait=("dve",))
                p.op("vector", lambda e, dd=dst_h, s=st32: e.tensor_copy(
                    out=dd, in_=s), inc="dve", wait=("ld",))
            stage_cast(wxs_h[:].rearrange("p a b -> p (a b)"), w_in_xs, 8 * DSL)
            stage_cast(wz_h[:].rearrange("p a b -> p (a b)"), w_in_z, 8 * DSL)
            stage_cast(wxp_h[:].rearrange("p a b -> p (a b)"), w_xp, NDB * 129)
            stage_cast(wout_h[:].rearrange("p a b c -> p (a b c)"), w_out_sl,
                       NDB * 8 * 128)
            p.op("vector", lambda e: e.tensor_copy(out=ident_f[:],
                                                   in_=ident[:]),
                 inc="dve", wait=("ld",))
            asl_f = a_sl[:].rearrange("p a b -> p (a b)")
            p.op("sync", lambda e: e.dma_start(out=asl_f, in_=alog4),
                 inc="ld", inck=16, wait=("dve",))
            p.op("scalar", lambda e: e.activation(
                out=big_a[:, :NDB * DS], in_=asl_f, func=AF.Exp),
                inc="act", wait=("ld",))
            p.op("vector", lambda e: e.tensor_scalar_mul(
                out=asl_f, in0=big_a[:, :NDB * DS], scalar1=-1.0),
                inc="dve", wait=("act",))

            crow_t = xt[0:1, :]
            p.op("sync", lambda e: e.dma_start(out=crow_t, in_=c_row),
                 inc="ld", inck=16)
            csil = bct[0:1, :D]
            p.op("scalar", lambda e: e.activation(out=csil, in_=crow_t,
                                                  func=AF.Silu),
                 inc="act", wait=("ld",))
            for kc in range(8):
                p.op("tensor", lambda e, kc=kc: e.transpose(
                    out=ps_t[:, 0:1], in_=csil[:, kc * 128:(kc + 1) * 128],
                    identity=ident_f[0:1, 0:1]), inc="te",
                    wait=("dve", "ld", "act"))
                p.op("vector", lambda e, kc=kc: e.tensor_copy(
                    out=silc_f[:, kc:kc + 1], in_=ps_t[:, 0:1]),
                    inc="dve", wait=("te",))
            sa_r = big_a[0:1, 0 * D:1 * D]; ca_r = big_a[0:1, 1 * D:2 * D]
            ga_r = big_a[0:1, 2 * D:3 * D]; sb_r = big_a[0:1, 3 * D:4 * D]
            cb_r = big_c[0:1, 0 * D:1 * D]; gb_r = big_c[0:1, 1 * D:2 * D]
            segs = [sa_r, ca_r, ga_r, sb_r, cb_r, gb_r]
            for nch in range(12):
                st32 = big_b[:, :8 * 512]
                p.op("sync", lambda e, n=nch, dd=st32: e.dma_start(
                    out=dd.rearrange("p (k c) -> p k c", k=8),
                    in_=wmod_r[:, :, n * 512:(n + 1) * 512]),
                    inc="ld", inck=16, wait=("dve", "te"))
                for kc in range(8):
                    p.op("tensor", lambda e, kc=kc, s=st32: e.matmul(
                        out=ps_r[0:1, :], lhsT=silc_f[:, kc:kc + 1],
                        rhs=s.rearrange("p (k c) -> p k c", k=8)[:, kc, :],
                        start=(kc == 0), stop=(kc == 7)),
                        inc="te" if kc == 7 else None,
                        wait=("dve", "ld") if kc == 0 else ())
                tgt = segs[nch // 2][:, (nch % 2) * 512:(nch % 2) * 512 + 512]
                p.op("vector", lambda e, t=tgt: e.tensor_copy(
                    out=t, in_=ps_r[0:1, :]), inc="dve", wait=("te",))
            t1 = big_c[0:1, 2 * D:3 * D]
            t2 = big_c[0:1, 3 * D:4 * D]
            for g_r in (ga_r, gb_r):
                p.op("vector", lambda e, g=g_r: e.tensor_scalar(
                    out=t1, in0=g, scalar1=0.0, scalar2=2.0,
                    op0=ALU.is_ge, op1=ALU.mult), inc="dve", wait=("te",))
                p.op("vector", lambda e: e.tensor_single_scalar(
                    out=t1, in_=t1, scalar=1.0, op=ALU.subtract), inc="dve")
                p.op("vector", lambda e, g=g_r: e.tensor_scalar_mul(
                    out=t2, in0=g, scalar1=-1.0), inc="dve")
                p.op("vector", lambda e, g=g_r: e.tensor_tensor(
                    out=t2, in0=t2, in1=g, op=ALU.max), inc="dve")
                p.op("vector", lambda e: e.tensor_single_scalar(
                    out=t2, in_=t2, scalar=0.25, op=ALU.max), inc="dve")
                p.op("vector", lambda e, g=g_r: e.tensor_tensor(
                    out=g, in0=t1, in1=t2, op=ALU.mult), inc="dve")
            for c_r in (ca_r, cb_r):
                p.op("vector", lambda e, cr=c_r: e.tensor_single_scalar(
                    out=cr, in_=cr, scalar=1.0, op=ALU.add), inc="dve")
            for nmm, r in (("sa", sa_r), ("ca1", ca_r), ("ga", ga_r),
                           ("sbm", sb_r), ("cb1", cb_r), ("gb", gb_r)):
                for h in range(2):
                    p.op("tensor", lambda e, r=r, h=h: e.matmul(
                        out=ps_r[:], lhsT=ones_f[:],
                        rhs=r[:, h * 512:(h + 1) * 512], start=True, stop=True),
                        inc="te", wait=("dve",))
                    p.op("vector", lambda e, nm=nmm, h=h: e.tensor_copy(
                        out=reps[nm][:, h * 512:(h + 1) * 512], in_=ps_r[:]),
                        inc="dve", wait=("te",))

        def ln1(c):
            for tt in range(QT):
                r0 = c * Q + tt * 128
                p.op("sync", lambda e, r0=r0: e.dma_start(
                    out=xt[:], in_=x_b[r0:r0 + 128, :]),
                    inc="ld", inck=16, wait=("dve", "te"))
                c0, c1, c2, c3, c4, c5 = cols
                p.op("vector", lambda e: e.reduce_sum(
                    c0[:], xt[:], axis=AX.X), inc="dve", wait=("ld",))
                p.op("vector", lambda e: e.tensor_scalar_mul(
                    out=c1[:], in0=c0[:], scalar1=1.0 / D), inc="dve")
                p.op("vector", lambda e: e.tensor_tensor(
                    out=bct[:, :D], in0=xt[:], in1=xt[:], op=ALU.mult),
                    inc="dve", wait=("gp",))
                p.op("vector", lambda e: e.reduce_sum(
                    c2[:], bct[:, :D], axis=AX.X), inc="dve")
                p.op("vector", lambda e: e.tensor_scalar_mul(
                    out=c3[:], in0=c2[:], scalar1=1.0 / D), inc="dve")
                p.op("vector", lambda e: e.tensor_tensor(
                    out=c4[:], in0=c1[:], in1=c1[:], op=ALU.mult), inc="dve")
                p.op("vector", lambda e: e.tensor_tensor(
                    out=c5[:], in0=c3[:], in1=c4[:], op=ALU.subtract),
                    inc="dve")
                p.op("scalar", lambda e: e.activation(
                    out=c0[:], in_=c5[:], func=AF.Sqrt, bias=cst_eps[:]),
                    inc="act", wait=("dve",))
                p.op("vector", lambda e: e.reciprocal(c2[:], c0[:]),
                     inc="dve", wait=("act",))
                p.op("vector", lambda e: e.tensor_scalar(
                    out=ht_h[:], in0=xt[:], scalar1=c1[:], scalar2=c2[:],
                    op0=ALU.subtract, op1=ALU.mult), inc="dve")
                p.op("vector", lambda e: e.tensor_tensor(
                    out=ht_h2[:], in0=ht_h[:], in1=reps["ca1"][:],
                    op=ALU.mult), inc="dve")
                p.op("vector", lambda e: e.tensor_tensor(
                    out=ht_h2[:], in0=ht_h2[:], in1=reps["sa"][:], op=ALU.add),
                    inc="dve")
                for kc in range(8):
                    p.op("tensor", lambda e, kc=kc: e.transpose(
                        out=ps_th[:], in_=ht_h2[:, kc * 128:(kc + 1) * 128],
                        identity=ident[:]), inc="te", wait=("dve",))
                    p.op("vector", lambda e, kc=kc, tt=tt: e.tensor_copy(
                        out=hT3[:, kc, tt * 128:(tt + 1) * 128], in_=ps_th[:]),
                        inc="dve", wait=("te",))

        def in_proj_conv(c):
            for wsel, wtile in ((0, wxs_h), (1, wz_h)):
                for db in range(NDB):
                    for hf in range(2):
                        psd = ps_a if (db + hf) % 2 == 0 else ps_b
                        for kc in range(8):
                            p.op("tensor",
                                 lambda e, db=db, hf=hf, kc=kc, ps=psd, w=wtile:
                                 e.matmul(
                                     out=ps[:],
                                     lhsT=w[:, kc, db * 128:(db + 1) * 128],
                                     rhs=hT3[:, kc, hf * 512:(hf + 1) * 512],
                                     start=(kc == 0), stop=(kc == 7)),
                                 inc="te" if kc == 7 else None,
                                 wait=("dve",) if kc == 0 else ())
                        if wsel == 0:
                            p.op("vector",
                                 lambda e, db=db, hf=hf, ps=psd: e.tensor_copy(
                                     out=xsr[:, db,
                                             KC + hf * 512:KC + (hf + 1) * 512],
                                     in_=ps[:]), inc="dve", wait=("te",))
                        else:
                            p.op("vector",
                                 lambda e, db=db, hf=hf, ps=psd: e.tensor_copy(
                                     out=z3[:, db, hf * 512:(hf + 1) * 512],
                                     in_=ps[:]), inc="dve", wait=("te",))
            p.op("vector", lambda e: e.tensor_copy(
                out=xsr[:, :, 1:KC], in_=halo[:]), inc="dve")
            p.op("vector", lambda e: e.tensor_copy(
                out=halo[:], in_=xsr[:, :, KC + Q - (KC - 1):KC + Q]),
                inc="dve")
            for db in range(NDB):
                p.op("vector", lambda e, db=db: e.tensor_scalar(
                    out=tmp_a[:, db, :], in0=xsr[:, db, 1:1 + Q],
                    scalar1=convw[:, db, 0:1], scalar2=None, op0=ALU.mult),
                    inc="dve", wait=("gp",) if db == 0 else ())
                for k in range(1, KC):
                    p.op("vector", lambda e, db=db, k=k: e.scalar_tensor_tensor(
                        out=tmp_a[:, db, :], in0=xsr[:, db, 1 + k:1 + k + Q],
                        scalar=convw[:, db, k:k + 1], in1=tmp_a[:, db, :],
                        op0=ALU.mult, op1=ALU.add), inc="dve")
                p.op("scalar", lambda e, db=db: e.activation(
                    out=xssi3[:, db, :], in_=tmp_a[:, db, :], func=AF.Silu,
                    bias=convb[:, db:db + 1]), inc="act", wait=("dve",))

        def x_proj_reduce(c):
            for hf in range(2):
                for db in range(NDB):
                    p.op("tensor", lambda e, db=db, hf=hf: e.matmul(
                        out=ps_a[:], lhsT=wxp_h[:, db, 0:128],
                        rhs=xssi3[:, db, hf * 512:(hf + 1) * 512],
                        start=(db == 0), stop=(db == NDB - 1)),
                        inc="te" if db == NDB - 1 else None,
                        wait=("act", "dve") if db == 0 else ())
                p.op("vector", lambda e, hf=hf: e.tensor_copy(
                    out=bct[:, hf * 512:(hf + 1) * 512], in_=ps_a[:]),
                    inc="dve", wait=("te",))
                for db in range(NDB):
                    p.op("tensor", lambda e, db=db, hf=hf: e.matmul(
                        out=ps_b[0:1, :], lhsT=wxp_h[:, db, 128:129],
                        rhs=xssi3[:, db, hf * 512:(hf + 1) * 512],
                        start=(db == 0), stop=(db == NDB - 1)),
                        inc="te" if db == NDB - 1 else None)
                p.op("vector", lambda e, hf=hf: e.tensor_copy(
                    out=xt[0:1, hf * 512:(hf + 1) * 512], in_=ps_b[0:1, :]),
                    inc="dve", wait=("te",))
            p.op("gpsimd", lambda e, c=c: e.dma_start(
                out=xp_part[c][0:128, :], in_=bct[:]), inc="gp", inck=16,
                wait=("dve",))
            p.op("gpsimd", lambda e, c=c: e.dma_start(
                out=xp_part[c][128:129, :], in_=xt[0:1, :Q]),
                inc="gp", inck=16)
            ccn = new_cc()
            p.op("gpsimd", lambda e, c=c: e.collective_compute(
                "AllReduce", ALU.add, replica_groups=G2,
                ins=[xp_part[c]], outs=[xp_red[c]]), inc=ccn, wait=("gp",))
            p.op("sync", lambda e, c=c: e.dma_start(
                out=bct[:], in_=xp_red[c][0:128, :]), inc="ld", inck=16,
                wait=((ccn, 1), "dve"))
            p.op("sync", lambda e, c=c: e.dma_start(
                out=xt[0:1, :Q], in_=xp_red[c][128:129, :]),
                inc="ld", inck=16, wait=("te",))
            p.op("vector", lambda e: e.tensor_copy(out=bc_h, in_=bct[:]),
                 inc="dve", wait=("ld", "gp"))

        def dt_u_build(c):
            for hf in range(2):
                p.op("tensor", lambda e, hf=hf: e.matmul(
                    out=ps_r[:], lhsT=ones_f[:],
                    rhs=xt[0:1, hf * 512:(hf + 1) * 512], start=True,
                    stop=True), inc="te", wait=("dve", "ld"))
                for db in range(NDB):
                    p.op("vector", lambda e, db=db, hf=hf: e.tensor_scalar(
                        out=tmp_a[:, db, hf * 512:(hf + 1) * 512], in0=ps_r[:],
                        scalar1=dtw[:, db:db + 1], scalar2=dtb[:, db:db + 1],
                        op0=ALU.mult, op1=ALU.add), inc="dve",
                        wait=("te", "gp") if db == 0 else ("te",))
            p.op("vector", lambda e: e.tensor_scalar(
                out=dt_t[:], in0=tmp_a[:], scalar1=0.0, scalar2=None,
                op0=ALU.max), inc="dve", wait=("gp",))
            p.op("scalar", lambda e: e.activation(
                out=dAv[:], in_=tmp_a[:], func=AF.Abs), inc="act",
                wait=("dve",))
            p.op("scalar", lambda e: e.activation(
                out=tmp_a[:], in_=dAv[:], func=AF.Exp, scale=cst_neg1[:]),
                inc="act")
            p.op("scalar", lambda e: e.activation(
                out=dAv[:], in_=tmp_a[:], func=AF.Ln, bias=cst_one[:]), inc="act")
            p.op("vector", lambda e: e.tensor_tensor(
                out=dt_t[:], in0=dt_t[:], in1=dAv[:], op=ALU.add),
                inc="dve", wait=("act",))
            p.op("vector", lambda e: e.tensor_tensor(
                out=u_h, in0=big_a[:], in1=xssi, op=ALU.mult),
                inc="dve")

        def scan(c):
            p.op("vector", lambda e: e.memset(big_d[:], 0.0), inc="dve")

            def emit_bcast(n):
                nb = n % 2
                p.op("gpsimd", lambda e, n=n, nb=nb: e.dma_start(
                    out=bb[nb],
                    in_=bc_h[n:n + 1, :].unsqueeze(1)
                    .broadcast_to([1, 128, Q])), inc="gp", inck=16,
                    wait=("dve",))
                p.op("gpsimd", lambda e, n=n, nb=nb: e.dma_start(
                    out=cbb[nb],
                    in_=bc_h[64 + n:65 + n, :].unsqueeze(1)
                    .broadcast_to([1, 128, Q])), inc="gp", inck=16)
            emit_bcast(0)
            for n in range(DS):
                nb = n % 2
                for db in range(NDB):
                    p.op("scalar", lambda e, db=db, n=n: e.activation(
                        out=dAv[:, db, :], in_=dt_t[:, db, :], func=AF.Exp,
                        scale=a_sl[:, db, n:n + 1]),
                        inc="act" if db == NDB - 1 else None,
                        wait=("dve",) if db == 0 else ())
                if n + 1 < DS:
                    emit_bcast(n + 1)
                p.op("vector", lambda e, nb=nb: e.tensor_tensor(
                    out=inj3[:], in0=u3[:], in1=bcast3(bb[nb], NDB),
                    op=ALU.mult), inc="dve",
                    wait=("act", ("gp", p.cnt["gp"] - (32 if n + 1 < DS else 0))))
                for db in range(NDB):
                    p.op("vector", lambda e, db=db, n=n: e.tensor_tensor_scan(
                        out=dAv[:, db, :], data0=dAv[:, db, :],
                        data1=inj3[:, db, :], initial=carry[:, db, n:n + 1],
                        op0=ALU.mult, op1=ALU.add), inc="dve")
                p.op("vector", lambda e, n=n: e.tensor_copy(
                    out=carry[:, :, n:n + 1], in_=dAv[:, :, Q - 1:Q]),
                    inc="dve")
                p.op("vector", lambda e, nb=nb: e.tensor_tensor(
                    out=tmp_a[:], in0=dAv[:], in1=bcast3(cbb[nb], NDB),
                    op=ALU.mult), inc="dve")
                p.op("vector", lambda e: e.tensor_tensor(
                    out=big_d[:], in0=big_d[:], in1=big_b[:], op=ALU.add),
                    inc="dve")
            for db in range(NDB):
                p.op("vector", lambda e, db=db: e.scalar_tensor_tensor(
                    out=tmp_a[:, db, :], in0=xssi3[:, db, :],
                    scalar=d_col[:, db:db + 1], in1=yacc[:, db, :],
                    op0=ALU.mult, op1=ALU.add), inc="dve")
            p.op("scalar", lambda e: e.activation(
                out=u_h, in_=z_h, func=AF.Silu), inc="act",
                wait=("dve",))
            p.op("vector", lambda e: e.tensor_tensor(
                out=z_h, in0=big_b[:], in1=u_h, op=ALU.mult),
                inc="dve", wait=("act",))

        def out_proj(c):
            po_lo = dt_t
            po_hi = tmp_a
            for mt in range(8):
                dst = po_lo if mt < 4 else po_hi
                for hf in range(2):
                    psd = ps_a if (mt + hf) % 2 == 0 else ps_b
                    for db in range(NDB):
                        p.op("tensor", lambda e, mt=mt, hf=hf, db=db, ps=psd:
                             e.matmul(out=ps[:],
                                      lhsT=wout_h[:, db, mt, :],
                                      rhs=z3[:, db, hf * 512:(hf + 1) * 512],
                                      start=(db == 0), stop=(db == NDB - 1)),
                             inc="te" if db == NDB - 1 else None,
                             wait=("dve",) if db == 0 else ())
                    p.op("vector", lambda e, mt=mt, hf=hf, dd=dst, ps=psd:
                         e.tensor_copy(
                             out=dd[:, mt % 4, hf * 512:(hf + 1) * 512],
                             in_=ps[:]), inc="dve",
                         wait=("te", "ld") if (DBG and mt == 0 and hf == 0)
                         else ("te",))
            for sq in range(4):
                p.op("gpsimd", lambda e, c=c, sq=sq: e.dma_start(
                    out=po_d[c][sq * 1024:sq * 1024 + 512, :]
                    .rearrange("(a p) q -> p a q", p=128),
                    in_=po_lo[:, :, sq * NQ4:(sq + 1) * NQ4]),
                    inc="gp", inck=16, wait=("dve",) if sq == 0 else ())
                p.op("gpsimd", lambda e, c=c, sq=sq: e.dma_start(
                    out=po_d[c][sq * 1024 + 512:(sq + 1) * 1024, :]
                    .rearrange("(a p) q -> p a q", p=128),
                    in_=po_hi[:, :, sq * NQ4:(sq + 1) * NQ4]),
                    inc="gp", inck=16)
            ccn = new_cc()
            p.op("gpsimd", lambda e, c=c: e.collective_compute(
                "ReduceScatter", ALU.add, replica_groups=G2,
                ins=[po_d[c]], outs=[attn_q[c]]), inc=ccn, wait=("gp",))
            return ccn

        def ffn_weight_quant():
            mats = [(wg_t, wq_g_d, 0), (wu_t, wq_u_d, 1), (wd_t, wq_d_d, 2)]
            for src, dst, mi in mats:
                srcf = src.rearrange("p a b c -> p (a b c)")
                dstf = dst.rearrange("p a b c -> p (a b c)")
                acc, tmpc = cols[0], cols[1]
                p.op("vector", lambda e, a=acc: e.memset(a[:], 0.0), inc="dve")
                for ch in range(8):
                    st32 = big_a[:, :4096]
                    p.op("sync", lambda e, s=srcf, ch=ch, dd=st32: e.dma_start(
                        out=dd, in_=s[:, ch * 4096:(ch + 1) * 4096]),
                        inc="ld", inck=16, wait=("dve", "gp", "te"))
                    p.op("vector", lambda e, dd=st32, t=tmpc: e.reduce_sum(
                        t[:], dd, axis=AX.X, apply_absolute_value=True),
                        inc="dve", wait=("ld",))
                    p.op("vector", lambda e, a=acc, t=tmpc: e.tensor_tensor(
                        out=a[:], in0=a[:], in1=t[:], op=ALU.add), inc="dve")
                p.op("tensor", lambda e, a=acc: e.matmul(
                    out=ps_r[0:1, 0:1], lhsT=onesc_f[:], rhs=a[:],
                    start=True, stop=True), inc="te", wait=("dve",))
                p.op("vector", lambda e: e.tensor_scalar(
                    out=rows4[:, 0:1], in0=ps_r[0:1, 0:1],
                    scalar1=1.0 / (4096.0 * 1024.0), scalar2=1e-5,
                    op0=ALU.mult, op1=ALU.max), inc="dve", wait=("te",))
                p.op("vector", lambda e: e.reciprocal(
                    rows4[:, 1:2], rows4[:, 0:1]), inc="dve")
                p.op("gpsimd", lambda e: e.dma_start(
                    out=swb_d[0:1, 0:2], in_=rows4[:, 0:2]),
                    inc="gp", inck=16, wait=("dve",))
                p.op("sync", lambda e, mi=mi: e.dma_start(
                    out=sw3[:, mi:mi + 1],
                    in_=swb_d[0:1, 0:1].broadcast_to([128, 1])),
                    inc="ld", inck=16, wait=("gp",))
                p.op("sync", lambda e: e.dma_start(
                    out=swq[:], in_=swb_d[0:1, 1:2].broadcast_to([128, 1])),
                    inc="ld", inck=16)
                for ch in range(8):
                    st32 = big_a[:, :4096]
                    qv = big_b[:, :4096]
                    g2v = big_c[:, :4096]
                    st16 = wstr16[:, :4096]
                    p.op("sync", lambda e, s=srcf, ch=ch, dd=st32: e.dma_start(
                        out=dd, in_=s[:, ch * 4096:(ch + 1) * 4096]),
                        inc="ld", inck=16, wait=("dve",))
                    p.op("vector", lambda e, dd=st32, q=qv: e.tensor_scalar_mul(
                        out=q, in0=dd, scalar1=swq[:]), inc="dve",
                        wait=("ld", "gp"))
                    p.op("vector", lambda e, q=qv, g=g2v: e.tensor_single_scalar(
                        out=g, in_=q, scalar=-0.5, op=ALU.is_lt), inc="dve")
                    p.op("vector", lambda e, q=qv: e.tensor_single_scalar(
                        out=q, in_=q, scalar=0.5, op=ALU.is_gt), inc="dve")
                    p.op("vector", lambda e, q=qv, g=g2v: e.tensor_tensor(
                        out=q, in0=q, in1=g, op=ALU.subtract), inc="dve")
                    p.op("vector", lambda e, q=qv, dd=st16: e.tensor_copy(
                        out=dd, in_=q), inc="dve")
                    p.op("gpsimd", lambda e, dd=dstf, ch=ch, s=st16:
                         e.dma_start(out=dd[:, ch * 4096:(ch + 1) * 4096],
                                     in_=s), inc="gp", inck=16, wait=("dve",))

        def ffn_quant1(rs_sems):
            ntile = 0
            for c in range(NCH):
                aq32 = big_b[:, :8 * NQ4].rearrange("p (k t) -> p k t", k=8)
                p.op("sync", lambda e, c=c, a=aq32: e.dma_start(
                    out=a, in_=attn_q[c].rearrange("(k p) t -> p k t", p=128)),
                    inc="ld", inck=16, wait=((rs_sems[c], 1), "dve", "gp"))
                for tt in range(NQ4 // 128):
                    r0 = c * NQ4 + tt * 128
                    p.op("sync", lambda e, r0=r0: e.dma_start(
                        out=xt[:], in_=x_own[r0:r0 + 128, :]),
                        inc="ld", inck=16, wait=("dve", "te"))
                    x2t = big_d[:, :D]
                    for kc in range(8):
                        p.op("tensor", lambda e, kc=kc, tt=tt, a=aq32:
                             e.transpose(
                                 out=ps_t[:],
                                 in_=a[:, kc, tt * 128:(tt + 1) * 128],
                                 identity=ident_f[:]),
                             inc="te", wait=("ld", "dve"))
                        p.op("vector", lambda e, kc=kc: e.tensor_copy(
                            out=big_c[:, kc * 128:(kc + 1) * 128],
                            in_=ps_t[:]), inc="dve", wait=("te",))
                    p.op("vector", lambda e: e.tensor_tensor(
                        out=x2t, in0=big_c[:, :D], in1=reps["ga"][:],
                        op=ALU.mult), inc="dve", wait=("ld",))
                    p.op("vector", lambda e: e.tensor_tensor(
                        out=x2t, in0=x2t, in1=xt[:], op=ALU.add), inc="dve")
                    p.op("gpsimd", lambda e, r0=r0: e.dma_start(
                        out=x2_d[r0:r0 + 128, :], in_=x2t),
                        inc="gp", inck=16, wait=("dve",))
                    c0, c1, c2, c3, c4, c5 = cols
                    h2 = big_c[:, :D]
                    p.op("vector", lambda e: e.reduce_sum(
                        c0[:], x2t, axis=AX.X), inc="dve")
                    p.op("vector", lambda e: e.tensor_scalar_mul(
                        out=c1[:], in0=c0[:], scalar1=1.0 / D), inc="dve")
                    p.op("vector", lambda e: e.tensor_tensor(
                        out=h2, in0=x2t, in1=x2t, op=ALU.mult), inc="dve")
                    p.op("vector", lambda e: e.reduce_sum(
                        c2[:], h2, axis=AX.X), inc="dve")
                    p.op("vector", lambda e: e.tensor_scalar_mul(
                        out=c3[:], in0=c2[:], scalar1=1.0 / D), inc="dve")
                    p.op("vector", lambda e: e.tensor_tensor(
                        out=c4[:], in0=c1[:], in1=c1[:], op=ALU.mult),
                        inc="dve")
                    p.op("vector", lambda e: e.tensor_tensor(
                        out=c5[:], in0=c3[:], in1=c4[:], op=ALU.subtract),
                        inc="dve")
                    p.op("scalar", lambda e: e.activation(
                        out=c0[:], in_=c5[:], func=AF.Sqrt, bias=cst_eps[:]),
                        inc="act", wait=("dve",))
                    p.op("vector", lambda e: e.reciprocal(c2[:], c0[:]),
                         inc="dve", wait=("act",))
                    p.op("vector", lambda e: e.tensor_scalar(
                        out=h2, in0=x2t, scalar1=c1[:], scalar2=c2[:],
                        op0=ALU.subtract, op1=ALU.mult), inc="dve")
                    p.op("vector", lambda e: e.tensor_tensor(
                        out=h2, in0=h2, in1=reps["cb1"][:], op=ALU.mult),
                        inc="dve")
                    p.op("vector", lambda e: e.tensor_tensor(
                        out=h2, in0=h2, in1=reps["sbm"][:], op=ALU.add),
                        inc="dve")
                    mx, sc, iv = cols[3], cols[4], cols[5]
                    p.op("vector", lambda e: e.reduce_max(
                        mx[:], h2, axis=AX.X, apply_absolute_value=True),
                        inc="dve")
                    p.op("vector", lambda e: e.tensor_single_scalar(
                        out=mx[:], in_=mx[:], scalar=1e-5, op=ALU.max),
                        inc="dve")
                    p.op("vector", lambda e: e.reciprocal(sc[:], mx[:]),
                         inc="dve")
                    p.op("vector", lambda e: e.tensor_scalar_mul(
                        out=sc[:], in0=sc[:], scalar1=127.0), inc="dve")
                    p.op("vector", lambda e: e.tensor_scalar_mul(
                        out=iv[:], in0=mx[:], scalar1=1.0 / 127.0), inc="dve")
                    p.op("vector", lambda e, nt=ntile: e.tensor_copy(
                        out=big_a[:, nt:nt + 1], in_=iv[:]), inc="dve")
                    q = big_d[:, D:2 * D]
                    p.op("vector", lambda e: e.tensor_scalar_mul(
                        out=q, in0=h2, scalar1=sc[:]), inc="dve")
                    p.op("vector", lambda e: e.tensor_scalar(
                        out=q, in0=q, scalar1=MAGIC, scalar2=MAGIC,
                        op0=ALU.add, op1=ALU.subtract), inc="dve")
                    p.op("vector", lambda e: e.tensor_scalar(
                        out=q, in0=q, scalar1=127.0, scalar2=-128.0,
                        op0=ALU.min, op1=ALU.max), inc="dve")
                    p.op("vector", lambda e: e.tensor_copy(
                        out=ht_h[:], in_=q), inc="dve")
                    p.op("gpsimd", lambda e, r0=r0: e.dma_start(
                        out=q1l_in[r0:r0 + 128, :], in_=ht_h[:]),
                        inc="gp", inck=16, wait=("dve",))
                    ntile += 1
            ccn = new_cc()
            p.op("gpsimd", lambda e: e.collective_compute(
                "AllGather", ALU.bypass, replica_groups=G1,
                ins=[q1l_in], outs=[q1l_out]), inc=ccn, wait=("gp",))
            return ccn

        def ffn_main(q1cc):
            invcols = big_a
            for blk in range(NB4):
                xqT = hT3
                for tt in range(4):
                    r0 = blk * 512 + tt * 128
                    p.op("sync", lambda e, r0=r0: e.dma_start(
                        out=ht_h[:], in_=q1l_out[r0:r0 + 128, :]),
                        inc="ld", inck=16, wait=((q1cc, 1), "dve", "te"))
                    for kc in range(8):
                        p.op("tensor", lambda e, kc=kc: e.transpose(
                            out=ps_th[:],
                            in_=ht_h[:, kc * 128:(kc + 1) * 128],
                            identity=ident[:]), inc="te", wait=("ld", "dve"))
                        p.op("vector", lambda e, kc=kc, tt=tt: e.tensor_copy(
                            out=xqT[:, kc, tt * 128:(tt + 1) * 128],
                            in_=ps_th[:]), inc="dve", wait=("te",))
                er = xt[0:1, 0:512]
                for tt in range(4):
                    cidx = blk * 4 + tt
                    p.op("tensor", lambda e, cidx=cidx: e.transpose(
                        out=ps_t[0:1, 0:128], in_=invcols[:, cidx:cidx + 1],
                        identity=ident_f[:]), inc="te", wait=("dve",))
                    p.op("vector", lambda e, tt=tt: e.tensor_copy(
                        out=er[:, tt * 128:(tt + 1) * 128],
                        in_=ps_t[0:1, 0:128]), inc="dve", wait=("te",))
                epi_g = big_c[:, 0:512]
                epi_u = big_c[:, 512:1024]
                p.op("tensor", lambda e: e.matmul(
                    out=ps_r[:], lhsT=ones_f[:], rhs=er, start=True,
                    stop=True), inc="te", wait=("dve",))
                p.op("vector", lambda e: e.tensor_scalar_mul(
                    out=epi_g, in0=ps_r[:], scalar1=sw3[:, 0:1]),
                    inc="dve", wait=("te",))
                p.op("vector", lambda e: e.tensor_scalar_mul(
                    out=epi_u, in0=ps_r[:], scalar1=sw3[:, 1:2]), inc="dve")
                for mt in range(32):
                    p.op("sync", lambda e, mt=mt: e.dma_start(
                        out=wst3[:, 0:8, :], in_=wq_g_d[:, mt, :, :]),
                        inc="ld", inck=16, wait=("te", "dve"))
                    for kcc in range(8):
                        p.op("tensor", lambda e, kcc=kcc: e.matmul(
                            out=ps_a[:], lhsT=wst3[:, kcc, :],
                            rhs=xqT[:, kcc, 0:512], start=(kcc == 0),
                            stop=(kcc == 7)),
                            inc="te" if kcc == 7 else None,
                            wait=("ld", "dve") if kcc == 0 else ())
                    p.op("sync", lambda e, mt=mt: e.dma_start(
                        out=wst3[:, 8:16, :], in_=wq_u_d[:, mt, :, :]),
                        inc="ld", inck=16)
                    for kcc in range(8):
                        p.op("tensor", lambda e, kcc=kcc: e.matmul(
                            out=ps_b[:], lhsT=wst3[:, 8 + kcc, :],
                            rhs=xqT[:, kcc, 0:512], start=(kcc == 0),
                            stop=(kcc == 7)),
                            inc="te" if kcc == 7 else None,
                            wait=("ld", "dve") if kcc == 0 else ())
                    p.op("vector", lambda e: e.tensor_tensor(
                        out=gtl, in0=ps_a[:], in1=epi_g, op=ALU.mult),
                        inc="dve", wait=("te",))
                    p.op("scalar", lambda e: e.activation(
                        out=gtl2, in_=gtl, func=AF.Silu), inc="act",
                        wait=("dve",))
                    p.op("vector", lambda e: e.tensor_tensor(
                        out=utl, in0=ps_b[:], in1=epi_u, op=ALU.mult),
                        inc="dve")
                    p.op("vector", lambda e, mt=mt: e.tensor_tensor(
                        out=gu3[:, mt, :], in0=gtl2, in1=utl,
                        op=ALU.mult), inc="dve", wait=("act",))
                p.op("vector", lambda e: e.tensor_copy(
                    out=gtl, in_=gu3[:, 0, :]), inc="dve", wait=("te",))
                p.op("vector", lambda e: e.tensor_copy(
                    out=gtl2, in_=gu3[:, 0, :]), inc="dve")
                for mt in range(1, 32):
                    p.op("vector", lambda e, mt=mt: e.tensor_tensor(
                        out=gtl, in0=gtl, in1=gu3[:, mt, :],
                        op=ALU.max), inc="dve")
                    p.op("vector", lambda e, mt=mt: e.tensor_tensor(
                        out=gtl2, in0=gtl2, in1=gu3[:, mt, :],
                        op=ALU.min), inc="dve")
                p.op("vector", lambda e: e.tensor_scalar_mul(
                    out=gtl2, in0=gtl2, scalar1=-1.0), inc="dve")
                p.op("gpsimd", lambda e: e.tensor_reduce(
                    out=big_b[0:1, 0:512], in_=gtl,
                    axis=AX.C, op=ALU.max), inc="gpc", wait=("dve",))
                p.op("gpsimd", lambda e: e.tensor_reduce(
                    out=big_b[0:1, 512:1024], in_=gtl2,
                    axis=AX.C, op=ALU.max), inc="gpc")
                p.op("vector", lambda e: e.tensor_tensor(
                    out=xt[0:1, 0:512], in0=big_b[0:1, 0:512],
                    in1=big_b[0:1, 512:1024], op=ALU.max),
                    inc="dve", wait=("gpc", "te"))
                p.op("vector", lambda e: e.tensor_single_scalar(
                    out=xt[0:1, 0:512], in_=xt[0:1, 0:512], scalar=1e-5,
                    op=ALU.max), inc="dve")
                p.op("vector", lambda e: e.tensor_scalar_mul(
                    out=xt[0:1, 512:1024], in0=xt[0:1, 0:512],
                    scalar1=1.0 / 127.0), inc="dve")
                p.op("vector", lambda e: e.reciprocal(
                    xt[0:1, 0:512], xt[0:1, 0:512]), inc="dve")
                p.op("vector", lambda e: e.tensor_scalar_mul(
                    out=xt[0:1, 0:512], in0=xt[0:1, 0:512], scalar1=127.0),
                    inc="dve")
                p.op("tensor", lambda e: e.matmul(
                    out=ps_r[:], lhsT=ones_f[:], rhs=xt[0:1, 0:512],
                    start=True, stop=True), inc="te", wait=("dve",))
                s2rep = big_c[:, 1024:1536]
                p.op("vector", lambda e: e.tensor_copy(
                    out=s2rep, in_=ps_r[:]), inc="dve", wait=("te",))
                for mt in range(32):
                    qq = big_b[:, 0:512]
                    p.op("vector", lambda e, mt=mt: e.tensor_tensor(
                        out=qq, in0=gu3[:, mt, :], in1=s2rep, op=ALU.mult),
                        inc="dve")
                    p.op("vector", lambda e: e.tensor_scalar(
                        out=qq, in0=qq, scalar1=MAGIC, scalar2=MAGIC,
                        op0=ALU.add, op1=ALU.subtract), inc="dve")
                    p.op("vector", lambda e: e.tensor_scalar(
                        out=qq, in0=qq, scalar1=127.0, scalar2=-128.0,
                        op0=ALU.min, op1=ALU.max), inc="dve")
                    p.op("vector", lambda e, mt=mt: e.tensor_copy(
                        out=gu3[:, mt, :], in_=qq), inc="dve")
                p.op("gpsimd", lambda e, blk=blk: e.dma_start(
                    out=q2l_in[blk].rearrange("(k p) t -> p k t", p=128),
                    in_=gu3[:]), inc="gp", inck=16, wait=("dve",))
                ccn = new_cc()
                p.op("gpsimd", lambda e, blk=blk: e.collective_compute(
                    "AllGather", ALU.bypass, replica_groups=G1,
                    ins=[q2l_in[blk]], outs=[q2l_out[blk]]), inc=ccn,
                    wait=("gp",))
                p.op("sync", lambda e, blk=blk: e.dma_start(
                    out=gu3[:],
                    in_=q2l_out[blk].rearrange("(k p) t -> p k t", p=128)),
                    inc="ld", inck=16, wait=((ccn, 1), "dve"))
                p.op("tensor", lambda e: e.matmul(
                    out=ps_r[:], lhsT=ones_f[:], rhs=xt[0:1, 512:1024],
                    start=True, stop=True), inc="te", wait=("dve", "ld"))
                epi_d = big_c[:, 0:512]
                p.op("vector", lambda e: e.tensor_scalar_mul(
                    out=epi_d, in0=ps_r[:], scalar1=sw3[:, 2:3]),
                    inc="dve", wait=("te",))
                for mt in range(8):
                    p.op("sync", lambda e, mt=mt: e.dma_start(
                        out=wst3[:, 0:32, :], in_=wq_d_d[:, mt, :, :]),
                        inc="ld", inck=16, wait=("te", "dve"))
                    for kcc in range(32):
                        p.op("tensor", lambda e, kcc=kcc: e.matmul(
                            out=ps_a[:], lhsT=wst3[:, kcc, :],
                            rhs=gu3[:, kcc, :], start=(kcc == 0),
                            stop=(kcc == 31)),
                            inc="te" if kcc == 31 else None,
                            wait=("ld", "dve") if kcc == 0 else ())
                    p.op("vector", lambda e, mt=mt: e.tensor_tensor(
                        out=ffn8[:, mt, :], in0=ps_a[:], in1=epi_d,
                        op=ALU.mult), inc="dve", wait=("te",))
                for tt in range(4):
                    r0 = blk * 512 + tt * 128
                    p.op("sync", lambda e, r0=r0: e.dma_start(
                        out=xt[:], in_=x2_d[r0:r0 + 128, :]),
                        inc="ld", inck=16, wait=("dve", "te", "gp"))
                    fft = big_c[:, 0:D]
                    for kc in range(8):
                        p.op("tensor", lambda e, kc=kc, tt=tt: e.transpose(
                            out=ps_t[:],
                            in_=ffn8[:, kc, tt * 128:(tt + 1) * 128],
                            identity=ident_f[:]), inc="te", wait=("dve",))
                        p.op("vector", lambda e, kc=kc: e.tensor_copy(
                            out=fft[:, kc * 128:(kc + 1) * 128], in_=ps_t[:]),
                            inc="dve", wait=("te",))
                    x3 = big_c[:, D:2 * D]
                    p.op("vector", lambda e: e.tensor_tensor(
                        out=x3, in0=fft, in1=reps["gb"][:], op=ALU.mult),
                        inc="dve", wait=("ld",))
                    p.op("vector", lambda e: e.tensor_tensor(
                        out=x3, in0=x3, in1=xt[:], op=ALU.add), inc="dve")
                    p.op("gpsimd", lambda e, r0=r0: e.dma_start(
                        out=y_out[r0:r0 + 128, :], in_=x3),
                        inc="gp", inck=16, wait=("dve",))

        def dump(name, src):
            p.op("sync", lambda e, nm=name, s2=src: e.dma_start(
                out=dbg[nm], in_=s2),
                inc="ld", inck=16, wait=("dve", "act", "te", "gp"))

        setup()
        rs_sems = []
        for c in range(NCH):
            ln1(c)
            in_proj_conv(c)
            x_proj_reduce(c)
            dt_u_build(c)
            scan(c)
            if DBG and c == 0:
                dump("dbg_xt", xt[:])
                for ci in range(6):
                    p.op("sync", lambda e, ci=ci: e.dma_start(
                        out=dbg["dbg_cols"][:, ci:ci + 1], in_=cols[ci][:],
                        allow_slow_non_contiguous=True),
                        inc="ld", inck=16, wait=("dve", "act", "te", "gp"))
                dump("dbg_hth", ht_h[:])
                dump("dbg_hth2", ht_h2[:])
                dump("dbg_hT", hT[:])
                dump("dbg_xsr", xsr[:].rearrange("p a q -> p (a q)"))
                dump("dbg_xssi", xssi)
                dump("dbg_bct", bct[:])
                dump("dbg_dtr", xt[0:1, :Q])
                dump("dbg_dt", big_a[:])
                dump("dbg_u", u_h)
                dump("dbg_y", big_b[:])
                dump("dbg_yz", z_h)
                dump("dbg_carry", carry[:].rearrange("p a q -> p (a q)"))
                dump("dbg_ca1", reps["ca1"][:])
                dump("dbg_sa", reps["sa"][:])
                dump("dbg_ga", reps["ga"][:])
                dump("dbg_z", u_h)
                dump("dbg_asl", a_sl[:].rearrange("p a q -> p (a q)"))
            rs_sems.append(out_proj(c))
            if DBG and c == 0:
                p.op("sync", lambda e: e.dma_start(
                    out=dbg["dbg_aq"], in_=attn_q[0]),
                    inc="ld", inck=16, wait=((rs_sems[0], 1),))
        ffn_weight_quant()
        q1cc = ffn_quant1(rs_sems)
        ffn_main(q1cc)
        p.op("gpsimd", lambda e: e.engine_nop(), wait=("gp", "ld"))

        sem_names = ["ld", "gp", "te", "dve", "act", "gpc"] + cc_names
        with ExitStack() as sctx:
            H = {sn: sctx.enter_context(nc.semaphore(sn)) for sn in sem_names}
            with nc.Block() as block:
                @block.sync
                def _(e):
                    for fn in p.seq["sync"]:
                        fn(e, H)

                @block.scalar
                def _(e):
                    for fn in p.seq["scalar"]:
                        fn(e, H)

                @block.vector
                def _(e):
                    for fn in p.seq["vector"]:
                        fn(e, H)

                @block.tensor
                def _(e):
                    for fn in p.seq["tensor"]:
                        fn(e, H)

                @block.gpsimd
                def _(e):
                    for fn in p.seq["gpsimd"]:
                        fn(e, H)
    return nc


def make_inputs(core, L, Q, ins):
    import ml_dtypes
    g, j = core // 4, core % 4
    NCH = L // Q
    NQ4 = Q // 4
    sl = slice(j * DSL, (j + 1) * DSL)
    x = np.ascontiguousarray(np.asarray(ins["L0_attn_x"])[g])
    x_own = np.concatenate(
        [x[c * Q + j * NQ4: c * Q + (j + 1) * NQ4] for c in range(NCH)],
        axis=0)
    W = np.asarray(ins["L0_attn_in_proj_w"])
    d = {
        "x_b": x,
        "x_own": x_own,
        "w_in_xs": W.reshape(8, 128, 2 * DI)[:, :, sl].transpose(1, 0, 2),
        "w_in_z": W.reshape(8, 128, 2 * DI)
        [:, :, DI + j * DSL: DI + (j + 1) * DSL].transpose(1, 0, 2),
        "w_xp": np.asarray(ins["L0_attn_x_proj_w"])[sl]
        .reshape(NDB, 128, 129).transpose(1, 0, 2),
        "conv_w4": np.asarray(ins["L0_attn_conv_w"])[sl, 0, :]
        .reshape(NDB, 128, KC).transpose(1, 0, 2),
        "conv_b4": np.asarray(ins["L0_attn_conv_b"])[sl].reshape(NDB, 128).T,
        "dtw4": np.asarray(ins["L0_attn_dt_w"])[0, sl].reshape(NDB, 128).T,
        "dtb4": np.asarray(ins["L0_attn_dt_b"])[sl].reshape(NDB, 128).T,
        "alog4": np.asarray(ins["L0_attn_A_log"])[sl]
        .reshape(NDB, 128, DS).transpose(1, 0, 2),
        "d4": np.asarray(ins["L0_attn_D"])[sl].reshape(NDB, 128).T,
        "w_out_sl": np.asarray(ins["L0_attn_out_proj_w"])[sl]
        .reshape(NDB, 128, 8, 128).transpose(1, 0, 2, 3),
        "wg_t": np.asarray(ins["L0_ffn_w_gate"]).T
        .reshape(8, 128, 32, 128).transpose(1, 2, 0, 3),
        "wu_t": np.asarray(ins["L0_ffn_w_up"]).T
        .reshape(8, 128, 32, 128).transpose(1, 2, 0, 3),
        "wd_t": np.asarray(ins["L0_ffn_w_down"]).T
        .reshape(32, 128, 8, 128).transpose(1, 2, 0, 3),
        "c_row": np.asarray(ins["L0_c"])[g:g + 1],
        "wmod_r": np.asarray(ins["L0_wmod"]).reshape(8, 128, 6 * D)
        .transpose(1, 0, 2),
        "ident_h": np.eye(128, dtype=ml_dtypes.bfloat16),
    }
    return {k: np.ascontiguousarray(np.asarray(v, dtype=np.float32))
            if k != "ident_h" else v for k, v in d.items()}


def assemble(L, Q, results):
    NCH = L // Q
    NQ4 = Q // 4
    y = np.empty((2, L, D), np.float32)
    for core in range(8):
        g, j = core // 4, core % 4
        yo = results[core]["y_out"]
        for c in range(NCH):
            y[g, c * Q + j * NQ4: c * Q + (j + 1) * NQ4] = \
                yo[c * NQ4:(c + 1) * NQ4]
    return y


def kernel(**inputs):
    L, Q = 8192, 1024
    nc = build_nc(L, Q)
    in_maps = [make_inputs(c, L, Q, inputs) for c in range(8)]
    res = run_bass_kernel_spmd(nc, in_maps, core_ids=list(range(8)))
    return assemble(L, Q, res.results)


# version 02
# speedup vs baseline: 1.1963x; 1.1963x over previous
import numpy as np
from contextlib import ExitStack

import concourse.bass as bass
import concourse.mybir as mybir
from concourse.bass_utils import run_bass_kernel_spmd

F32 = mybir.dt.float32
BF16 = mybir.dt.bfloat16
AF = mybir.ActivationFunctionType
ALU = mybir.AluOpType
AX = mybir.AxisListType

MAGIC = float(np.float32(1.5 * 2.0 ** 23))
LN_EPS = 1e-5
D = 1024
DI = 2048
DSL = 512
NDB = 4
DS = 64
KC = 4
G2 = [[0, 1, 2, 3], [4, 5, 6, 7]]
G1 = [[c] for c in range(8)]

ENGS = ("sync", "scalar", "vector", "tensor", "gpsimd")


class Plan:
    def __init__(self):
        self.seq = {e: [] for e in ENGS}
        self.cnt = {}

    def sem(self, name):
        self.cnt.setdefault(name, 0)

    SELF_SEM = {"vector": "dve", "scalar": "act"}

    def op(self, eng, build, inc=None, inck=1, wait=()):
        waits = []
        if eng in self.SELF_SEM:
            sn = self.SELF_SEM[eng]
            waits.append((sn, self.cnt[sn]))
        for w in wait:
            if isinstance(w, tuple):
                waits.append(w)
            else:
                waits.append((w, self.cnt[w]))
        incs = None
        if inc is not None:
            self.cnt[inc] += inck
            incs = (inc, inck)

        def fn(e, H, waits=tuple(waits), build=build, incs=incs):
            for s, v in waits:
                if v > 0:
                    e.wait_ge(H[s], v)
            ins = build(e)
            if incs is not None:
                ins.then_inc(H[incs[0]], incs[1])
        self.seq[eng].append(fn)


DBG = False


def build_nc(L, Q):
    NCH = L // Q
    QT = Q // 128
    NQ4 = Q // 4
    OWN = NCH * NQ4
    NB4 = OWN // 512
    assert L % Q == 0 and Q == 1024 and NQ4 % 128 == 0 and OWN % 512 == 0

    nc = bass.Bass("TRN2", target_bir_lowering=False, debug=False,
                   detect_race_conditions=False)

    def din(name, shape, dt=F32):
        return nc.dram_tensor(name, shape, dt, kind="ExternalInput").ap()

    x_b = din("x_b", [L, D])
    x_own = din("x_own", [OWN, D])
    w_in_xs = din("w_in_xs", [128, 8, DSL])
    w_in_z = din("w_in_z", [128, 8, DSL])
    w_xp = din("w_xp", [128, NDB, 129])
    conv_w4 = din("conv_w4", [128, NDB, KC])
    conv_b4 = din("conv_b4", [128, NDB])
    dtw4 = din("dtw4", [128, NDB])
    dtb4 = din("dtb4", [128, NDB])
    alog4 = din("alog4", [128, NDB, DS])
    d4 = din("d4", [128, NDB])
    w_out_sl = din("w_out_sl", [128, NDB, 8, 128])
    wg_t = din("wg_t", [128, 32, 8, 128])
    wu_t = din("wu_t", [128, 32, 8, 128])
    wd_t = din("wd_t", [128, 8, 32, 128])
    c_row = din("c_row", [1, D])
    wmod_r = din("wmod_r", [128, 8, 6 * D])
    ident_h = din("ident_h", [128, 128], BF16)

    y_out = nc.dram_tensor("y_out", [OWN, D], F32, kind="ExternalOutput").ap()
    dbg = {}
    if DBG:
        for nm, shape, dt in [
                ("dbg_hT", [128, 8 * Q], BF16), ("dbg_xsr", [128, NDB * (KC + Q)], BF16),
                ("dbg_xssi", [128, NDB * Q], BF16), ("dbg_bct", [128, Q], F32),
                ("dbg_dtr", [1, Q], F32), ("dbg_dt", [128, NDB * Q], F32),
                ("dbg_u", [128, NDB * Q], BF16), ("dbg_y", [128, NDB * Q], F32),
                ("dbg_yz", [128, NDB * Q], BF16), ("dbg_carry", [128, NDB * DS], F32),
                ("dbg_ca1", [128, D], BF16), ("dbg_sa", [128, D], BF16),
                ("dbg_ga", [128, D], BF16), ("dbg_aq", [1024, NQ4], F32),
                ("dbg_z", [128, NDB * Q], BF16), ("dbg_asl", [128, NDB * DS], F32),
                ("dbg_xt", [128, D], F32), ("dbg_cols", [128, 6], F32),
                ("dbg_hth", [128, D], BF16), ("dbg_hth2", [128, D], BF16)]:
            dbg[nm] = nc.dram_tensor(nm, shape, dt, kind="ExternalOutput").ap()

    dscr = lambda n, s, dt=F32: nc.dram_tensor(n, s, dt).ap()
    xp_part = [dscr(f"xp_part{c}", [129, Q]) for c in range(NCH)]
    xp_red = [dscr(f"xp_red{c}", [129, Q]) for c in range(NCH)]
    po_d = [dscr(f"po_d{c}", [4 * 1024, NQ4]) for c in range(NCH)]
    attn_q = [dscr(f"attn_q{c}", [1024, NQ4]) for c in range(NCH)]
    wq_g_d = dscr("wq_g_d", [128, 32, 8, 128], BF16)
    wq_u_d = dscr("wq_u_d", [128, 32, 8, 128], BF16)
    wq_d_d = dscr("wq_d_d", [128, 8, 32, 128], BF16)
    x2_d = dscr("x2_d", [OWN, D])
    q1l_in = dscr("q1l_in", [OWN, D], BF16)
    q1l_out = dscr("q1l_out", [OWN, D], BF16)
    q2l_in = [dscr(f"q2l_in{b}", [4096, 512], BF16) for b in range(NB4)]
    q2l_out = [dscr(f"q2l_out{b}", [4096, 512], BF16) for b in range(NB4)]
    swb_d = dscr("swb_d", [1, 4])

    p = Plan()
    for s in ("ld", "gp", "te", "dve", "act", "gpc"):
        p.sem(s)
    cc_names = []

    def new_cc():
        n = f"cc{len(cc_names)}"
        cc_names.append(n)
        p.sem(n)
        return n

    with ExitStack() as ctx:
        sb = lambda n, s, dt=F32: ctx.enter_context(nc.sbuf_tensor(n, s, dt))
        psum = lambda n, s, dt=F32: ctx.enter_context(nc.psum_tensor(n, s, dt))

        wxs_h = sb("wxs_h", [128, 8, DSL], BF16)
        wz_h = sb("wz_h", [128, 8, DSL], BF16)
        wxp_h = sb("wxp_h", [128, NDB, 129], BF16)
        wout_h = sb("wout_h", [128, NDB, 8, 128], BF16)
        convw = sb("convw", [128, NDB, KC])
        convb = sb("convb", [128, NDB])
        dtw = sb("dtw", [128, NDB])
        dtb = sb("dtb", [128, NDB])
        a_sl = sb("a_sl", [128, NDB, DS])
        d_col = sb("d_col", [128, NDB])
        ident = sb("ident", [128, 128], BF16)
        ident_f = sb("ident_f", [128, 128])
        ones_f = sb("ones_f", [1, 128])
        onesc_f = sb("onesc_f", [128, 1])
        reps = {}
        for nmm in ("ca1", "sa", "ga", "cb1", "sbm", "gb"):
            reps[nmm] = sb("rep_" + nmm, [128, D], BF16)
        carry = sb("carry", [128, NDB, DS])
        halo = sb("halo", [128, NDB, KC - 1], BF16)
        silc_f = sb("silc_f", [128, 8])
        sw3 = sb("sw3", [128, 4])
        swq = sb("swq", [128, 1])
        cols = [sb(f"col{i}", [128, 1]) for i in range(6)]
        rows4 = sb("rows4", [1, 16])
        cst_eps = sb("cst_eps", [128, 1])
        cst_neg1 = sb("cst_neg1", [128, 1])
        cst_one = sb("cst_one", [128, 1])

        big_a = sb("big_a", [128, NDB * Q])
        big_b = sb("big_b", [128, NDB * Q])
        big_c = sb("big_c", [128, NDB * Q])
        big_d = sb("big_d", [128, NDB * Q])
        xt = sb("xt", [128, D])
        bct = sb("bct", [128, Q])
        hT = sb("hT", [128, 8 * Q], BF16)
        NXS = NDB * (KC + Q)
        bfB = sb("bfB", [128, NXS + 4 * NDB * Q + 7 * Q], BF16)
        o = 0
        def carve(n):
            nonlocal o
            v = bfB[:, o:o + n]
            o += n
            return v
        xsr = carve(NXS).rearrange("p (a q) -> p a q", a=NDB)
        xssi = carve(NDB * Q)
        z_h = carve(NDB * Q)
        u_h = carve(NDB * Q)
        inj = carve(NDB * Q)
        bb = [carve(Q) for i in range(3)]
        cbb = [carve(Q) for i in range(3)]
        bc_h = carve(Q)
        ht_h = sb("ht_h", [128, D], BF16)
        ht_h2 = sb("ht_h2", [128, D], BF16)
        gu16 = bfB[:, 0:32 * 512]
        wstr16 = bfB[:, 32 * 512:32 * 512 + 32 * 128]
        gtl = bfB[:, 20480:20992]
        gtl2 = bfB[:, 20992:21504]
        utl = bfB[:, 21504:22016]
        ps_all = psum("ps_all", [128, 4096])
        ps_a = ps_all[:, 0:512]
        ps_b = ps_all[:, 512:1024]
        ps_r = ps_all[:, 1024:1536]
        ps_t = ps_all[:, 1536:1664]

        dt_t = big_a[:].rearrange("p (a q) -> p a q", a=NDB)
        tmp_a = big_b[:].rearrange("p (a q) -> p a q", a=NDB)
        dAv = big_c[:].rearrange("p (a q) -> p a q", a=NDB)
        yacc = big_d[:].rearrange("p (a q) -> p a q", a=NDB)
        xssi3 = xssi.rearrange("p (a q) -> p a q", a=NDB)
        z3 = z_h.rearrange("p (a q) -> p a q", a=NDB)
        u3 = u_h.rearrange("p (a q) -> p a q", a=NDB)
        inj3 = inj.rearrange("p (a q) -> p a q", a=NDB)
        hT3 = hT[:].rearrange("p (k q) -> p k q", k=8)
        gu3 = gu16.rearrange("p (m t) -> p m t", m=32)
        wst3 = wstr16.rearrange("p (k m) -> p k m", m=128)
        ffn8 = big_d[:].rearrange("p (m t) -> p m t", m=8)

        def bcast3(t, n):
            return t.unsqueeze(1).broadcast_to([128, n, t.shape[-1]])

        def setup():
            p.op("vector", lambda e: e.memset(ones_f[:], 1.0), inc="dve")
            p.op("vector", lambda e: e.memset(cst_eps[:], LN_EPS), inc="dve")
            p.op("vector", lambda e: e.memset(cst_neg1[:], -1.0), inc="dve")
            p.op("vector", lambda e: e.memset(cst_one[:], 1.0), inc="dve")
            p.op("vector", lambda e: e.memset(onesc_f[:], 1.0), inc="dve")
            p.op("vector", lambda e: e.memset(carry[:], 0.0), inc="dve")
            p.op("vector", lambda e: e.memset(halo[:], 0.0), inc="dve")
            for dst, src in [(convw[:], conv_w4), (convb[:], conv_b4),
                             (dtw[:], dtw4), (dtb[:], dtb4), (d_col[:], d4),
                             (ident[:], ident_h)]:
                p.op("sync", lambda e, d=dst, s=src: e.dma_start(out=d, in_=s),
                     inc="ld", inck=16)

            def stage_cast(dst_h, src, nelem):
                st32 = big_a[:, :nelem]
                p.op("sync", lambda e, s=src, dd=st32: e.dma_start(out=dd, in_=s),
                     inc="ld", inck=16, wait=("dve",))
                p.op("vector", lambda e, dd=dst_h, s=st32: e.tensor_copy(
                    out=dd, in_=s), inc="dve", wait=("ld",))
            stage_cast(wxs_h[:].rearrange("p a b -> p (a b)"), w_in_xs, 8 * DSL)
            stage_cast(wz_h[:].rearrange("p a b -> p (a b)"), w_in_z, 8 * DSL)
            stage_cast(wxp_h[:].rearrange("p a b -> p (a b)"), w_xp, NDB * 129)
            stage_cast(wout_h[:].rearrange("p a b c -> p (a b c)"), w_out_sl,
                       NDB * 8 * 128)
            p.op("vector", lambda e: e.tensor_copy(out=ident_f[:],
                                                   in_=ident[:]),
                 inc="dve", wait=("ld",))
            asl_f = a_sl[:].rearrange("p a b -> p (a b)")
            p.op("sync", lambda e: e.dma_start(out=asl_f, in_=alog4),
                 inc="ld", inck=16, wait=("dve",))
            p.op("scalar", lambda e: e.activation(
                out=big_a[:, :NDB * DS], in_=asl_f, func=AF.Exp),
                inc="act", wait=("ld",))
            p.op("vector", lambda e: e.tensor_scalar_mul(
                out=asl_f, in0=big_a[:, :NDB * DS], scalar1=-1.0),
                inc="dve", wait=("act",))

            crow_t = xt[0:1, :]
            p.op("sync", lambda e: e.dma_start(out=crow_t, in_=c_row),
                 inc="ld", inck=16)
            csil = bct[0:1, :D]
            p.op("scalar", lambda e: e.activation(out=csil, in_=crow_t,
                                                  func=AF.Silu),
                 inc="act", wait=("ld",))
            for kc in range(8):
                p.op("tensor", lambda e, kc=kc: e.transpose(
                    out=ps_t[:, 0:1], in_=csil[:, kc * 128:(kc + 1) * 128],
                    identity=ident_f[0:1, 0:1]), inc="te",
                    wait=("dve", "ld", "act"))
                p.op("vector", lambda e, kc=kc: e.tensor_copy(
                    out=silc_f[:, kc:kc + 1], in_=ps_t[:, 0:1]),
                    inc="dve", wait=("te",))
            sa_r = big_a[0:1, 0 * D:1 * D]; ca_r = big_a[0:1, 1 * D:2 * D]
            ga_r = big_a[0:1, 2 * D:3 * D]; sb_r = big_a[0:1, 3 * D:4 * D]
            cb_r = big_c[0:1, 0 * D:1 * D]; gb_r = big_c[0:1, 1 * D:2 * D]
            segs = [sa_r, ca_r, ga_r, sb_r, cb_r, gb_r]
            for nch in range(12):
                st32 = big_b[:, :8 * 512]
                p.op("sync", lambda e, n=nch, dd=st32: e.dma_start(
                    out=dd.rearrange("p (k c) -> p k c", k=8),
                    in_=wmod_r[:, :, n * 512:(n + 1) * 512]),
                    inc="ld", inck=16, wait=("dve", "te"))
                for kc in range(8):
                    p.op("tensor", lambda e, kc=kc, s=st32: e.matmul(
                        out=ps_r[0:1, :], lhsT=silc_f[:, kc:kc + 1],
                        rhs=s.rearrange("p (k c) -> p k c", k=8)[:, kc, :],
                        start=(kc == 0), stop=(kc == 7)),
                        inc="te" if kc == 7 else None,
                        wait=("dve", "ld") if kc == 0 else ())
                tgt = segs[nch // 2][:, (nch % 2) * 512:(nch % 2) * 512 + 512]
                p.op("vector", lambda e, t=tgt: e.tensor_copy(
                    out=t, in_=ps_r[0:1, :]), inc="dve", wait=("te",))
            t1 = big_c[0:1, 2 * D:3 * D]
            t2 = big_c[0:1, 3 * D:4 * D]
            for g_r in (ga_r, gb_r):
                p.op("vector", lambda e, g=g_r: e.tensor_scalar(
                    out=t1, in0=g, scalar1=0.0, scalar2=2.0,
                    op0=ALU.is_ge, op1=ALU.mult), inc="dve", wait=("te",))
                p.op("vector", lambda e: e.tensor_single_scalar(
                    out=t1, in_=t1, scalar=1.0, op=ALU.subtract), inc="dve")
                p.op("vector", lambda e, g=g_r: e.tensor_scalar_mul(
                    out=t2, in0=g, scalar1=-1.0), inc="dve")
                p.op("vector", lambda e, g=g_r: e.tensor_tensor(
                    out=t2, in0=t2, in1=g, op=ALU.max), inc="dve")
                p.op("vector", lambda e: e.tensor_single_scalar(
                    out=t2, in_=t2, scalar=0.25, op=ALU.max), inc="dve")
                p.op("vector", lambda e, g=g_r: e.tensor_tensor(
                    out=g, in0=t1, in1=t2, op=ALU.mult), inc="dve")
            for c_r in (ca_r, cb_r):
                p.op("vector", lambda e, cr=c_r: e.tensor_single_scalar(
                    out=cr, in_=cr, scalar=1.0, op=ALU.add), inc="dve")
            for nmm, r in (("sa", sa_r), ("ca1", ca_r), ("ga", ga_r),
                           ("sbm", sb_r), ("cb1", cb_r), ("gb", gb_r)):
                for h in range(2):
                    p.op("tensor", lambda e, r=r, h=h: e.matmul(
                        out=ps_r, lhsT=ones_f[:],
                        rhs=r[:, h * 512:(h + 1) * 512], start=True, stop=True),
                        inc="te", wait=("dve",))
                    p.op("vector", lambda e, nm=nmm, h=h: e.tensor_copy(
                        out=reps[nm][:, h * 512:(h + 1) * 512], in_=ps_r),
                        inc="dve", wait=("te",))

        def ln1(c):
            for tt in range(QT):
                r0 = c * Q + tt * 128
                p.op("sync", lambda e, r0=r0: e.dma_start(
                    out=xt[:], in_=x_b[r0:r0 + 128, :]),
                    inc="ld", inck=16, wait=("dve", "te"))
                c0, c1, c2, c3, c4, c5 = cols
                p.op("vector", lambda e: e.reduce_sum(
                    c0[:], xt[:], axis=AX.X), inc="dve", wait=("ld",))
                p.op("vector", lambda e: e.tensor_scalar_mul(
                    out=c1[:], in0=c0[:], scalar1=1.0 / D), inc="dve")
                p.op("vector", lambda e: e.tensor_tensor(
                    out=bct[:, :D], in0=xt[:], in1=xt[:], op=ALU.mult),
                    inc="dve", wait=("gp",))
                p.op("vector", lambda e: e.reduce_sum(
                    c2[:], bct[:, :D], axis=AX.X), inc="dve")
                p.op("vector", lambda e: e.tensor_scalar_mul(
                    out=c3[:], in0=c2[:], scalar1=1.0 / D), inc="dve")
                p.op("vector", lambda e: e.tensor_tensor(
                    out=c4[:], in0=c1[:], in1=c1[:], op=ALU.mult), inc="dve")
                p.op("vector", lambda e: e.tensor_tensor(
                    out=c5[:], in0=c3[:], in1=c4[:], op=ALU.subtract),
                    inc="dve")
                p.op("scalar", lambda e: e.activation(
                    out=c0[:], in_=c5[:], func=AF.Sqrt, bias=cst_eps[:]),
                    inc="act", wait=("dve",))
                p.op("vector", lambda e: e.reciprocal(c2[:], c0[:]),
                     inc="dve", wait=("act",))
                hf32 = big_d[:, 0:D]
                p.op("vector", lambda e, h=hf32: e.tensor_scalar(
                    out=h, in0=xt[:], scalar1=c1[:], scalar2=c2[:],
                    op0=ALU.subtract, op1=ALU.mult), inc="dve")
                p.op("vector", lambda e, h=hf32: e.tensor_tensor(
                    out=h, in0=h, in1=reps["ca1"][:],
                    op=ALU.mult), inc="dve")
                p.op("vector", lambda e, h=hf32: e.tensor_tensor(
                    out=h, in0=h, in1=reps["sa"][:], op=ALU.add),
                    inc="dve")
                for kc in range(8):
                    p.op("tensor", lambda e, kc=kc, h=hf32: e.transpose(
                        out=ps_t[:, 0:128], in_=h[:, kc * 128:(kc + 1) * 128],
                        identity=ident_f[:]), inc="te", wait=("dve",))
                    p.op("vector", lambda e, kc=kc, tt=tt: e.tensor_copy(
                        out=hT3[:, kc, tt * 128:(tt + 1) * 128],
                        in_=ps_t[:, 0:128]),
                        inc="dve", wait=("te",))

        def in_proj_conv(c):
            for wsel, wtile in ((0, wxs_h), (1, wz_h)):
                for db in range(NDB):
                    for hf in range(2):
                        psd = ps_a if (db + hf) % 2 == 0 else ps_b
                        for kc in range(8):
                            p.op("tensor",
                                 lambda e, db=db, hf=hf, kc=kc, ps=psd, w=wtile:
                                 e.matmul(
                                     out=ps[:],
                                     lhsT=w[:, kc, db * 128:(db + 1) * 128],
                                     rhs=hT3[:, kc, hf * 512:(hf + 1) * 512],
                                     start=(kc == 0), stop=(kc == 7)),
                                 inc="te" if kc == 7 else None,
                                 wait=("dve",) if kc == 0 else ())
                        if wsel == 0:
                            p.op("vector",
                                 lambda e, db=db, hf=hf, ps=psd: e.tensor_copy(
                                     out=xsr[:, db,
                                             KC + hf * 512:KC + (hf + 1) * 512],
                                     in_=ps[:]), inc="dve", wait=("te",))
                        else:
                            p.op("vector",
                                 lambda e, db=db, hf=hf, ps=psd: e.tensor_copy(
                                     out=z3[:, db, hf * 512:(hf + 1) * 512],
                                     in_=ps[:]), inc="dve", wait=("te",))
            p.op("vector", lambda e: e.tensor_copy(
                out=xsr[:, :, 1:KC], in_=halo[:]), inc="dve")
            p.op("vector", lambda e: e.tensor_copy(
                out=halo[:], in_=xsr[:, :, KC + Q - (KC - 1):KC + Q]),
                inc="dve")
            for db in range(NDB):
                p.op("vector", lambda e, db=db: e.tensor_scalar(
                    out=tmp_a[:, db, :], in0=xsr[:, db, 1:1 + Q],
                    scalar1=convw[:, db, 0:1], scalar2=None, op0=ALU.mult),
                    inc="dve", wait=("gp",) if db == 0 else ())
                for k in range(1, KC):
                    p.op("vector", lambda e, db=db, k=k: e.scalar_tensor_tensor(
                        out=tmp_a[:, db, :], in0=xsr[:, db, 1 + k:1 + k + Q],
                        scalar=convw[:, db, k:k + 1], in1=tmp_a[:, db, :],
                        op0=ALU.mult, op1=ALU.add), inc="dve")
                p.op("scalar", lambda e, db=db: e.activation(
                    out=xssi3[:, db, :], in_=tmp_a[:, db, :], func=AF.Silu,
                    bias=convb[:, db:db + 1]), inc="act", wait=("dve",))

        def x_proj_reduce(c):
            for hf in range(2):
                for db in range(NDB):
                    p.op("tensor", lambda e, db=db, hf=hf: e.matmul(
                        out=ps_a, lhsT=wxp_h[:, db, 0:128],
                        rhs=xssi3[:, db, hf * 512:(hf + 1) * 512],
                        start=(db == 0), stop=(db == NDB - 1)),
                        inc="te" if db == NDB - 1 else None,
                        wait=("act", "dve") if db == 0 else ())
                p.op("vector", lambda e, hf=hf: e.tensor_copy(
                    out=bct[:, hf * 512:(hf + 1) * 512], in_=ps_a),
                    inc="dve", wait=("te",))
                for db in range(NDB):
                    p.op("tensor", lambda e, db=db, hf=hf: e.matmul(
                        out=ps_b[0:1, :], lhsT=wxp_h[:, db, 128:129],
                        rhs=xssi3[:, db, hf * 512:(hf + 1) * 512],
                        start=(db == 0), stop=(db == NDB - 1)),
                        inc="te" if db == NDB - 1 else None)
                p.op("vector", lambda e, hf=hf: e.tensor_copy(
                    out=xt[0:1, hf * 512:(hf + 1) * 512], in_=ps_b[0:1, :]),
                    inc="dve", wait=("te",))
            p.op("gpsimd", lambda e, c=c: e.dma_start(
                out=xp_part[c][0:128, :], in_=bct[:]), inc="gp", inck=16,
                wait=("dve",))
            p.op("gpsimd", lambda e, c=c: e.dma_start(
                out=xp_part[c][128:129, :], in_=xt[0:1, :Q]),
                inc="gp", inck=16)
            ccn = new_cc()
            p.op("gpsimd", lambda e, c=c: e.collective_compute(
                "AllReduce", ALU.add, replica_groups=G2,
                ins=[xp_part[c]], outs=[xp_red[c]]), inc=ccn, wait=("gp",))
            p.op("sync", lambda e, c=c: e.dma_start(
                out=bct[:], in_=xp_red[c][0:128, :]), inc="ld", inck=16,
                wait=((ccn, 1), "dve"))
            p.op("sync", lambda e, c=c: e.dma_start(
                out=xt[0:1, :Q], in_=xp_red[c][128:129, :]),
                inc="ld", inck=16, wait=("te",))
            p.op("vector", lambda e: e.tensor_copy(out=bc_h, in_=bct[:]),
                 inc="dve", wait=("ld", "gp"))

        def dt_u_build(c):
            for hf in range(2):
                p.op("tensor", lambda e, hf=hf: e.matmul(
                    out=ps_r, lhsT=ones_f[:],
                    rhs=xt[0:1, hf * 512:(hf + 1) * 512], start=True,
                    stop=True), inc="te", wait=("dve", "ld"))
                for db in range(NDB):
                    p.op("vector", lambda e, db=db, hf=hf: e.tensor_scalar(
                        out=tmp_a[:, db, hf * 512:(hf + 1) * 512], in0=ps_r,
                        scalar1=dtw[:, db:db + 1], scalar2=dtb[:, db:db + 1],
                        op0=ALU.mult, op1=ALU.add), inc="dve",
                        wait=("te", "gp") if db == 0 else ("te",))
            p.op("vector", lambda e: e.tensor_scalar(
                out=dt_t[:], in0=tmp_a[:], scalar1=0.0, scalar2=None,
                op0=ALU.max), inc="dve", wait=("gp",))
            p.op("scalar", lambda e: e.activation(
                out=dAv[:], in_=tmp_a[:], func=AF.Abs), inc="act",
                wait=("dve",))
            p.op("scalar", lambda e: e.activation(
                out=tmp_a[:], in_=dAv[:], func=AF.Exp, scale=cst_neg1[:]),
                inc="act")
            p.op("scalar", lambda e: e.activation(
                out=dAv[:], in_=tmp_a[:], func=AF.Ln, bias=cst_one[:]), inc="act")
            p.op("vector", lambda e: e.tensor_tensor(
                out=dt_t[:], in0=dt_t[:], in1=dAv[:], op=ALU.add),
                inc="dve", wait=("act",))
            p.op("vector", lambda e: e.tensor_tensor(
                out=u_h, in0=big_a[:], in1=xssi, op=ALU.mult),
                inc="dve")

        def scan(c):
            dabuf = [dAv, yacc]
            NB3 = 3
            gp_marks = {}
            dve_marks = {}

            def emit_bcast(n):
                nb = n % NB3
                p.op("gpsimd", lambda e, n=n, nb=nb: e.dma_start(
                    out=bb[nb],
                    in_=bc_h[n:n + 1, :].unsqueeze(1)
                    .broadcast_to([1, 128, Q])), inc="gp", inck=16,
                    wait=("dve",))
                p.op("gpsimd", lambda e, n=n, nb=nb: e.dma_start(
                    out=cbb[nb],
                    in_=bc_h[64 + n:65 + n, :].unsqueeze(1)
                    .broadcast_to([1, 128, Q])), inc="gp", inck=16)
                gp_marks[n] = p.cnt["gp"]
            emit_bcast(0)
            emit_bcast(1)
            dve_marks[-2] = dve_marks[-1] = p.cnt["dve"]
            for n in range(DS):
                nb = n % NB3
                da = dabuf[n % 2]
                for db in range(NDB):
                    p.op("scalar", lambda e, db=db, n=n, da=da: e.activation(
                        out=da[:, db, :], in_=dt_t[:, db, :], func=AF.Exp,
                        scale=a_sl[:, db, n:n + 1]),
                        inc="act" if db == NDB - 1 else None,
                        wait=(("dve", dve_marks[n - 2]),) if db == 0 else ())
                if n + 2 < DS:
                    emit_bcast(n + 2)
                for db in range(NDB):
                    p.op("vector", lambda e, db=db, nb=nb: e.tensor_tensor(
                        out=inj3[:, db, :], in0=u3[:, db, :], in1=bb[nb],
                        op=ALU.mult), inc="dve",
                        wait=(("act", p.cnt["act"]), ("gp", gp_marks[n]))
                        if db == 0 else ())
                for db in range(NDB):
                    p.op("vector", lambda e, db=db, n=n, da=da:
                         e.tensor_tensor_scan(
                             out=da[:, db, :], data0=da[:, db, :],
                             data1=inj3[:, db, :],
                             initial=carry[:, db, n:n + 1],
                             op0=ALU.mult, op1=ALU.add), inc="dve")
                p.op("vector", lambda e, n=n, da=da: e.tensor_copy(
                    out=carry[:, :, n:n + 1], in_=da[:, :, Q - 1:Q]),
                    inc="dve")
                p.op("vector", lambda e, nb=nb, da=da: e.tensor_tensor(
                    out=tmp_a[:], in0=da[:],
                    in1=cbb[nb].unsqueeze(1).broadcast_to([128, NDB, Q]),
                    op=ALU.mult), inc="dve", wait=("te",))
                for db in range(NDB):
                    for tb in range(8):
                        blk = (db * 8 + tb) * 128
                        p.op("tensor",
                             lambda e, db=db, tb=tb, blk=blk, n=n: e.matmul(
                                 out=ps_all[:, blk:blk + 128],
                                 lhsT=tmp_a[:, db, tb * 128:(tb + 1) * 128],
                                 rhs=ident_f[:], is_transpose=True,
                                 start=(n == 0), stop=(n == DS - 1)),
                             inc="te" if (db == NDB - 1 and tb == 7) else None,
                             wait=("dve",) if (db == 0 and tb == 0) else ())
                dve_marks[n] = p.cnt["dve"]
            p.op("vector", lambda e: e.tensor_copy(
                out=big_c[:], in_=ps_all[:]), inc="dve", wait=("te",))
            for db in range(NDB):
                for tb in range(8):
                    blk = (db * 8 + tb) * 128
                    p.op("tensor", lambda e, blk=blk: e.matmul(
                        out=ps_all[:, blk:blk + 128],
                        lhsT=big_c[:, blk:blk + 128],
                        rhs=ident_f[:], is_transpose=True,
                        start=True, stop=True),
                        inc="te" if blk == (NDB * 8 - 1) * 128 else None,
                        wait=("dve",) if blk == 0 else ())
            p.op("vector", lambda e: e.tensor_copy(
                out=big_d[:], in_=ps_all[:]), inc="dve", wait=("te",))
            for db in range(NDB):
                p.op("vector", lambda e, db=db: e.scalar_tensor_tensor(
                    out=yacc[:, db, :], in0=xssi3[:, db, :],
                    scalar=d_col[:, db:db + 1], in1=yacc[:, db, :],
                    op0=ALU.mult, op1=ALU.add), inc="dve")
            p.op("scalar", lambda e: e.activation(
                out=u_h, in_=z_h, func=AF.Silu), inc="act",
                wait=("dve",))
            p.op("vector", lambda e: e.tensor_tensor(
                out=z_h, in0=big_d[:], in1=u_h, op=ALU.mult),
                inc="dve", wait=("act",))

        def out_proj(c):
            po_lo = dt_t
            po_hi = tmp_a
            for mt in range(8):
                dst = po_lo if mt < 4 else po_hi
                for hf in range(2):
                    psd = ps_a if (mt + hf) % 2 == 0 else ps_b
                    for db in range(NDB):
                        p.op("tensor", lambda e, mt=mt, hf=hf, db=db, ps=psd:
                             e.matmul(out=ps[:],
                                      lhsT=wout_h[:, db, mt, :],
                                      rhs=z3[:, db, hf * 512:(hf + 1) * 512],
                                      start=(db == 0), stop=(db == NDB - 1)),
                             inc="te" if db == NDB - 1 else None,
                             wait=("dve",) if db == 0 else ())
                    p.op("vector", lambda e, mt=mt, hf=hf, dd=dst, ps=psd:
                         e.tensor_copy(
                             out=dd[:, mt % 4, hf * 512:(hf + 1) * 512],
                             in_=ps[:]), inc="dve",
                         wait=("te", "ld") if (DBG and mt == 0 and hf == 0)
                         else ("te",))
            for sq in range(4):
                p.op("gpsimd", lambda e, c=c, sq=sq: e.dma_start(
                    out=po_d[c][sq * 1024:sq * 1024 + 512, :]
                    .rearrange("(a p) q -> p a q", p=128),
                    in_=po_lo[:, :, sq * NQ4:(sq + 1) * NQ4]),
                    inc="gp", inck=16, wait=("dve",) if sq == 0 else ())
                p.op("gpsimd", lambda e, c=c, sq=sq: e.dma_start(
                    out=po_d[c][sq * 1024 + 512:(sq + 1) * 1024, :]
                    .rearrange("(a p) q -> p a q", p=128),
                    in_=po_hi[:, :, sq * NQ4:(sq + 1) * NQ4]),
                    inc="gp", inck=16)
            ccn = new_cc()
            p.op("gpsimd", lambda e, c=c: e.collective_compute(
                "ReduceScatter", ALU.add, replica_groups=G2,
                ins=[po_d[c]], outs=[attn_q[c]]), inc=ccn, wait=("gp",))
            return ccn

        def ffn_weight_quant():
            mats = [(wg_t, wq_g_d, 0), (wu_t, wq_u_d, 1), (wd_t, wq_d_d, 2)]
            for src, dst, mi in mats:
                srcf = src.rearrange("p a b c -> p (a b c)")
                dstf = dst.rearrange("p a b c -> p (a b c)")
                acc, tmpc = cols[0], cols[1]
                p.op("vector", lambda e, a=acc: e.memset(a[:], 0.0), inc="dve")
                for ch in range(8):
                    st32 = big_a[:, :4096]
                    p.op("sync", lambda e, s=srcf, ch=ch, dd=st32: e.dma_start(
                        out=dd, in_=s[:, ch * 4096:(ch + 1) * 4096]),
                        inc="ld", inck=16, wait=("dve", "gp", "te"))
                    p.op("vector", lambda e, dd=st32, t=tmpc: e.reduce_sum(
                        t[:], dd, axis=AX.X, apply_absolute_value=True),
                        inc="dve", wait=("ld",))
                    p.op("vector", lambda e, a=acc, t=tmpc: e.tensor_tensor(
                        out=a[:], in0=a[:], in1=t[:], op=ALU.add), inc="dve")
                p.op("tensor", lambda e, a=acc: e.matmul(
                    out=ps_r[0:1, 0:1], lhsT=onesc_f[:], rhs=a[:],
                    start=True, stop=True), inc="te", wait=("dve",))
                p.op("vector", lambda e: e.tensor_scalar(
                    out=rows4[:, 0:1], in0=ps_r[0:1, 0:1],
                    scalar1=1.0 / (4096.0 * 1024.0), scalar2=1e-5,
                    op0=ALU.mult, op1=ALU.max), inc="dve", wait=("te",))
                p.op("vector", lambda e: e.reciprocal(
                    rows4[:, 1:2], rows4[:, 0:1]), inc="dve")
                p.op("gpsimd", lambda e: e.dma_start(
                    out=swb_d[0:1, 0:2], in_=rows4[:, 0:2]),
                    inc="gp", inck=16, wait=("dve",))
                p.op("sync", lambda e, mi=mi: e.dma_start(
                    out=sw3[:, mi:mi + 1],
                    in_=swb_d[0:1, 0:1].broadcast_to([128, 1])),
                    inc="ld", inck=16, wait=("gp",))
                p.op("sync", lambda e: e.dma_start(
                    out=swq[:], in_=swb_d[0:1, 1:2].broadcast_to([128, 1])),
                    inc="ld", inck=16)
                for ch in range(8):
                    st32 = big_a[:, :4096]
                    qv = big_b[:, :4096]
                    g2v = big_c[:, :4096]
                    st16 = wstr16[:, :4096]
                    p.op("sync", lambda e, s=srcf, ch=ch, dd=st32: e.dma_start(
                        out=dd, in_=s[:, ch * 4096:(ch + 1) * 4096]),
                        inc="ld", inck=16, wait=("dve",))
                    p.op("vector", lambda e, dd=st32, q=qv: e.tensor_scalar_mul(
                        out=q, in0=dd, scalar1=swq[:]), inc="dve",
                        wait=("ld", "gp"))
                    p.op("vector", lambda e, q=qv, g=g2v: e.tensor_single_scalar(
                        out=g, in_=q, scalar=-0.5, op=ALU.is_lt), inc="dve")
                    p.op("vector", lambda e, q=qv: e.tensor_single_scalar(
                        out=q, in_=q, scalar=0.5, op=ALU.is_gt), inc="dve")
                    p.op("vector", lambda e, q=qv, g=g2v: e.tensor_tensor(
                        out=q, in0=q, in1=g, op=ALU.subtract), inc="dve")
                    p.op("vector", lambda e, q=qv, dd=st16: e.tensor_copy(
                        out=dd, in_=q), inc="dve")
                    p.op("gpsimd", lambda e, dd=dstf, ch=ch, s=st16:
                         e.dma_start(out=dd[:, ch * 4096:(ch + 1) * 4096],
                                     in_=s), inc="gp", inck=16, wait=("dve",))

        def ffn_quant1(rs_sems):
            ntile = 0
            for c in range(NCH):
                aq32 = big_b[:, :8 * NQ4].rearrange("p (k t) -> p k t", k=8)
                p.op("sync", lambda e, c=c, a=aq32: e.dma_start(
                    out=a, in_=attn_q[c].rearrange("(k p) t -> p k t", p=128)),
                    inc="ld", inck=16, wait=((rs_sems[c], 1), "dve", "gp"))
                for tt in range(NQ4 // 128):
                    r0 = c * NQ4 + tt * 128
                    p.op("sync", lambda e, r0=r0: e.dma_start(
                        out=xt[:], in_=x_own[r0:r0 + 128, :]),
                        inc="ld", inck=16, wait=("dve", "te"))
                    x2t = big_d[:, :D]
                    for kc in range(8):
                        p.op("tensor", lambda e, kc=kc, tt=tt, a=aq32:
                             e.transpose(
                                 out=ps_t,
                                 in_=a[:, kc, tt * 128:(tt + 1) * 128],
                                 identity=ident_f[:]),
                             inc="te", wait=("ld", "dve"))
                        p.op("vector", lambda e, kc=kc: e.tensor_copy(
                            out=big_c[:, kc * 128:(kc + 1) * 128],
                            in_=ps_t), inc="dve", wait=("te",))
                    p.op("vector", lambda e: e.tensor_tensor(
                        out=x2t, in0=big_c[:, :D], in1=reps["ga"][:],
                        op=ALU.mult), inc="dve", wait=("ld",))
                    p.op("vector", lambda e: e.tensor_tensor(
                        out=x2t, in0=x2t, in1=xt[:], op=ALU.add), inc="dve")
                    p.op("gpsimd", lambda e, r0=r0: e.dma_start(
                        out=x2_d[r0:r0 + 128, :], in_=x2t),
                        inc="gp", inck=16, wait=("dve",))
                    c0, c1, c2, c3, c4, c5 = cols
                    h2 = big_c[:, :D]
                    p.op("vector", lambda e: e.reduce_sum(
                        c0[:], x2t, axis=AX.X), inc="dve")
                    p.op("vector", lambda e: e.tensor_scalar_mul(
                        out=c1[:], in0=c0[:], scalar1=1.0 / D), inc="dve")
                    p.op("vector", lambda e: e.tensor_tensor(
                        out=h2, in0=x2t, in1=x2t, op=ALU.mult), inc="dve")
                    p.op("vector", lambda e: e.reduce_sum(
                        c2[:], h2, axis=AX.X), inc="dve")
                    p.op("vector", lambda e: e.tensor_scalar_mul(
                        out=c3[:], in0=c2[:], scalar1=1.0 / D), inc="dve")
                    p.op("vector", lambda e: e.tensor_tensor(
                        out=c4[:], in0=c1[:], in1=c1[:], op=ALU.mult),
                        inc="dve")
                    p.op("vector", lambda e: e.tensor_tensor(
                        out=c5[:], in0=c3[:], in1=c4[:], op=ALU.subtract),
                        inc="dve")
                    p.op("scalar", lambda e: e.activation(
                        out=c0[:], in_=c5[:], func=AF.Sqrt, bias=cst_eps[:]),
                        inc="act", wait=("dve",))
                    p.op("vector", lambda e: e.reciprocal(c2[:], c0[:]),
                         inc="dve", wait=("act",))
                    p.op("vector", lambda e: e.tensor_scalar(
                        out=h2, in0=x2t, scalar1=c1[:], scalar2=c2[:],
                        op0=ALU.subtract, op1=ALU.mult), inc="dve")
                    p.op("vector", lambda e: e.tensor_tensor(
                        out=h2, in0=h2, in1=reps["cb1"][:], op=ALU.mult),
                        inc="dve")
                    p.op("vector", lambda e: e.tensor_tensor(
                        out=h2, in0=h2, in1=reps["sbm"][:], op=ALU.add),
                        inc="dve")
                    mx, sc, iv = cols[3], cols[4], cols[5]
                    p.op("vector", lambda e: e.reduce_max(
                        mx[:], h2, axis=AX.X, apply_absolute_value=True),
                        inc="dve")
                    p.op("vector", lambda e: e.tensor_single_scalar(
                        out=mx[:], in_=mx[:], scalar=1e-5, op=ALU.max),
                        inc="dve")
                    p.op("vector", lambda e: e.reciprocal(sc[:], mx[:]),
                         inc="dve")
                    p.op("vector", lambda e: e.tensor_scalar_mul(
                        out=sc[:], in0=sc[:], scalar1=127.0), inc="dve")
                    p.op("vector", lambda e: e.tensor_scalar_mul(
                        out=iv[:], in0=mx[:], scalar1=1.0 / 127.0), inc="dve")
                    p.op("vector", lambda e, nt=ntile: e.tensor_copy(
                        out=big_a[:, nt:nt + 1], in_=iv[:]), inc="dve")
                    q = big_d[:, D:2 * D]
                    p.op("vector", lambda e: e.tensor_scalar_mul(
                        out=q, in0=h2, scalar1=sc[:]), inc="dve")
                    p.op("vector", lambda e: e.tensor_scalar(
                        out=q, in0=q, scalar1=MAGIC, scalar2=MAGIC,
                        op0=ALU.add, op1=ALU.subtract), inc="dve")
                    p.op("vector", lambda e: e.tensor_scalar(
                        out=q, in0=q, scalar1=127.0, scalar2=-128.0,
                        op0=ALU.min, op1=ALU.max), inc="dve")
                    p.op("vector", lambda e: e.tensor_copy(
                        out=ht_h[:], in_=q), inc="dve")
                    p.op("gpsimd", lambda e, r0=r0: e.dma_start(
                        out=q1l_in[r0:r0 + 128, :], in_=ht_h[:]),
                        inc="gp", inck=16, wait=("dve",))
                    ntile += 1
            ccn = new_cc()
            p.op("gpsimd", lambda e: e.collective_compute(
                "AllGather", ALU.bypass, replica_groups=G1,
                ins=[q1l_in], outs=[q1l_out]), inc=ccn, wait=("gp",))
            return ccn

        def ffn_main(q1cc):
            invcols = big_a
            for blk in range(NB4):
                xqT = hT3
                for tt in range(4):
                    r0 = blk * 512 + tt * 128
                    p.op("sync", lambda e, r0=r0: e.dma_start(
                        out=ht_h[:], in_=q1l_out[r0:r0 + 128, :]),
                        inc="ld", inck=16, wait=((q1cc, 1), "dve", "te"))
                    p.op("vector", lambda e: e.tensor_copy(
                        out=xt[:], in_=ht_h[:]), inc="dve", wait=("ld",))
                    for kc in range(8):
                        p.op("tensor", lambda e, kc=kc: e.transpose(
                            out=ps_t[:, 0:128],
                            in_=xt[:, kc * 128:(kc + 1) * 128],
                            identity=ident_f[:]), inc="te", wait=("dve",))
                        p.op("vector", lambda e, kc=kc, tt=tt: e.tensor_copy(
                            out=xqT[:, kc, tt * 128:(tt + 1) * 128],
                            in_=ps_t[:, 0:128]), inc="dve", wait=("te",))
                er = xt[0:1, 0:512]
                for tt in range(4):
                    cidx = blk * 4 + tt
                    p.op("tensor", lambda e, cidx=cidx: e.transpose(
                        out=ps_t[0:1, 0:128], in_=invcols[:, cidx:cidx + 1],
                        identity=ident_f[:]), inc="te", wait=("dve",))
                    p.op("vector", lambda e, tt=tt: e.tensor_copy(
                        out=er[:, tt * 128:(tt + 1) * 128],
                        in_=ps_t[0:1, 0:128]), inc="dve", wait=("te",))
                epi_g = big_c[:, 0:512]
                epi_u = big_c[:, 512:1024]
                p.op("tensor", lambda e: e.matmul(
                    out=ps_r, lhsT=ones_f[:], rhs=er, start=True,
                    stop=True), inc="te", wait=("dve",))
                p.op("vector", lambda e: e.tensor_scalar_mul(
                    out=epi_g, in0=ps_r, scalar1=sw3[:, 0:1]),
                    inc="dve", wait=("te",))
                p.op("vector", lambda e: e.tensor_scalar_mul(
                    out=epi_u, in0=ps_r, scalar1=sw3[:, 1:2]), inc="dve")
                for mt in range(32):
                    p.op("sync", lambda e, mt=mt: e.dma_start(
                        out=wst3[:, 0:8, :], in_=wq_g_d[:, mt, :, :]),
                        inc="ld", inck=16, wait=("te", "dve"))
                    for kcc in range(8):
                        p.op("tensor", lambda e, kcc=kcc: e.matmul(
                            out=ps_a, lhsT=wst3[:, kcc, :],
                            rhs=xqT[:, kcc, 0:512], start=(kcc == 0),
                            stop=(kcc == 7)),
                            inc="te" if kcc == 7 else None,
                            wait=("ld", "dve") if kcc == 0 else ())
                    p.op("sync", lambda e, mt=mt: e.dma_start(
                        out=wst3[:, 8:16, :], in_=wq_u_d[:, mt, :, :]),
                        inc="ld", inck=16)
                    for kcc in range(8):
                        p.op("tensor", lambda e, kcc=kcc: e.matmul(
                            out=ps_b, lhsT=wst3[:, 8 + kcc, :],
                            rhs=xqT[:, kcc, 0:512], start=(kcc == 0),
                            stop=(kcc == 7)),
                            inc="te" if kcc == 7 else None,
                            wait=("ld", "dve") if kcc == 0 else ())
                    p.op("vector", lambda e: e.tensor_tensor(
                        out=gtl, in0=ps_a, in1=epi_g, op=ALU.mult),
                        inc="dve", wait=("te",))
                    p.op("scalar", lambda e: e.activation(
                        out=gtl2, in_=gtl, func=AF.Silu), inc="act",
                        wait=("dve",))
                    p.op("vector", lambda e: e.tensor_tensor(
                        out=utl, in0=ps_b, in1=epi_u, op=ALU.mult),
                        inc="dve")
                    p.op("vector", lambda e, mt=mt: e.tensor_tensor(
                        out=gu3[:, mt, :], in0=gtl2, in1=utl,
                        op=ALU.mult), inc="dve", wait=("act",))
                p.op("vector", lambda e: e.tensor_copy(
                    out=gtl, in_=gu3[:, 0, :]), inc="dve", wait=("te",))
                p.op("vector", lambda e: e.tensor_copy(
                    out=gtl2, in_=gu3[:, 0, :]), inc="dve")
                for mt in range(1, 32):
                    p.op("vector", lambda e, mt=mt: e.tensor_tensor(
                        out=gtl, in0=gtl, in1=gu3[:, mt, :],
                        op=ALU.max), inc="dve")
                    p.op("vector", lambda e, mt=mt: e.tensor_tensor(
                        out=gtl2, in0=gtl2, in1=gu3[:, mt, :],
                        op=ALU.min), inc="dve")
                p.op("vector", lambda e: e.tensor_scalar_mul(
                    out=gtl2, in0=gtl2, scalar1=-1.0), inc="dve")
                p.op("gpsimd", lambda e: e.tensor_reduce(
                    out=big_b[0:1, 0:512], in_=gtl,
                    axis=AX.C, op=ALU.max), inc="gpc", wait=("dve",))
                p.op("gpsimd", lambda e: e.tensor_reduce(
                    out=big_b[0:1, 512:1024], in_=gtl2,
                    axis=AX.C, op=ALU.max), inc="gpc")
                p.op("vector", lambda e: e.tensor_tensor(
                    out=xt[0:1, 0:512], in0=big_b[0:1, 0:512],
                    in1=big_b[0:1, 512:1024], op=ALU.max),
                    inc="dve", wait=("gpc", "te"))
                p.op("vector", lambda e: e.tensor_single_scalar(
                    out=xt[0:1, 0:512], in_=xt[0:1, 0:512], scalar=1e-5,
                    op=ALU.max), inc="dve")
                p.op("vector", lambda e: e.tensor_scalar_mul(
                    out=xt[0:1, 512:1024], in0=xt[0:1, 0:512],
                    scalar1=1.0 / 127.0), inc="dve")
                p.op("vector", lambda e: e.reciprocal(
                    xt[0:1, 0:512], xt[0:1, 0:512]), inc="dve")
                p.op("vector", lambda e: e.tensor_scalar_mul(
                    out=xt[0:1, 0:512], in0=xt[0:1, 0:512], scalar1=127.0),
                    inc="dve")
                p.op("tensor", lambda e: e.matmul(
                    out=ps_r, lhsT=ones_f[:], rhs=xt[0:1, 0:512],
                    start=True, stop=True), inc="te", wait=("dve",))
                s2rep = big_c[:, 1024:1536]
                p.op("vector", lambda e: e.tensor_copy(
                    out=s2rep, in_=ps_r), inc="dve", wait=("te",))
                for mt in range(32):
                    qq = big_b[:, 0:512]
                    p.op("vector", lambda e, mt=mt: e.tensor_tensor(
                        out=qq, in0=gu3[:, mt, :], in1=s2rep, op=ALU.mult),
                        inc="dve")
                    p.op("vector", lambda e: e.tensor_scalar(
                        out=qq, in0=qq, scalar1=MAGIC, scalar2=MAGIC,
                        op0=ALU.add, op1=ALU.subtract), inc="dve")
                    p.op("vector", lambda e: e.tensor_scalar(
                        out=qq, in0=qq, scalar1=127.0, scalar2=-128.0,
                        op0=ALU.min, op1=ALU.max), inc="dve")
                    p.op("vector", lambda e, mt=mt: e.tensor_copy(
                        out=gu3[:, mt, :], in_=qq), inc="dve")
                p.op("gpsimd", lambda e, blk=blk: e.dma_start(
                    out=q2l_in[blk].rearrange("(k p) t -> p k t", p=128),
                    in_=gu3[:]), inc="gp", inck=16, wait=("dve",))
                ccn = new_cc()
                p.op("gpsimd", lambda e, blk=blk: e.collective_compute(
                    "AllGather", ALU.bypass, replica_groups=G1,
                    ins=[q2l_in[blk]], outs=[q2l_out[blk]]), inc=ccn,
                    wait=("gp",))
                p.op("sync", lambda e, blk=blk: e.dma_start(
                    out=gu3[:],
                    in_=q2l_out[blk].rearrange("(k p) t -> p k t", p=128)),
                    inc="ld", inck=16, wait=((ccn, 1), "dve"))
                p.op("tensor", lambda e: e.matmul(
                    out=ps_r, lhsT=ones_f[:], rhs=xt[0:1, 512:1024],
                    start=True, stop=True), inc="te", wait=("dve", "ld"))
                epi_d = big_c[:, 0:512]
                p.op("vector", lambda e: e.tensor_scalar_mul(
                    out=epi_d, in0=ps_r, scalar1=sw3[:, 2:3]),
                    inc="dve", wait=("te",))
                for mt in range(8):
                    p.op("sync", lambda e, mt=mt: e.dma_start(
                        out=wst3[:, 0:32, :], in_=wq_d_d[:, mt, :, :]),
                        inc="ld", inck=16, wait=("te", "dve"))
                    for kcc in range(32):
                        p.op("tensor", lambda e, kcc=kcc: e.matmul(
                            out=ps_a, lhsT=wst3[:, kcc, :],
                            rhs=gu3[:, kcc, :], start=(kcc == 0),
                            stop=(kcc == 31)),
                            inc="te" if kcc == 31 else None,
                            wait=("ld", "dve") if kcc == 0 else ())
                    p.op("vector", lambda e, mt=mt: e.tensor_tensor(
                        out=ffn8[:, mt, :], in0=ps_a, in1=epi_d,
                        op=ALU.mult), inc="dve", wait=("te",))
                for tt in range(4):
                    r0 = blk * 512 + tt * 128
                    p.op("sync", lambda e, r0=r0: e.dma_start(
                        out=xt[:], in_=x2_d[r0:r0 + 128, :]),
                        inc="ld", inck=16, wait=("dve", "te", "gp"))
                    fft = big_c[:, 0:D]
                    for kc in range(8):
                        p.op("tensor", lambda e, kc=kc, tt=tt: e.transpose(
                            out=ps_t,
                            in_=ffn8[:, kc, tt * 128:(tt + 1) * 128],
                            identity=ident_f[:]), inc="te", wait=("dve",))
                        p.op("vector", lambda e, kc=kc: e.tensor_copy(
                            out=fft[:, kc * 128:(kc + 1) * 128], in_=ps_t),
                            inc="dve", wait=("te",))
                    x3 = big_c[:, D:2 * D]
                    p.op("vector", lambda e: e.tensor_tensor(
                        out=x3, in0=fft, in1=reps["gb"][:], op=ALU.mult),
                        inc="dve", wait=("ld",))
                    p.op("vector", lambda e: e.tensor_tensor(
                        out=x3, in0=x3, in1=xt[:], op=ALU.add), inc="dve")
                    p.op("gpsimd", lambda e, r0=r0: e.dma_start(
                        out=y_out[r0:r0 + 128, :], in_=x3),
                        inc="gp", inck=16, wait=("dve",))

        def dump(name, src):
            p.op("sync", lambda e, nm=name, s2=src: e.dma_start(
                out=dbg[nm], in_=s2),
                inc="ld", inck=16, wait=("dve", "act", "te", "gp"))

        setup()
        rs_sems = []
        for c in range(NCH):
            ln1(c)
            in_proj_conv(c)
            x_proj_reduce(c)
            dt_u_build(c)
            scan(c)
            if DBG and c == 0:
                dump("dbg_xt", xt[:])
                for ci in range(6):
                    p.op("sync", lambda e, ci=ci: e.dma_start(
                        out=dbg["dbg_cols"][:, ci:ci + 1], in_=cols[ci][:],
                        allow_slow_non_contiguous=True),
                        inc="ld", inck=16, wait=("dve", "act", "te", "gp"))
                dump("dbg_hth", ht_h[:])
                dump("dbg_hth2", ht_h2[:])
                dump("dbg_hT", hT[:])
                dump("dbg_xsr", xsr[:].rearrange("p a q -> p (a q)"))
                dump("dbg_xssi", xssi)
                dump("dbg_bct", bct[:])
                dump("dbg_dtr", xt[0:1, :Q])
                dump("dbg_dt", big_a[:])
                dump("dbg_u", u_h)
                dump("dbg_y", big_d[:])
                dump("dbg_yz", z_h)
                dump("dbg_carry", carry[:].rearrange("p a q -> p (a q)"))
                dump("dbg_ca1", reps["ca1"][:])
                dump("dbg_sa", reps["sa"][:])
                dump("dbg_ga", reps["ga"][:])
                dump("dbg_z", u_h)
                dump("dbg_asl", a_sl[:].rearrange("p a q -> p (a q)"))
            rs_sems.append(out_proj(c))
            if DBG and c == 0:
                p.op("sync", lambda e: e.dma_start(
                    out=dbg["dbg_aq"], in_=attn_q[0]),
                    inc="ld", inck=16, wait=((rs_sems[0], 1),))
        ffn_weight_quant()
        q1cc = ffn_quant1(rs_sems)
        ffn_main(q1cc)
        p.op("gpsimd", lambda e: e.engine_nop(), wait=("gp", "ld"))

        sem_names = ["ld", "gp", "te", "dve", "act", "gpc"] + cc_names
        with ExitStack() as sctx:
            H = {sn: sctx.enter_context(nc.semaphore(sn)) for sn in sem_names}
            with nc.Block() as block:
                @block.sync
                def _(e):
                    for fn in p.seq["sync"]:
                        fn(e, H)

                @block.scalar
                def _(e):
                    for fn in p.seq["scalar"]:
                        fn(e, H)

                @block.vector
                def _(e):
                    for fn in p.seq["vector"]:
                        fn(e, H)

                @block.tensor
                def _(e):
                    for fn in p.seq["tensor"]:
                        fn(e, H)

                @block.gpsimd
                def _(e):
                    for fn in p.seq["gpsimd"]:
                        fn(e, H)
    return nc


def make_inputs(core, L, Q, ins):
    import ml_dtypes
    g, j = core // 4, core % 4
    NCH = L // Q
    NQ4 = Q // 4
    sl = slice(j * DSL, (j + 1) * DSL)
    x = np.ascontiguousarray(np.asarray(ins["L0_attn_x"])[g])
    x_own = np.concatenate(
        [x[c * Q + j * NQ4: c * Q + (j + 1) * NQ4] for c in range(NCH)],
        axis=0)
    W = np.asarray(ins["L0_attn_in_proj_w"])
    d = {
        "x_b": x,
        "x_own": x_own,
        "w_in_xs": W.reshape(8, 128, 2 * DI)[:, :, sl].transpose(1, 0, 2),
        "w_in_z": W.reshape(8, 128, 2 * DI)
        [:, :, DI + j * DSL: DI + (j + 1) * DSL].transpose(1, 0, 2),
        "w_xp": np.asarray(ins["L0_attn_x_proj_w"])[sl]
        .reshape(NDB, 128, 129).transpose(1, 0, 2),
        "conv_w4": np.asarray(ins["L0_attn_conv_w"])[sl, 0, :]
        .reshape(NDB, 128, KC).transpose(1, 0, 2),
        "conv_b4": np.asarray(ins["L0_attn_conv_b"])[sl].reshape(NDB, 128).T,
        "dtw4": np.asarray(ins["L0_attn_dt_w"])[0, sl].reshape(NDB, 128).T,
        "dtb4": np.asarray(ins["L0_attn_dt_b"])[sl].reshape(NDB, 128).T,
        "alog4": np.asarray(ins["L0_attn_A_log"])[sl]
        .reshape(NDB, 128, DS).transpose(1, 0, 2),
        "d4": np.asarray(ins["L0_attn_D"])[sl].reshape(NDB, 128).T,
        "w_out_sl": np.asarray(ins["L0_attn_out_proj_w"])[sl]
        .reshape(NDB, 128, 8, 128).transpose(1, 0, 2, 3),
        "wg_t": np.asarray(ins["L0_ffn_w_gate"]).T
        .reshape(8, 128, 32, 128).transpose(1, 2, 0, 3),
        "wu_t": np.asarray(ins["L0_ffn_w_up"]).T
        .reshape(8, 128, 32, 128).transpose(1, 2, 0, 3),
        "wd_t": np.asarray(ins["L0_ffn_w_down"]).T
        .reshape(32, 128, 8, 128).transpose(1, 2, 0, 3),
        "c_row": np.asarray(ins["L0_c"])[g:g + 1],
        "wmod_r": np.asarray(ins["L0_wmod"]).reshape(8, 128, 6 * D)
        .transpose(1, 0, 2),
        "ident_h": np.eye(128, dtype=ml_dtypes.bfloat16),
    }
    return {k: np.ascontiguousarray(np.asarray(v, dtype=np.float32))
            if k != "ident_h" else v for k, v in d.items()}


def assemble(L, Q, results):
    NCH = L // Q
    NQ4 = Q // 4
    y = np.empty((2, L, D), np.float32)
    for core in range(8):
        g, j = core // 4, core % 4
        yo = results[core]["y_out"]
        for c in range(NCH):
            y[g, c * Q + j * NQ4: c * Q + (j + 1) * NQ4] = \
                yo[c * NQ4:(c + 1) * NQ4]
    return y


def kernel(**inputs):
    L, Q = 8192, 1024
    nc = build_nc(L, Q)
    in_maps = [make_inputs(c, L, Q, inputs) for c in range(8)]
    res = run_bass_kernel_spmd(nc, in_maps, core_ids=list(range(8)))
    return assemble(L, Q, res.results)


# version 03
# speedup vs baseline: 1.3012x; 1.0877x over previous
import numpy as np
from contextlib import ExitStack

import concourse.bass as bass
import concourse.mybir as mybir
from concourse.bass_utils import run_bass_kernel_spmd

F32 = mybir.dt.float32
BF16 = mybir.dt.bfloat16
AF = mybir.ActivationFunctionType
ALU = mybir.AluOpType
AX = mybir.AxisListType

MAGIC = float(np.float32(1.5 * 2.0 ** 23))
LN_EPS = 1e-5
D = 1024
DI = 2048
DSL = 512
NDB = 4
DS = 64
KC = 4
G2 = [[0, 1, 2, 3], [4, 5, 6, 7]]
G1 = [[c] for c in range(8)]

ENGS = ("sync", "scalar", "vector", "tensor", "gpsimd")


class Plan:
    def __init__(self):
        self.seq = {e: [] for e in ENGS}
        self.cnt = {}

    def sem(self, name):
        self.cnt.setdefault(name, 0)

    SELF_SEM = {"vector": "dve", "scalar": "act"}

    def op(self, eng, build, inc=None, inck=1, wait=(), sw=True):
        waits = []
        if sw and eng in self.SELF_SEM:
            sn = self.SELF_SEM[eng]
            waits.append((sn, self.cnt[sn]))
        for w in wait:
            if isinstance(w, tuple):
                waits.append(w)
            else:
                waits.append((w, self.cnt[w]))
        incs = None
        if inc is not None:
            self.cnt[inc] += inck
            incs = (inc, inck)

        def fn(e, H, waits=tuple(waits), build=build, incs=incs):
            for s, v in waits:
                if v > 0:
                    e.wait_ge(H[s], v)
            ins = build(e)
            if incs is not None:
                ins.then_inc(H[incs[0]], incs[1])
        self.seq[eng].append(fn)


DBG = False


def build_nc(L, Q):
    NCH = L // Q
    QT = Q // 128
    NQ4 = Q // 4
    OWN = NCH * NQ4
    NB4 = OWN // 512
    assert L % Q == 0 and Q == 1024 and NQ4 % 128 == 0 and OWN % 512 == 0

    nc = bass.Bass("TRN2", target_bir_lowering=False, debug=False,
                   detect_race_conditions=False)

    def din(name, shape, dt=F32):
        return nc.dram_tensor(name, shape, dt, kind="ExternalInput").ap()

    x_b = din("x_b", [L, D])
    x_own = din("x_own", [OWN, D])
    w_in_xs = din("w_in_xs", [128, 8, DSL])
    w_in_z = din("w_in_z", [128, 8, DSL])
    w_xp = din("w_xp", [128, NDB, 129])
    conv_w4 = din("conv_w4", [128, NDB, KC])
    conv_b4 = din("conv_b4", [128, NDB])
    dtw4 = din("dtw4", [128, NDB])
    dtb4 = din("dtb4", [128, NDB])
    alog4 = din("alog4", [128, NDB, DS])
    d4 = din("d4", [128, NDB])
    w_out_sl = din("w_out_sl", [128, NDB, 8, 128])
    wg_t = din("wg_t", [128, 32, 8, 128])
    wu_t = din("wu_t", [128, 32, 8, 128])
    wd_t = din("wd_t", [128, 8, 32, 128])
    c_row = din("c_row", [1, D])
    wmod_r = din("wmod_r", [128, 8, 6 * D])
    ident_h = din("ident_h", [128, 128], BF16)

    y_out = nc.dram_tensor("y_out", [OWN, D], F32, kind="ExternalOutput").ap()
    dbg = {}
    if DBG:
        for nm, shape, dt in [
                ("dbg_hT", [128, 8 * Q], BF16), ("dbg_xsr", [128, NDB * (KC + Q)], BF16),
                ("dbg_xssi", [128, NDB * Q], BF16), ("dbg_bct", [128, Q], F32),
                ("dbg_dtr", [1, Q], F32), ("dbg_dt", [128, NDB * Q], F32),
                ("dbg_u", [128, NDB * Q], BF16), ("dbg_y", [128, NDB * Q], F32),
                ("dbg_yz", [128, NDB * Q], BF16), ("dbg_carry", [128, NDB * DS], F32),
                ("dbg_ca1", [128, D], BF16), ("dbg_sa", [128, D], BF16),
                ("dbg_ga", [128, D], BF16), ("dbg_aq", [1024, NQ4], F32),
                ("dbg_z", [128, NDB * Q], BF16), ("dbg_asl", [128, NDB * DS], F32),
                ("dbg_xt", [128, D], F32), ("dbg_cols", [128, 6], F32),
                ("dbg_hth", [128, D], BF16), ("dbg_hth2", [128, D], BF16)]:
            dbg[nm] = nc.dram_tensor(nm, shape, dt, kind="ExternalOutput").ap()

    dscr = lambda n, s, dt=F32: nc.dram_tensor(n, s, dt).ap()
    xp_part = [dscr(f"xp_part{c}", [129, Q]) for c in range(NCH)]
    xp_red = [dscr(f"xp_red{c}", [129, Q]) for c in range(NCH)]
    po_d = [dscr(f"po_d{c}", [4 * 1024, NQ4]) for c in range(NCH)]
    attn_q = [dscr(f"attn_q{c}", [1024, NQ4]) for c in range(NCH)]
    wq_g_d = dscr("wq_g_d", [128, 32, 8, 128], BF16)
    wq_u_d = dscr("wq_u_d", [128, 32, 8, 128], BF16)
    wq_d_d = dscr("wq_d_d", [128, 8, 32, 128], BF16)
    x2_d = dscr("x2_d", [OWN, D])
    q1l_in = dscr("q1l_in", [OWN, D], BF16)
    q1l_out = dscr("q1l_out", [OWN, D], BF16)
    q2l_in = [dscr(f"q2l_in{b}", [4096, 512], BF16) for b in range(NB4)]
    q2l_out = [dscr(f"q2l_out{b}", [4096, 512], BF16) for b in range(NB4)]
    swb_d = dscr("swb_d", [1, 4])

    p = Plan()
    for s in ("ld", "gp", "te", "dve", "act", "gpc"):
        p.sem(s)
    cc_names = []

    def new_cc():
        n = f"cc{len(cc_names)}"
        cc_names.append(n)
        p.sem(n)
        return n

    with ExitStack() as ctx:
        sb = lambda n, s, dt=F32: ctx.enter_context(nc.sbuf_tensor(n, s, dt))
        psum = lambda n, s, dt=F32: ctx.enter_context(nc.psum_tensor(n, s, dt))

        wxs_h = sb("wxs_h", [128, 8, DSL], BF16)
        wz_h = sb("wz_h", [128, 8, DSL], BF16)
        wxp_h = sb("wxp_h", [128, NDB, 129], BF16)
        wout_h = sb("wout_h", [128, NDB, 8, 128], BF16)
        convw = sb("convw", [128, NDB, KC])
        convb = sb("convb", [128, NDB])
        dtw = sb("dtw", [128, NDB])
        dtb = sb("dtb", [128, NDB])
        a_sl = sb("a_sl", [128, NDB, DS])
        d_col = sb("d_col", [128, NDB])
        ident = sb("ident", [128, 128], BF16)
        ident_f = sb("ident_f", [128, 128])
        ones_f = sb("ones_f", [1, 128])
        onesc_f = sb("onesc_f", [128, 1])
        reps = {}
        for nmm in ("ca1", "sa", "ga", "cb1", "sbm", "gb"):
            reps[nmm] = sb("rep_" + nmm, [128, D], BF16)
        carry = sb("carry", [128, NDB, DS])
        halo = sb("halo", [128, NDB, KC - 1], BF16)
        silc_f = sb("silc_f", [128, 8])
        sw3 = sb("sw3", [128, 4])
        swq = sb("swq", [128, 1])
        cols = [sb(f"col{i}", [128, 1]) for i in range(6)]
        rows4 = sb("rows4", [1, 16])
        cst_eps = sb("cst_eps", [128, 1])
        cst_neg1 = sb("cst_neg1", [128, 1])
        cst_one = sb("cst_one", [128, 1])

        big_a = sb("big_a", [128, NDB * Q])
        big_b = sb("big_b", [128, NDB * Q])
        big_c = sb("big_c", [128, NDB * Q])
        big_d = sb("big_d", [128, NDB * Q])
        xt = sb("xt", [128, D])
        bct = sb("bct", [128, Q])
        dtr_t = sb("dtr_t", [1, Q])
        hT = sb("hT", [128, 8 * Q], BF16)
        NXS = NDB * (KC + Q)
        bfB = sb("bfB", [128, NXS + 4 * NDB * Q + 9 * Q], BF16)
        o = 0
        def carve(n):
            nonlocal o
            v = bfB[:, o:o + n]
            o += n
            return v
        xsr = carve(NXS).rearrange("p (a q) -> p a q", a=NDB)
        xssi = carve(NDB * Q)
        z_h = carve(NDB * Q)
        u_h = carve(NDB * Q)
        inj = carve(NDB * Q)
        bb = [carve(Q) for i in range(4)]
        cbb = [carve(Q) for i in range(4)]
        bc_h = carve(Q)
        ht_h = sb("ht_h", [128, D], BF16)
        ht_h2 = sb("ht_h2", [128, D], BF16)
        gu16 = bfB[:, 0:32 * 512]
        wstr16 = bfB[:, 32 * 512:32 * 512 + 32 * 128]
        gtl = bfB[:, 20480:20992]
        gtl2 = bfB[:, 20992:21504]
        utl = bfB[:, 21504:22016]
        ps_all = psum("ps_all", [128, 4096])
        ps_a = ps_all[:, 0:512]
        ps_b = ps_all[:, 512:1024]
        ps_r = ps_all[:, 1024:1536]
        ps_t = ps_all[:, 1536:1664]

        dt_t = big_a[:].rearrange("p (a q) -> p a q", a=NDB)
        tmp_a = big_b[:].rearrange("p (a q) -> p a q", a=NDB)
        dAv = big_c[:].rearrange("p (a q) -> p a q", a=NDB)
        yacc = big_d[:].rearrange("p (a q) -> p a q", a=NDB)
        xssi3 = xssi.rearrange("p (a q) -> p a q", a=NDB)
        z3 = z_h.rearrange("p (a q) -> p a q", a=NDB)
        u3 = u_h.rearrange("p (a q) -> p a q", a=NDB)
        inj3 = inj.rearrange("p (a q) -> p a q", a=NDB)
        hT3 = hT[:].rearrange("p (k q) -> p k q", k=8)
        gu3 = gu16.rearrange("p (m t) -> p m t", m=32)
        wst3 = wstr16.rearrange("p (k m) -> p k m", m=128)
        ffn8 = big_d[:].rearrange("p (m t) -> p m t", m=8)

        def bcast3(t, n):
            return t.unsqueeze(1).broadcast_to([128, n, t.shape[-1]])

        def setup():
            p.op("vector", lambda e: e.memset(ones_f[:], 1.0), inc="dve")
            p.op("vector", lambda e: e.memset(cst_eps[:], LN_EPS), inc="dve")
            p.op("vector", lambda e: e.memset(cst_neg1[:], -1.0), inc="dve")
            p.op("vector", lambda e: e.memset(cst_one[:], 1.0), inc="dve")
            p.op("vector", lambda e: e.memset(onesc_f[:], 1.0), inc="dve")
            p.op("vector", lambda e: e.memset(carry[:], 0.0), inc="dve")
            p.op("vector", lambda e: e.memset(halo[:], 0.0), inc="dve")
            for dst, src in [(convw[:], conv_w4), (convb[:], conv_b4),
                             (dtw[:], dtw4), (dtb[:], dtb4), (d_col[:], d4),
                             (ident[:], ident_h)]:
                p.op("sync", lambda e, d=dst, s=src: e.dma_start(out=d, in_=s),
                     inc="ld", inck=16)

            def stage_cast(dst_h, src, nelem):
                st32 = big_a[:, :nelem]
                p.op("sync", lambda e, s=src, dd=st32: e.dma_start(out=dd, in_=s),
                     inc="ld", inck=16, wait=("dve",))
                p.op("vector", lambda e, dd=dst_h, s=st32: e.tensor_copy(
                    out=dd, in_=s), inc="dve", wait=("ld",))
            stage_cast(wxs_h[:].rearrange("p a b -> p (a b)"), w_in_xs, 8 * DSL)
            stage_cast(wz_h[:].rearrange("p a b -> p (a b)"), w_in_z, 8 * DSL)
            stage_cast(wxp_h[:].rearrange("p a b -> p (a b)"), w_xp, NDB * 129)
            stage_cast(wout_h[:].rearrange("p a b c -> p (a b c)"), w_out_sl,
                       NDB * 8 * 128)
            p.op("vector", lambda e: e.tensor_copy(out=ident_f[:],
                                                   in_=ident[:]),
                 inc="dve", wait=("ld",))
            asl_f = a_sl[:].rearrange("p a b -> p (a b)")
            p.op("sync", lambda e: e.dma_start(out=asl_f, in_=alog4),
                 inc="ld", inck=16, wait=("dve",))
            p.op("scalar", lambda e: e.activation(
                out=big_a[:, :NDB * DS], in_=asl_f, func=AF.Exp),
                inc="act", wait=("ld",))
            p.op("vector", lambda e: e.tensor_scalar_mul(
                out=asl_f, in0=big_a[:, :NDB * DS], scalar1=-1.0),
                inc="dve", wait=("act",))

            crow_t = xt[0:1, :]
            p.op("sync", lambda e: e.dma_start(out=crow_t, in_=c_row),
                 inc="ld", inck=16)
            csil = bct[0:1, :D]
            p.op("scalar", lambda e: e.activation(out=csil, in_=crow_t,
                                                  func=AF.Silu),
                 inc="act", wait=("ld",))
            for kc in range(8):
                p.op("tensor", lambda e, kc=kc: e.transpose(
                    out=ps_t[:, 0:1], in_=csil[:, kc * 128:(kc + 1) * 128],
                    identity=ident_f[0:1, 0:1]), inc="te",
                    wait=("dve", "ld", "act"))
                p.op("vector", lambda e, kc=kc: e.tensor_copy(
                    out=silc_f[:, kc:kc + 1], in_=ps_t[:, 0:1]),
                    inc="dve", wait=("te",))
            sa_r = big_a[0:1, 0 * D:1 * D]; ca_r = big_a[0:1, 1 * D:2 * D]
            ga_r = big_a[0:1, 2 * D:3 * D]; sb_r = big_a[0:1, 3 * D:4 * D]
            cb_r = big_c[0:1, 0 * D:1 * D]; gb_r = big_c[0:1, 1 * D:2 * D]
            segs = [sa_r, ca_r, ga_r, sb_r, cb_r, gb_r]
            for nch in range(12):
                st32 = big_b[:, :8 * 512]
                p.op("sync", lambda e, n=nch, dd=st32: e.dma_start(
                    out=dd.rearrange("p (k c) -> p k c", k=8),
                    in_=wmod_r[:, :, n * 512:(n + 1) * 512]),
                    inc="ld", inck=16, wait=("dve", "te"))
                for kc in range(8):
                    p.op("tensor", lambda e, kc=kc, s=st32: e.matmul(
                        out=ps_r[0:1, :], lhsT=silc_f[:, kc:kc + 1],
                        rhs=s.rearrange("p (k c) -> p k c", k=8)[:, kc, :],
                        start=(kc == 0), stop=(kc == 7)),
                        inc="te" if kc == 7 else None,
                        wait=("dve", "ld") if kc == 0 else ())
                tgt = segs[nch // 2][:, (nch % 2) * 512:(nch % 2) * 512 + 512]
                p.op("vector", lambda e, t=tgt: e.tensor_copy(
                    out=t, in_=ps_r[0:1, :]), inc="dve", wait=("te",))
            t1 = big_c[0:1, 2 * D:3 * D]
            t2 = big_c[0:1, 3 * D:4 * D]
            for g_r in (ga_r, gb_r):
                p.op("vector", lambda e, g=g_r: e.tensor_scalar(
                    out=t1, in0=g, scalar1=0.0, scalar2=2.0,
                    op0=ALU.is_ge, op1=ALU.mult), inc="dve", wait=("te",))
                p.op("vector", lambda e: e.tensor_single_scalar(
                    out=t1, in_=t1, scalar=1.0, op=ALU.subtract), inc="dve")
                p.op("vector", lambda e, g=g_r: e.tensor_scalar_mul(
                    out=t2, in0=g, scalar1=-1.0), inc="dve")
                p.op("vector", lambda e, g=g_r: e.tensor_tensor(
                    out=t2, in0=t2, in1=g, op=ALU.max), inc="dve")
                p.op("vector", lambda e: e.tensor_single_scalar(
                    out=t2, in_=t2, scalar=0.25, op=ALU.max), inc="dve")
                p.op("vector", lambda e, g=g_r: e.tensor_tensor(
                    out=g, in0=t1, in1=t2, op=ALU.mult), inc="dve")
            for c_r in (ca_r, cb_r):
                p.op("vector", lambda e, cr=c_r: e.tensor_single_scalar(
                    out=cr, in_=cr, scalar=1.0, op=ALU.add), inc="dve")
            for nmm, r in (("sa", sa_r), ("ca1", ca_r), ("ga", ga_r),
                           ("sbm", sb_r), ("cb1", cb_r), ("gb", gb_r)):
                for h in range(2):
                    p.op("tensor", lambda e, r=r, h=h: e.matmul(
                        out=ps_r, lhsT=ones_f[:],
                        rhs=r[:, h * 512:(h + 1) * 512], start=True, stop=True),
                        inc="te", wait=("dve",))
                    p.op("vector", lambda e, nm=nmm, h=h: e.tensor_copy(
                        out=reps[nm][:, h * 512:(h + 1) * 512], in_=ps_r),
                        inc="dve", wait=("te",))

        def ln1(c):
            for tt in range(QT):
                r0 = c * Q + tt * 128
                p.op("sync", lambda e, r0=r0: e.dma_start(
                    out=xt[:], in_=x_b[r0:r0 + 128, :]),
                    inc="ld", inck=16, wait=("dve", "te"))
                c0, c1, c2, c3, c4, c5 = cols
                p.op("vector", lambda e: e.reduce_sum(
                    c0[:], xt[:], axis=AX.X), inc="dve", wait=("ld",))
                p.op("vector", lambda e: e.tensor_scalar_mul(
                    out=c1[:], in0=c0[:], scalar1=1.0 / D), inc="dve")
                p.op("vector", lambda e: e.tensor_tensor(
                    out=big_c[:, :D], in0=xt[:], in1=xt[:], op=ALU.mult),
                    inc="dve", wait=("gp",))
                p.op("vector", lambda e: e.reduce_sum(
                    c2[:], big_c[:, :D], axis=AX.X), inc="dve")
                p.op("vector", lambda e: e.tensor_scalar_mul(
                    out=c3[:], in0=c2[:], scalar1=1.0 / D), inc="dve")
                p.op("vector", lambda e: e.tensor_tensor(
                    out=c4[:], in0=c1[:], in1=c1[:], op=ALU.mult), inc="dve")
                p.op("vector", lambda e: e.tensor_tensor(
                    out=c5[:], in0=c3[:], in1=c4[:], op=ALU.subtract),
                    inc="dve")
                p.op("scalar", lambda e: e.activation(
                    out=c0[:], in_=c5[:], func=AF.Sqrt, bias=cst_eps[:]),
                    inc="act", wait=("dve",))
                p.op("vector", lambda e: e.reciprocal(c2[:], c0[:]),
                     inc="dve", wait=("act",))
                hf32 = big_d[:, 0:D]
                p.op("vector", lambda e, h=hf32: e.tensor_scalar(
                    out=h, in0=xt[:], scalar1=c1[:], scalar2=c2[:],
                    op0=ALU.subtract, op1=ALU.mult), inc="dve")
                p.op("vector", lambda e, h=hf32: e.tensor_tensor(
                    out=h, in0=h, in1=reps["ca1"][:],
                    op=ALU.mult), inc="dve")
                p.op("vector", lambda e, h=hf32: e.tensor_tensor(
                    out=h, in0=h, in1=reps["sa"][:], op=ALU.add),
                    inc="dve")
                for kc in range(8):
                    p.op("tensor", lambda e, kc=kc, h=hf32: e.transpose(
                        out=ps_t[:, 0:128], in_=h[:, kc * 128:(kc + 1) * 128],
                        identity=ident_f[:]), inc="te", wait=("dve",))
                    p.op("vector", lambda e, kc=kc, tt=tt: e.tensor_copy(
                        out=hT3[:, kc, tt * 128:(tt + 1) * 128],
                        in_=ps_t[:, 0:128]),
                        inc="dve", wait=("te",))

        def in_proj_conv(c):
            for wsel, wtile in ((0, wxs_h), (1, wz_h)):
                for db in range(NDB):
                    for hf in range(2):
                        psd = ps_a if (db + hf) % 2 == 0 else ps_b
                        for kc in range(8):
                            p.op("tensor",
                                 lambda e, db=db, hf=hf, kc=kc, ps=psd, w=wtile:
                                 e.matmul(
                                     out=ps[:],
                                     lhsT=w[:, kc, db * 128:(db + 1) * 128],
                                     rhs=hT3[:, kc, hf * 512:(hf + 1) * 512],
                                     start=(kc == 0), stop=(kc == 7)),
                                 inc="te" if kc == 7 else None,
                                 wait=("dve",) if kc == 0 else ())
                        if wsel == 0:
                            p.op("vector",
                                 lambda e, db=db, hf=hf, ps=psd: e.tensor_copy(
                                     out=xsr[:, db,
                                             KC + hf * 512:KC + (hf + 1) * 512],
                                     in_=ps[:]), inc="dve", wait=("te",))
                        else:
                            p.op("vector",
                                 lambda e, db=db, hf=hf, ps=psd: e.tensor_copy(
                                     out=z3[:, db, hf * 512:(hf + 1) * 512],
                                     in_=ps[:]), inc="dve", wait=("te",))
            p.op("vector", lambda e: e.tensor_copy(
                out=xsr[:, :, 1:KC], in_=halo[:]), inc="dve")
            p.op("vector", lambda e: e.tensor_copy(
                out=halo[:], in_=xsr[:, :, KC + Q - (KC - 1):KC + Q]),
                inc="dve")
            for db in range(NDB):
                p.op("vector", lambda e, db=db: e.tensor_scalar(
                    out=tmp_a[:, db, :], in0=xsr[:, db, 1:1 + Q],
                    scalar1=convw[:, db, 0:1], scalar2=None, op0=ALU.mult),
                    inc="dve", wait=("gp",) if db == 0 else ())
                for k in range(1, KC):
                    p.op("vector", lambda e, db=db, k=k: e.scalar_tensor_tensor(
                        out=tmp_a[:, db, :], in0=xsr[:, db, 1 + k:1 + k + Q],
                        scalar=convw[:, db, k:k + 1], in1=tmp_a[:, db, :],
                        op0=ALU.mult, op1=ALU.add), inc="dve")
                p.op("scalar", lambda e, db=db: e.activation(
                    out=xssi3[:, db, :], in_=tmp_a[:, db, :], func=AF.Silu,
                    bias=convb[:, db:db + 1]), inc="act", wait=("dve",))

        def x_proj_reduce(c):
            for hf in range(2):
                for db in range(NDB):
                    p.op("tensor", lambda e, db=db, hf=hf: e.matmul(
                        out=ps_a, lhsT=wxp_h[:, db, 0:128],
                        rhs=xssi3[:, db, hf * 512:(hf + 1) * 512],
                        start=(db == 0), stop=(db == NDB - 1)),
                        inc="te" if db == NDB - 1 else None,
                        wait=("act", "dve") if db == 0 else ())
                p.op("vector", lambda e, hf=hf: e.tensor_copy(
                    out=bct[:, hf * 512:(hf + 1) * 512], in_=ps_a),
                    inc="dve", wait=("te",))
                for db in range(NDB):
                    p.op("tensor", lambda e, db=db, hf=hf: e.matmul(
                        out=ps_b[0:1, :], lhsT=wxp_h[:, db, 128:129],
                        rhs=xssi3[:, db, hf * 512:(hf + 1) * 512],
                        start=(db == 0), stop=(db == NDB - 1)),
                        inc="te" if db == NDB - 1 else None)
                p.op("vector", lambda e, hf=hf: e.tensor_copy(
                    out=dtr_t[:, hf * 512:(hf + 1) * 512], in_=ps_b[0:1, :]),
                    inc="dve", wait=("te",))
            p.op("gpsimd", lambda e, c=c: e.dma_start(
                out=xp_part[c][0:128, :], in_=bct[:]), inc="gp", inck=16,
                wait=("dve",))
            p.op("gpsimd", lambda e, c=c: e.dma_start(
                out=xp_part[c][128:129, :], in_=dtr_t[:]),
                inc="gp", inck=16)
            ccn = new_cc()
            p.op("gpsimd", lambda e, c=c: e.collective_compute(
                "AllReduce", ALU.add, replica_groups=G2,
                ins=[xp_part[c]], outs=[xp_red[c]]), inc=ccn, wait=("gp",))
            p.op("sync", lambda e, c=c: e.dma_start(
                out=bct[:], in_=xp_red[c][0:128, :]), inc="ld", inck=16,
                wait=((ccn, 1), "dve"))
            p.op("sync", lambda e, c=c: e.dma_start(
                out=dtr_t[:], in_=xp_red[c][128:129, :]),
                inc="ld", inck=16, wait=("te",))
            p.op("vector", lambda e: e.tensor_copy(out=bc_h, in_=bct[:]),
                 inc="dve", wait=("ld", "gp"))

        def dt_u_build(c):
            for hf in range(2):
                p.op("tensor", lambda e, hf=hf: e.matmul(
                    out=ps_r, lhsT=ones_f[:],
                    rhs=dtr_t[:, hf * 512:(hf + 1) * 512], start=True,
                    stop=True), inc="te", wait=("dve", "ld"))
                for db in range(NDB):
                    p.op("vector", lambda e, db=db, hf=hf: e.tensor_scalar(
                        out=tmp_a[:, db, hf * 512:(hf + 1) * 512], in0=ps_r,
                        scalar1=dtw[:, db:db + 1], scalar2=dtb[:, db:db + 1],
                        op0=ALU.mult, op1=ALU.add), inc="dve",
                        wait=("te", "gp") if db == 0 else ("te",))
            p.op("vector", lambda e: e.tensor_scalar(
                out=dt_t[:], in0=tmp_a[:], scalar1=0.0, scalar2=None,
                op0=ALU.max), inc="dve", sw=False, wait=("gp",))
            p.op("scalar", lambda e: e.activation(
                out=dAv[:], in_=tmp_a[:], func=AF.Abs), inc="act",
                wait=("dve",))
            p.op("scalar", lambda e: e.activation(
                out=tmp_a[:], in_=dAv[:], func=AF.Exp, scale=cst_neg1[:]),
                inc="act")
            p.op("scalar", lambda e: e.activation(
                out=dAv[:], in_=tmp_a[:], func=AF.Ln, bias=cst_one[:]), inc="act")
            p.op("vector", lambda e: e.tensor_tensor(
                out=dt_t[:], in0=dt_t[:], in1=dAv[:], op=ALU.add),
                inc="dve", sw=False, wait=("act",))
            p.op("vector", lambda e: e.tensor_tensor(
                out=u_h, in0=big_a[:], in1=xssi, op=ALU.mult),
                inc="dve", sw=False)

        def scan(c):
            dabuf = [dAv, yacc]
            NB3 = 4
            gp_marks = {}
            dve_marks = {}

            def emit_bcast(n):
                nb = n % NB3
                p.op("gpsimd", lambda e, n=n, nb=nb: e.dma_start(
                    out=bb[nb],
                    in_=bc_h[n:n + 1, :].unsqueeze(1)
                    .broadcast_to([1, 128, Q])), inc="gp", inck=16,
                    wait=("dve",))
                p.op("gpsimd", lambda e, n=n, nb=nb: e.dma_start(
                    out=cbb[nb],
                    in_=bc_h[64 + n:65 + n, :].unsqueeze(1)
                    .broadcast_to([1, 128, Q])), inc="gp", inck=16)
                gp_marks[n] = p.cnt["gp"]
            emit_bcast(0)
            emit_bcast(1)
            emit_bcast(2)
            dve_marks[-2] = dve_marks[-1] = p.cnt["dve"]
            for n in range(DS):
                nb = n % NB3
                da = dabuf[n % 2]
                for db in range(NDB):
                    p.op("scalar", lambda e, db=db, n=n, da=da: e.activation(
                        out=da[:, db, :], in_=dt_t[:, db, :], func=AF.Exp,
                        scale=a_sl[:, db, n:n + 1]),
                        inc="act" if db == NDB - 1 else None,
                        wait=(("dve", dve_marks[n - 2]),) if db == 0 else ())
                if n + 3 < DS:
                    emit_bcast(n + 3)
                for db in range(NDB):
                    p.op("vector", lambda e, db=db, nb=nb: e.tensor_tensor(
                        out=inj3[:, db, :], in0=u3[:, db, :], in1=bb[nb],
                        op=ALU.mult), inc="dve", sw=False,
                        wait=(("act", p.cnt["act"]), ("gp", gp_marks[n]))
                        if db == 0 else ())
                for db in range(NDB):
                    p.op("vector", lambda e, db=db, n=n, da=da:
                         e.tensor_tensor_scan(
                             out=da[:, db, :], data0=da[:, db, :],
                             data1=inj3[:, db, :],
                             initial=carry[:, db, n:n + 1],
                             op0=ALU.mult, op1=ALU.add), inc="dve", sw=False)
                p.op("vector", lambda e, n=n, da=da: e.tensor_copy(
                    out=carry[:, :, n:n + 1], in_=da[:, :, Q - 1:Q]),
                    inc="dve")
                p.op("vector", lambda e, nb=nb, da=da: e.tensor_tensor(
                    out=tmp_a[:], in0=da[:],
                    in1=cbb[nb].unsqueeze(1).broadcast_to([128, NDB, Q]),
                    op=ALU.mult), inc="dve", sw=False, wait=("te",))
                for db in range(NDB):
                    for tb in range(8):
                        blk = (db * 8 + tb) * 128
                        p.op("tensor",
                             lambda e, db=db, tb=tb, blk=blk, n=n: e.matmul(
                                 out=ps_all[:, blk:blk + 128],
                                 lhsT=tmp_a[:, db, tb * 128:(tb + 1) * 128],
                                 rhs=ident_f[:], is_transpose=True,
                                 start=(n == 0), stop=(n == DS - 1)),
                             inc="te" if (db == NDB - 1 and tb == 7) else None,
                             wait=("dve",) if (db == 0 and tb == 0) else ())
                dve_marks[n] = p.cnt["dve"]
            p.op("vector", lambda e: e.tensor_copy(
                out=big_c[:], in_=ps_all[:]), inc="dve", wait=("te",))
            for db in range(NDB):
                for tb in range(8):
                    blk = (db * 8 + tb) * 128
                    p.op("tensor", lambda e, blk=blk: e.matmul(
                        out=ps_all[:, blk:blk + 128],
                        lhsT=big_c[:, blk:blk + 128],
                        rhs=ident_f[:], is_transpose=True,
                        start=True, stop=True),
                        inc="te" if blk == (NDB * 8 - 1) * 128 else None,
                        wait=("dve",) if blk == 0 else ())
            p.op("vector", lambda e: e.tensor_copy(
                out=big_d[:], in_=ps_all[:]), inc="dve", wait=("te",))
            for db in range(NDB):
                p.op("vector", lambda e, db=db: e.scalar_tensor_tensor(
                    out=yacc[:, db, :], in0=xssi3[:, db, :],
                    scalar=d_col[:, db:db + 1], in1=yacc[:, db, :],
                    op0=ALU.mult, op1=ALU.add), inc="dve")
            p.op("scalar", lambda e: e.activation(
                out=u_h, in_=z_h, func=AF.Silu), inc="act",
                wait=("dve",))
            p.op("vector", lambda e: e.tensor_tensor(
                out=z_h, in0=big_d[:], in1=u_h, op=ALU.mult),
                inc="dve", wait=("act",))

        def out_proj(c):
            po_lo = dt_t
            po_hi = tmp_a
            for mt in range(8):
                dst = po_lo if mt < 4 else po_hi
                for hf in range(2):
                    psd = ps_a if (mt + hf) % 2 == 0 else ps_b
                    for db in range(NDB):
                        p.op("tensor", lambda e, mt=mt, hf=hf, db=db, ps=psd:
                             e.matmul(out=ps[:],
                                      lhsT=wout_h[:, db, mt, :],
                                      rhs=z3[:, db, hf * 512:(hf + 1) * 512],
                                      start=(db == 0), stop=(db == NDB - 1)),
                             inc="te" if db == NDB - 1 else None,
                             wait=("dve",) if db == 0 else ())
                    p.op("vector", lambda e, mt=mt, hf=hf, dd=dst, ps=psd:
                         e.tensor_copy(
                             out=dd[:, mt % 4, hf * 512:(hf + 1) * 512],
                             in_=ps[:]), inc="dve",
                         wait=("te", "ld") if (DBG and mt == 0 and hf == 0)
                         else ("te",))
            for sq in range(4):
                p.op("gpsimd", lambda e, c=c, sq=sq: e.dma_start(
                    out=po_d[c][sq * 1024:sq * 1024 + 512, :]
                    .rearrange("(a p) q -> p a q", p=128),
                    in_=po_lo[:, :, sq * NQ4:(sq + 1) * NQ4]),
                    inc="gp", inck=16, wait=("dve",) if sq == 0 else ())
                p.op("gpsimd", lambda e, c=c, sq=sq: e.dma_start(
                    out=po_d[c][sq * 1024 + 512:(sq + 1) * 1024, :]
                    .rearrange("(a p) q -> p a q", p=128),
                    in_=po_hi[:, :, sq * NQ4:(sq + 1) * NQ4]),
                    inc="gp", inck=16)
            ccn = new_cc()
            p.op("gpsimd", lambda e, c=c: e.collective_compute(
                "ReduceScatter", ALU.add, replica_groups=G2,
                ins=[po_d[c]], outs=[attn_q[c]]), inc=ccn, wait=("gp",))
            return ccn

        def ffn_weight_quant():
            mats = [(wg_t, wq_g_d, 0), (wu_t, wq_u_d, 1), (wd_t, wq_d_d, 2)]
            for src, dst, mi in mats:
                srcf = src.rearrange("p a b c -> p (a b c)")
                dstf = dst.rearrange("p a b c -> p (a b c)")
                acc, tmpc = cols[0], cols[1]
                p.op("vector", lambda e, a=acc: e.memset(a[:], 0.0), inc="dve")
                for ch in range(8):
                    st32 = big_a[:, :4096]
                    p.op("sync", lambda e, s=srcf, ch=ch, dd=st32: e.dma_start(
                        out=dd, in_=s[:, ch * 4096:(ch + 1) * 4096]),
                        inc="ld", inck=16, wait=("dve", "gp", "te"))
                    p.op("vector", lambda e, dd=st32, t=tmpc: e.reduce_sum(
                        t[:], dd, axis=AX.X, apply_absolute_value=True),
                        inc="dve", wait=("ld",))
                    p.op("vector", lambda e, a=acc, t=tmpc: e.tensor_tensor(
                        out=a[:], in0=a[:], in1=t[:], op=ALU.add), inc="dve")
                p.op("tensor", lambda e, a=acc: e.matmul(
                    out=ps_r[0:1, 0:1], lhsT=onesc_f[:], rhs=a[:],
                    start=True, stop=True), inc="te", wait=("dve",))
                p.op("vector", lambda e: e.tensor_scalar(
                    out=rows4[:, 0:1], in0=ps_r[0:1, 0:1],
                    scalar1=1.0 / (4096.0 * 1024.0), scalar2=1e-5,
                    op0=ALU.mult, op1=ALU.max), inc="dve", wait=("te",))
                p.op("vector", lambda e: e.reciprocal(
                    rows4[:, 1:2], rows4[:, 0:1]), inc="dve")
                p.op("gpsimd", lambda e: e.dma_start(
                    out=swb_d[0:1, 0:2], in_=rows4[:, 0:2]),
                    inc="gp", inck=16, wait=("dve",))
                p.op("sync", lambda e, mi=mi: e.dma_start(
                    out=sw3[:, mi:mi + 1],
                    in_=swb_d[0:1, 0:1].broadcast_to([128, 1])),
                    inc="ld", inck=16, wait=("gp",))
                p.op("sync", lambda e: e.dma_start(
                    out=swq[:], in_=swb_d[0:1, 1:2].broadcast_to([128, 1])),
                    inc="ld", inck=16)
                for ch in range(8):
                    st32 = big_a[:, :4096]
                    qv = big_b[:, :4096]
                    g2v = big_c[:, :4096]
                    st16 = wstr16[:, :4096]
                    p.op("sync", lambda e, s=srcf, ch=ch, dd=st32: e.dma_start(
                        out=dd, in_=s[:, ch * 4096:(ch + 1) * 4096]),
                        inc="ld", inck=16, wait=("dve",))
                    p.op("vector", lambda e, dd=st32, q=qv: e.tensor_scalar_mul(
                        out=q, in0=dd, scalar1=swq[:]), inc="dve",
                        wait=("ld", "gp"))
                    p.op("vector", lambda e, q=qv, g=g2v: e.tensor_single_scalar(
                        out=g, in_=q, scalar=-0.5, op=ALU.is_lt), inc="dve")
                    p.op("vector", lambda e, q=qv: e.tensor_single_scalar(
                        out=q, in_=q, scalar=0.5, op=ALU.is_gt), inc="dve")
                    p.op("vector", lambda e, q=qv, g=g2v: e.tensor_tensor(
                        out=q, in0=q, in1=g, op=ALU.subtract), inc="dve")
                    p.op("vector", lambda e, q=qv, dd=st16: e.tensor_copy(
                        out=dd, in_=q), inc="dve")
                    p.op("gpsimd", lambda e, dd=dstf, ch=ch, s=st16:
                         e.dma_start(out=dd[:, ch * 4096:(ch + 1) * 4096],
                                     in_=s), inc="gp", inck=16, wait=("dve",))

        def ffn_quant1(rs_sems):
            ntile = 0
            for c in range(NCH):
                aq32 = big_b[:, :8 * NQ4].rearrange("p (k t) -> p k t", k=8)
                p.op("sync", lambda e, c=c, a=aq32: e.dma_start(
                    out=a, in_=attn_q[c].rearrange("(k p) t -> p k t", p=128)),
                    inc="ld", inck=16, wait=((rs_sems[c], 1), "dve", "gp"))
                for tt in range(NQ4 // 128):
                    r0 = c * NQ4 + tt * 128
                    p.op("sync", lambda e, r0=r0: e.dma_start(
                        out=xt[:], in_=x_own[r0:r0 + 128, :]),
                        inc="ld", inck=16, wait=("dve", "te"))
                    x2t = big_d[:, :D]
                    for kc in range(8):
                        p.op("tensor", lambda e, kc=kc, tt=tt, a=aq32:
                             e.transpose(
                                 out=ps_t,
                                 in_=a[:, kc, tt * 128:(tt + 1) * 128],
                                 identity=ident_f[:]),
                             inc="te", wait=("ld", "dve"))
                        p.op("vector", lambda e, kc=kc: e.tensor_copy(
                            out=big_c[:, kc * 128:(kc + 1) * 128],
                            in_=ps_t), inc="dve", wait=("te",))
                    p.op("vector", lambda e: e.tensor_tensor(
                        out=x2t, in0=big_c[:, :D], in1=reps["ga"][:],
                        op=ALU.mult), inc="dve", wait=("ld",))
                    p.op("vector", lambda e: e.tensor_tensor(
                        out=x2t, in0=x2t, in1=xt[:], op=ALU.add), inc="dve")
                    p.op("gpsimd", lambda e, r0=r0: e.dma_start(
                        out=x2_d[r0:r0 + 128, :], in_=x2t),
                        inc="gp", inck=16, wait=("dve",))
                    c0, c1, c2, c3, c4, c5 = cols
                    h2 = big_c[:, :D]
                    p.op("vector", lambda e: e.reduce_sum(
                        c0[:], x2t, axis=AX.X), inc="dve")
                    p.op("vector", lambda e: e.tensor_scalar_mul(
                        out=c1[:], in0=c0[:], scalar1=1.0 / D), inc="dve")
                    p.op("vector", lambda e: e.tensor_tensor(
                        out=h2, in0=x2t, in1=x2t, op=ALU.mult), inc="dve")
                    p.op("vector", lambda e: e.reduce_sum(
                        c2[:], h2, axis=AX.X), inc="dve")
                    p.op("vector", lambda e: e.tensor_scalar_mul(
                        out=c3[:], in0=c2[:], scalar1=1.0 / D), inc="dve")
                    p.op("vector", lambda e: e.tensor_tensor(
                        out=c4[:], in0=c1[:], in1=c1[:], op=ALU.mult),
                        inc="dve")
                    p.op("vector", lambda e: e.tensor_tensor(
                        out=c5[:], in0=c3[:], in1=c4[:], op=ALU.subtract),
                        inc="dve")
                    p.op("scalar", lambda e: e.activation(
                        out=c0[:], in_=c5[:], func=AF.Sqrt, bias=cst_eps[:]),
                        inc="act", wait=("dve",))
                    p.op("vector", lambda e: e.reciprocal(c2[:], c0[:]),
                         inc="dve", wait=("act",))
                    p.op("vector", lambda e: e.tensor_scalar(
                        out=h2, in0=x2t, scalar1=c1[:], scalar2=c2[:],
                        op0=ALU.subtract, op1=ALU.mult), inc="dve")
                    p.op("vector", lambda e: e.tensor_tensor(
                        out=h2, in0=h2, in1=reps["cb1"][:], op=ALU.mult),
                        inc="dve")
                    p.op("vector", lambda e: e.tensor_tensor(
                        out=h2, in0=h2, in1=reps["sbm"][:], op=ALU.add),
                        inc="dve")
                    mx, sc, iv = cols[3], cols[4], cols[5]
                    p.op("vector", lambda e: e.reduce_max(
                        mx[:], h2, axis=AX.X, apply_absolute_value=True),
                        inc="dve")
                    p.op("vector", lambda e: e.tensor_single_scalar(
                        out=mx[:], in_=mx[:], scalar=1e-5, op=ALU.max),
                        inc="dve")
                    p.op("vector", lambda e: e.reciprocal(sc[:], mx[:]),
                         inc="dve")
                    p.op("vector", lambda e: e.tensor_scalar_mul(
                        out=sc[:], in0=sc[:], scalar1=127.0), inc="dve")
                    p.op("vector", lambda e: e.tensor_scalar_mul(
                        out=iv[:], in0=mx[:], scalar1=1.0 / 127.0), inc="dve")
                    p.op("vector", lambda e, nt=ntile: e.tensor_copy(
                        out=big_a[:, nt:nt + 1], in_=iv[:]), inc="dve")
                    q = big_d[:, D:2 * D]
                    p.op("vector", lambda e: e.tensor_scalar_mul(
                        out=q, in0=h2, scalar1=sc[:]), inc="dve")
                    p.op("vector", lambda e: e.tensor_scalar(
                        out=q, in0=q, scalar1=MAGIC, scalar2=MAGIC,
                        op0=ALU.add, op1=ALU.subtract), inc="dve")
                    p.op("vector", lambda e: e.tensor_scalar(
                        out=q, in0=q, scalar1=127.0, scalar2=-128.0,
                        op0=ALU.min, op1=ALU.max), inc="dve")
                    p.op("vector", lambda e: e.tensor_copy(
                        out=ht_h[:], in_=q), inc="dve")
                    p.op("gpsimd", lambda e, r0=r0: e.dma_start(
                        out=q1l_in[r0:r0 + 128, :], in_=ht_h[:]),
                        inc="gp", inck=16, wait=("dve",))
                    ntile += 1
            ccn = new_cc()
            p.op("gpsimd", lambda e: e.collective_compute(
                "AllGather", ALU.bypass, replica_groups=G1,
                ins=[q1l_in], outs=[q1l_out]), inc=ccn, wait=("gp",))
            return ccn

        def ffn_main(q1cc):
            invcols = big_a
            q2ccs = []
            for blk in range(NB4):
                xqT = hT3
                for tt in range(4):
                    r0 = blk * 512 + tt * 128
                    p.op("sync", lambda e, r0=r0: e.dma_start(
                        out=ht_h[:], in_=q1l_out[r0:r0 + 128, :]),
                        inc="ld", inck=16, wait=((q1cc, 1), "dve", "te"))
                    p.op("vector", lambda e: e.tensor_copy(
                        out=xt[:], in_=ht_h[:]), inc="dve", wait=("ld",))
                    for kc in range(8):
                        p.op("tensor", lambda e, kc=kc: e.transpose(
                            out=ps_t[:, 0:128],
                            in_=xt[:, kc * 128:(kc + 1) * 128],
                            identity=ident_f[:]), inc="te", wait=("dve",))
                        p.op("vector", lambda e, kc=kc, tt=tt: e.tensor_copy(
                            out=xqT[:, kc, tt * 128:(tt + 1) * 128],
                            in_=ps_t[:, 0:128]), inc="dve", wait=("te",))
                er = xt[0:1, 0:512]
                for tt in range(4):
                    cidx = blk * 4 + tt
                    p.op("tensor", lambda e, cidx=cidx: e.transpose(
                        out=ps_t[0:1, 0:128], in_=invcols[:, cidx:cidx + 1],
                        identity=ident_f[:]), inc="te", wait=("dve",))
                    p.op("vector", lambda e, tt=tt: e.tensor_copy(
                        out=er[:, tt * 128:(tt + 1) * 128],
                        in_=ps_t[0:1, 0:128]), inc="dve", wait=("te",))
                epi_g = big_c[:, 0:512]
                epi_u = big_c[:, 512:1024]
                p.op("tensor", lambda e: e.matmul(
                    out=ps_r, lhsT=ones_f[:], rhs=er, start=True,
                    stop=True), inc="te", wait=("dve",))
                p.op("vector", lambda e: e.tensor_scalar_mul(
                    out=epi_g, in0=ps_r, scalar1=sw3[:, 0:1]),
                    inc="dve", wait=("te",))
                p.op("vector", lambda e: e.tensor_scalar_mul(
                    out=epi_u, in0=ps_r, scalar1=sw3[:, 1:2]), inc="dve")
                for mt in range(32):
                    p.op("sync", lambda e, mt=mt: e.dma_start(
                        out=wst3[:, 0:8, :], in_=wq_g_d[:, mt, :, :]),
                        inc="ld", inck=16, wait=("te", "dve"))
                    for kcc in range(8):
                        p.op("tensor", lambda e, kcc=kcc: e.matmul(
                            out=ps_a, lhsT=wst3[:, kcc, :],
                            rhs=xqT[:, kcc, 0:512], start=(kcc == 0),
                            stop=(kcc == 7)),
                            inc="te" if kcc == 7 else None,
                            wait=("ld", "dve") if kcc == 0 else ())
                    p.op("sync", lambda e, mt=mt: e.dma_start(
                        out=wst3[:, 8:16, :], in_=wq_u_d[:, mt, :, :]),
                        inc="ld", inck=16)
                    for kcc in range(8):
                        p.op("tensor", lambda e, kcc=kcc: e.matmul(
                            out=ps_b, lhsT=wst3[:, 8 + kcc, :],
                            rhs=xqT[:, kcc, 0:512], start=(kcc == 0),
                            stop=(kcc == 7)),
                            inc="te" if kcc == 7 else None,
                            wait=("ld", "dve") if kcc == 0 else ())
                    p.op("vector", lambda e: e.tensor_tensor(
                        out=gtl, in0=ps_a, in1=epi_g, op=ALU.mult),
                        inc="dve", wait=("te",))
                    p.op("scalar", lambda e: e.activation(
                        out=gtl2, in_=gtl, func=AF.Silu), inc="act",
                        wait=("dve",))
                    p.op("vector", lambda e: e.tensor_tensor(
                        out=utl, in0=ps_b, in1=epi_u, op=ALU.mult),
                        inc="dve")
                    p.op("vector", lambda e, mt=mt: e.tensor_tensor(
                        out=gu3[:, mt, :], in0=gtl2, in1=utl,
                        op=ALU.mult), inc="dve", wait=("act",))
                p.op("vector", lambda e: e.tensor_copy(
                    out=gtl, in_=gu3[:, 0, :]), inc="dve", wait=("te",))
                p.op("vector", lambda e: e.tensor_copy(
                    out=gtl2, in_=gu3[:, 0, :]), inc="dve")
                for mt in range(1, 32):
                    p.op("vector", lambda e, mt=mt: e.tensor_tensor(
                        out=gtl, in0=gtl, in1=gu3[:, mt, :],
                        op=ALU.max), inc="dve")
                    p.op("vector", lambda e, mt=mt: e.tensor_tensor(
                        out=gtl2, in0=gtl2, in1=gu3[:, mt, :],
                        op=ALU.min), inc="dve")
                p.op("vector", lambda e: e.tensor_scalar_mul(
                    out=gtl2, in0=gtl2, scalar1=-1.0), inc="dve")
                p.op("gpsimd", lambda e: e.tensor_reduce(
                    out=big_b[0:1, 0:512], in_=gtl,
                    axis=AX.C, op=ALU.max), inc="gpc", wait=("dve",))
                p.op("gpsimd", lambda e: e.tensor_reduce(
                    out=big_b[0:1, 512:1024], in_=gtl2,
                    axis=AX.C, op=ALU.max), inc="gpc")
                p.op("vector", lambda e: e.tensor_tensor(
                    out=xt[0:1, 0:512], in0=big_b[0:1, 0:512],
                    in1=big_b[0:1, 512:1024], op=ALU.max),
                    inc="dve", wait=("gpc", "te"))
                p.op("vector", lambda e: e.tensor_single_scalar(
                    out=xt[0:1, 0:512], in_=xt[0:1, 0:512], scalar=1e-5,
                    op=ALU.max), inc="dve")
                p.op("vector", lambda e: e.tensor_scalar_mul(
                    out=xt[0:1, 512:1024], in0=xt[0:1, 0:512],
                    scalar1=1.0 / 127.0), inc="dve")
                p.op("vector", lambda e: e.reciprocal(
                    xt[0:1, 0:512], xt[0:1, 0:512]), inc="dve")
                p.op("vector", lambda e: e.tensor_scalar_mul(
                    out=xt[0:1, 0:512], in0=xt[0:1, 0:512], scalar1=127.0),
                    inc="dve")
                p.op("tensor", lambda e: e.matmul(
                    out=ps_r, lhsT=ones_f[:], rhs=xt[0:1, 0:512],
                    start=True, stop=True), inc="te", wait=("dve",))
                s2rep = big_c[:, 1024:1536]
                p.op("vector", lambda e: e.tensor_copy(
                    out=s2rep, in_=ps_r), inc="dve", wait=("te",))
                for mt in range(32):
                    qq = big_b[:, 0:512]
                    p.op("vector", lambda e, mt=mt: e.tensor_tensor(
                        out=qq, in0=gu3[:, mt, :], in1=s2rep, op=ALU.mult),
                        inc="dve")
                    p.op("vector", lambda e: e.tensor_scalar(
                        out=qq, in0=qq, scalar1=MAGIC, scalar2=MAGIC,
                        op0=ALU.add, op1=ALU.subtract), inc="dve")
                    p.op("vector", lambda e: e.tensor_scalar(
                        out=qq, in0=qq, scalar1=127.0, scalar2=-128.0,
                        op0=ALU.min, op1=ALU.max), inc="dve")
                    p.op("vector", lambda e, mt=mt: e.tensor_copy(
                        out=gu3[:, mt, :], in_=qq), inc="dve")
                p.op("gpsimd", lambda e, blk=blk: e.dma_start(
                    out=q2l_in[blk].rearrange("(k p) t -> p k t", p=128),
                    in_=gu3[:]), inc="gp", inck=16, wait=("dve",))
                ccn = new_cc()
                p.op("gpsimd", lambda e, blk=blk: e.collective_compute(
                    "AllGather", ALU.bypass, replica_groups=G1,
                    ins=[q2l_in[blk]], outs=[q2l_out[blk]]), inc=ccn,
                    wait=("gp",))
                p.op("sync", lambda e, blk=blk: e.dma_start(
                    out=gu3[:],
                    in_=q2l_out[blk].rearrange("(k p) t -> p k t", p=128)),
                    inc="ld", inck=16, wait=((ccn, 1), "dve", "gp", "te"))
                p.op("tensor", lambda e: e.matmul(
                    out=ps_r, lhsT=ones_f[:], rhs=xt[0:1, 512:1024],
                    start=True, stop=True), inc="te", wait=("dve", "ld"))
                epi_d = big_c[:, 0:512]
                p.op("vector", lambda e: e.tensor_scalar_mul(
                    out=epi_d, in0=ps_r, scalar1=sw3[:, 2:3]),
                    inc="dve", wait=("te",))
                for mt in range(8):
                    p.op("sync", lambda e, mt=mt: e.dma_start(
                        out=wst3[:, 0:32, :], in_=wq_d_d[:, mt, :, :]),
                        inc="ld", inck=16, wait=("te", "dve"))
                    for kcc in range(32):
                        p.op("tensor", lambda e, kcc=kcc: e.matmul(
                            out=ps_a, lhsT=wst3[:, kcc, :],
                            rhs=gu3[:, kcc, :], start=(kcc == 0),
                            stop=(kcc == 31)),
                            inc="te" if kcc == 31 else None,
                            wait=("ld", "dve") if kcc == 0 else ())
                    p.op("vector", lambda e, mt=mt: e.tensor_tensor(
                        out=ffn8[:, mt, :], in0=ps_a, in1=epi_d,
                        op=ALU.mult), inc="dve", wait=("te",))
                for tt in range(4):
                    r0 = blk * 512 + tt * 128
                    p.op("sync", lambda e, r0=r0: e.dma_start(
                        out=xt[:], in_=x2_d[r0:r0 + 128, :]),
                        inc="ld", inck=16, wait=("dve", "te", "gp"))
                    fft = big_c[:, 0:D]
                    for kc in range(8):
                        p.op("tensor", lambda e, kc=kc, tt=tt: e.transpose(
                            out=ps_t,
                            in_=ffn8[:, kc, tt * 128:(tt + 1) * 128],
                            identity=ident_f[:]), inc="te", wait=("dve",))
                        p.op("vector", lambda e, kc=kc: e.tensor_copy(
                            out=fft[:, kc * 128:(kc + 1) * 128], in_=ps_t),
                            inc="dve", wait=("te",))
                    x3 = big_c[:, D:2 * D]
                    p.op("vector", lambda e: e.tensor_tensor(
                        out=x3, in0=fft, in1=reps["gb"][:], op=ALU.mult),
                        inc="dve", wait=("ld",))
                    p.op("vector", lambda e: e.tensor_tensor(
                        out=x3, in0=x3, in1=xt[:], op=ALU.add), inc="dve")
                    p.op("gpsimd", lambda e, r0=r0: e.dma_start(
                        out=y_out[r0:r0 + 128, :], in_=x3),
                        inc="gp", inck=16, wait=("dve",))

        def dump(name, src):
            p.op("sync", lambda e, nm=name, s2=src: e.dma_start(
                out=dbg[nm], in_=s2),
                inc="ld", inck=16, wait=("dve", "act", "te", "gp"))

        setup()
        rs_sems = []
        ln1(0)
        for c in range(NCH):
            in_proj_conv(c)
            x_proj_reduce(c)
            if c + 1 < NCH:
                ln1(c + 1)
            dt_u_build(c)
            scan(c)
            if DBG and c == 0:
                dump("dbg_xt", xt[:])
                for ci in range(6):
                    p.op("sync", lambda e, ci=ci: e.dma_start(
                        out=dbg["dbg_cols"][:, ci:ci + 1], in_=cols[ci][:],
                        allow_slow_non_contiguous=True),
                        inc="ld", inck=16, wait=("dve", "act", "te", "gp"))
                dump("dbg_hth", ht_h[:])
                dump("dbg_hth2", ht_h2[:])
                dump("dbg_hT", hT[:])
                dump("dbg_xsr", xsr[:].rearrange("p a q -> p (a q)"))
                dump("dbg_xssi", xssi)
                dump("dbg_bct", bct[:])
                dump("dbg_dtr", xt[0:1, :Q])
                dump("dbg_dt", big_a[:])
                dump("dbg_u", u_h)
                dump("dbg_y", big_d[:])
                dump("dbg_yz", z_h)
                dump("dbg_carry", carry[:].rearrange("p a q -> p (a q)"))
                dump("dbg_ca1", reps["ca1"][:])
                dump("dbg_sa", reps["sa"][:])
                dump("dbg_ga", reps["ga"][:])
                dump("dbg_z", u_h)
                dump("dbg_asl", a_sl[:].rearrange("p a q -> p (a q)"))
            rs_sems.append(out_proj(c))
            if DBG and c == 0:
                p.op("sync", lambda e: e.dma_start(
                    out=dbg["dbg_aq"], in_=attn_q[0]),
                    inc="ld", inck=16, wait=((rs_sems[0], 1),))
        ffn_weight_quant()
        q1cc = ffn_quant1(rs_sems)
        ffn_main(q1cc)
        p.op("gpsimd", lambda e: e.engine_nop(), wait=("gp", "ld"))

        sem_names = ["ld", "gp", "te", "dve", "act", "gpc"] + cc_names
        with ExitStack() as sctx:
            H = {sn: sctx.enter_context(nc.semaphore(sn)) for sn in sem_names}
            with nc.Block() as block:
                @block.sync
                def _(e):
                    for fn in p.seq["sync"]:
                        fn(e, H)

                @block.scalar
                def _(e):
                    for fn in p.seq["scalar"]:
                        fn(e, H)

                @block.vector
                def _(e):
                    for fn in p.seq["vector"]:
                        fn(e, H)

                @block.tensor
                def _(e):
                    for fn in p.seq["tensor"]:
                        fn(e, H)

                @block.gpsimd
                def _(e):
                    for fn in p.seq["gpsimd"]:
                        fn(e, H)
    return nc


def make_inputs(core, L, Q, ins):
    import ml_dtypes
    g, j = core // 4, core % 4
    NCH = L // Q
    NQ4 = Q // 4
    sl = slice(j * DSL, (j + 1) * DSL)
    x = np.ascontiguousarray(np.asarray(ins["L0_attn_x"])[g])
    x_own = np.concatenate(
        [x[c * Q + j * NQ4: c * Q + (j + 1) * NQ4] for c in range(NCH)],
        axis=0)
    W = np.asarray(ins["L0_attn_in_proj_w"])
    d = {
        "x_b": x,
        "x_own": x_own,
        "w_in_xs": W.reshape(8, 128, 2 * DI)[:, :, sl].transpose(1, 0, 2),
        "w_in_z": W.reshape(8, 128, 2 * DI)
        [:, :, DI + j * DSL: DI + (j + 1) * DSL].transpose(1, 0, 2),
        "w_xp": np.asarray(ins["L0_attn_x_proj_w"])[sl]
        .reshape(NDB, 128, 129).transpose(1, 0, 2),
        "conv_w4": np.asarray(ins["L0_attn_conv_w"])[sl, 0, :]
        .reshape(NDB, 128, KC).transpose(1, 0, 2),
        "conv_b4": np.asarray(ins["L0_attn_conv_b"])[sl].reshape(NDB, 128).T,
        "dtw4": np.asarray(ins["L0_attn_dt_w"])[0, sl].reshape(NDB, 128).T,
        "dtb4": np.asarray(ins["L0_attn_dt_b"])[sl].reshape(NDB, 128).T,
        "alog4": np.asarray(ins["L0_attn_A_log"])[sl]
        .reshape(NDB, 128, DS).transpose(1, 0, 2),
        "d4": np.asarray(ins["L0_attn_D"])[sl].reshape(NDB, 128).T,
        "w_out_sl": np.asarray(ins["L0_attn_out_proj_w"])[sl]
        .reshape(NDB, 128, 8, 128).transpose(1, 0, 2, 3),
        "wg_t": np.asarray(ins["L0_ffn_w_gate"]).T
        .reshape(8, 128, 32, 128).transpose(1, 2, 0, 3),
        "wu_t": np.asarray(ins["L0_ffn_w_up"]).T
        .reshape(8, 128, 32, 128).transpose(1, 2, 0, 3),
        "wd_t": np.asarray(ins["L0_ffn_w_down"]).T
        .reshape(32, 128, 8, 128).transpose(1, 2, 0, 3),
        "c_row": np.asarray(ins["L0_c"])[g:g + 1],
        "wmod_r": np.asarray(ins["L0_wmod"]).reshape(8, 128, 6 * D)
        .transpose(1, 0, 2),
        "ident_h": np.eye(128, dtype=ml_dtypes.bfloat16),
    }
    return {k: np.ascontiguousarray(np.asarray(v, dtype=np.float32))
            if k != "ident_h" else v for k, v in d.items()}


def assemble(L, Q, results):
    NCH = L // Q
    NQ4 = Q // 4
    y = np.empty((2, L, D), np.float32)
    for core in range(8):
        g, j = core // 4, core % 4
        yo = results[core]["y_out"]
        for c in range(NCH):
            y[g, c * Q + j * NQ4: c * Q + (j + 1) * NQ4] = \
                yo[c * NQ4:(c + 1) * NQ4]
    return y


def kernel(**inputs):
    L, Q = 8192, 1024
    nc = build_nc(L, Q)
    in_maps = [make_inputs(c, L, Q, inputs) for c in range(8)]
    res = run_bass_kernel_spmd(nc, in_maps, core_ids=list(range(8)))
    return assemble(L, Q, res.results)


# version 05
# speedup vs baseline: 1.3530x; 1.0398x over previous
import numpy as np
from contextlib import ExitStack

import concourse.bass as bass
import concourse.mybir as mybir
from concourse.bass_utils import run_bass_kernel_spmd

F32 = mybir.dt.float32
BF16 = mybir.dt.bfloat16
AF = mybir.ActivationFunctionType
ALU = mybir.AluOpType
AX = mybir.AxisListType

MAGIC = float(np.float32(1.5 * 2.0 ** 23))
LN_EPS = 1e-5
D = 1024
DI = 2048
DSL = 512
NDB = 4
DS = 64
KC = 4
G2 = [[0, 1, 2, 3], [4, 5, 6, 7]]
G1 = [[c] for c in range(8)]

ENGS = ("sync", "scalar", "vector", "tensor", "gpsimd")


class Plan:
    def __init__(self):
        self.seq = {e: [] for e in ENGS}
        self.cnt = {}

    def sem(self, name):
        self.cnt.setdefault(name, 0)

    SELF_SEM = {"vector": "dve", "scalar": "act"}

    def op(self, eng, build, inc=None, inck=1, wait=(), sw=True):
        waits = []
        if sw and eng in self.SELF_SEM:
            sn = self.SELF_SEM[eng]
            waits.append((sn, self.cnt[sn]))
        for w in wait:
            if isinstance(w, tuple):
                waits.append(w)
            else:
                waits.append((w, self.cnt[w]))
        incs = None
        if inc is not None:
            self.cnt[inc] += inck
            incs = (inc, inck)

        def fn(e, H, waits=tuple(waits), build=build, incs=incs):
            for s, v in waits:
                if v > 0:
                    e.wait_ge(H[s], v)
            ins = build(e)
            if incs is not None:
                ins.then_inc(H[incs[0]], incs[1])
        self.seq[eng].append(fn)


DBG = False


def build_nc(L, Q):
    NCH = L // Q
    QT = Q // 128
    NQ4 = Q // 4
    OWN = NCH * NQ4
    NB4 = OWN // 512
    assert L % Q == 0 and Q == 1024 and NQ4 % 128 == 0 and OWN % 512 == 0

    nc = bass.Bass("TRN2", target_bir_lowering=False, debug=False,
                   detect_race_conditions=False)

    def din(name, shape, dt=F32):
        return nc.dram_tensor(name, shape, dt, kind="ExternalInput").ap()

    x_b = din("x_b", [L, D])
    x_own = din("x_own", [OWN, D])
    w_in_xs = din("w_in_xs", [128, 8, DSL])
    w_in_z = din("w_in_z", [128, 8, DSL])
    w_xp = din("w_xp", [128, NDB, 129])
    conv_w4 = din("conv_w4", [128, NDB, KC])
    conv_b4 = din("conv_b4", [128, NDB])
    dtw4 = din("dtw4", [128, NDB])
    dtb4 = din("dtb4", [128, NDB])
    alog4 = din("alog4", [128, NDB, DS])
    d4 = din("d4", [128, NDB])
    w_out_sl = din("w_out_sl", [128, NDB, 8, 128])
    wg_t = din("wg_t", [128, 32, 8, 128])
    wu_t = din("wu_t", [128, 32, 8, 128])
    wd_t = din("wd_t", [128, 8, 32, 128])
    c_row = din("c_row", [1, D])
    wmod_r = din("wmod_r", [128, 8, 6 * D])
    ident_h = din("ident_h", [128, 128], BF16)

    y_out = nc.dram_tensor("y_out", [OWN, D], F32, kind="ExternalOutput").ap()
    dbg = {}
    if DBG:
        for nm, shape, dt in [
                ("dbg_hT", [128, 8 * Q], BF16), ("dbg_xsr", [128, NDB * (KC + Q)], BF16),
                ("dbg_xssi", [128, NDB * Q], BF16), ("dbg_bct", [128, Q], F32),
                ("dbg_dtr", [1, Q], F32), ("dbg_dt", [128, NDB * Q], F32),
                ("dbg_u", [128, NDB * Q], BF16), ("dbg_y", [128, NDB * Q], F32),
                ("dbg_yz", [128, NDB * Q], BF16), ("dbg_carry", [128, NDB * DS], F32),
                ("dbg_ca1", [128, D], BF16), ("dbg_sa", [128, D], BF16),
                ("dbg_ga", [128, D], BF16), ("dbg_aq", [1024, NQ4], F32),
                ("dbg_z", [128, NDB * Q], BF16), ("dbg_asl", [128, NDB * DS], F32),
                ("dbg_xt", [128, D], F32), ("dbg_cols", [128, 6], F32),
                ("dbg_hth", [128, D], BF16), ("dbg_hth2", [128, D], BF16)]:
            dbg[nm] = nc.dram_tensor(nm, shape, dt, kind="ExternalOutput").ap()

    dscr = lambda n, s, dt=F32: nc.dram_tensor(n, s, dt).ap()
    xp_part = [dscr(f"xp_part{c}", [129, Q]) for c in range(NCH)]
    xp_red = [dscr(f"xp_red{c}", [129, Q]) for c in range(NCH)]
    po_d = [dscr(f"po_d{c}", [4 * 1024, NQ4]) for c in range(NCH)]
    attn_q = [dscr(f"attn_q{c}", [1024, NQ4]) for c in range(NCH)]
    wq_g_d = dscr("wq_g_d", [128, 32, 8, 128], BF16)
    wq_u_d = dscr("wq_u_d", [128, 32, 8, 128], BF16)
    wq_d_d = dscr("wq_d_d", [128, 8, 32, 128], BF16)
    x2_d = dscr("x2_d", [OWN, D])
    q1l_in = dscr("q1l_in", [OWN, D], BF16)
    q1l_out = dscr("q1l_out", [OWN, D], BF16)
    q2l_in = [dscr(f"q2l_in{b}", [4096, 512], BF16) for b in range(NB4)]
    q2l_out = [dscr(f"q2l_out{b}", [4096, 512], BF16) for b in range(NB4)]
    swb_d = dscr("swb_d", [1, 4])

    p = Plan()
    for s in ("ld", "gp", "te", "dve", "act", "gpc"):
        p.sem(s)
    cc_names = []

    def new_cc():
        n = f"cc{len(cc_names)}"
        cc_names.append(n)
        p.sem(n)
        return n

    with ExitStack() as ctx:
        sb = lambda n, s, dt=F32: ctx.enter_context(nc.sbuf_tensor(n, s, dt))
        psum = lambda n, s, dt=F32: ctx.enter_context(nc.psum_tensor(n, s, dt))

        wxs_h = sb("wxs_h", [128, 8, DSL], BF16)
        wz_h = sb("wz_h", [128, 8, DSL], BF16)
        wxp_h = sb("wxp_h", [128, NDB, 129], BF16)
        wout_h = sb("wout_h", [128, NDB, 8, 128], BF16)
        convw = sb("convw", [128, NDB, KC])
        convb = sb("convb", [128, NDB])
        dtw = sb("dtw", [128, NDB])
        dtb = sb("dtb", [128, NDB])
        a_sl = sb("a_sl", [128, NDB, DS])
        d_col = sb("d_col", [128, NDB])
        ident = sb("ident", [128, 128], BF16)
        ident_f = sb("ident_f", [128, 128])
        ones_f = sb("ones_f", [1, 128])
        onesc_f = sb("onesc_f", [128, 1])
        reps = {}
        for nmm in ("ca1", "sa", "ga", "cb1", "sbm", "gb"):
            reps[nmm] = sb("rep_" + nmm, [128, D], BF16)
        carry = sb("carry", [128, NDB, DS])
        halo = sb("halo", [128, NDB, KC - 1], BF16)
        silc_f = sb("silc_f", [128, 8])
        sw3 = sb("sw3", [128, 4])
        swq = sb("swq", [128, 1])
        cols = [sb(f"col{i}", [128, 1]) for i in range(6)]
        rows4 = sb("rows4", [1, 16])
        cst_eps = sb("cst_eps", [128, 1])
        cst_neg1 = sb("cst_neg1", [128, 1])
        cst_one = sb("cst_one", [128, 1])

        big_a = sb("big_a", [128, NDB * Q])
        big_b = sb("big_b", [128, NDB * Q])
        big_c = sb("big_c", [128, NDB * Q])
        big_d = sb("big_d", [128, NDB * Q])
        xt = sb("xt", [128, D])
        bct = sb("bct", [128, Q])
        dtr_t = sb("dtr_t", [1, Q])
        hT = sb("hT", [128, 8 * Q], BF16)
        NXS = NDB * (KC + Q)
        bfB = sb("bfB", [128, NXS + 4 * NDB * Q + 9 * Q], BF16)
        o = 0
        def carve(n):
            nonlocal o
            v = bfB[:, o:o + n]
            o += n
            return v
        xsr = carve(NXS).rearrange("p (a q) -> p a q", a=NDB)
        xssi = carve(NDB * Q)
        z_h = carve(NDB * Q)
        u_h = carve(NDB * Q)
        inj = carve(NDB * Q)
        bb = [carve(Q) for i in range(4)]
        cbb = [carve(Q) for i in range(4)]
        bc_h = carve(Q)
        ht_h = sb("ht_h", [128, D], BF16)
        ht_h2 = sb("ht_h2", [128, D], BF16)
        gu16 = bfB[:, 0:32 * 512]
        wstr16 = bfB[:, 32 * 512:32 * 512 + 32 * 128]
        gtl = bfB[:, 20480:20992]
        gtl2 = bfB[:, 20992:21504]
        utl = bfB[:, 21504:22016]
        rmx = bfB[:, 22016:22528]
        rmn = bfB[:, 22528:23040]
        ps_all = psum("ps_all", [128, 4096])
        ps_a = ps_all[:, 0:512]
        ps_b = ps_all[:, 512:1024]
        ps_r = ps_all[:, 1024:1536]
        ps_t = ps_all[:, 1536:1664]

        dt_t = big_a[:].rearrange("p (a q) -> p a q", a=NDB)
        tmp_a = big_b[:].rearrange("p (a q) -> p a q", a=NDB)
        dAv = big_c[:].rearrange("p (a q) -> p a q", a=NDB)
        yacc = big_d[:].rearrange("p (a q) -> p a q", a=NDB)
        xssi3 = xssi.rearrange("p (a q) -> p a q", a=NDB)
        z3 = z_h.rearrange("p (a q) -> p a q", a=NDB)
        u3 = u_h.rearrange("p (a q) -> p a q", a=NDB)
        inj3 = inj.rearrange("p (a q) -> p a q", a=NDB)
        hT3 = hT[:].rearrange("p (k q) -> p k q", k=8)
        gu3 = gu16.rearrange("p (m t) -> p m t", m=32)
        wst3 = wstr16.rearrange("p (k m) -> p k m", m=128)
        ffn8 = big_d[:].rearrange("p (m t) -> p m t", m=8)

        def bcast3(t, n):
            return t.unsqueeze(1).broadcast_to([128, n, t.shape[-1]])

        def setup():
            p.op("vector", lambda e: e.memset(ones_f[:], 1.0), inc="dve")
            p.op("vector", lambda e: e.memset(cst_eps[:], LN_EPS), inc="dve")
            p.op("vector", lambda e: e.memset(cst_neg1[:], -1.0), inc="dve")
            p.op("vector", lambda e: e.memset(cst_one[:], 1.0), inc="dve")
            p.op("vector", lambda e: e.memset(onesc_f[:], 1.0), inc="dve")
            p.op("vector", lambda e: e.memset(carry[:], 0.0), inc="dve")
            p.op("vector", lambda e: e.memset(xsr[:, :, 0:KC], 0.0), inc="dve")
            p.op("vector", lambda e: e.memset(halo[:], 0.0), inc="dve")
            for dst, src in [(convw[:], conv_w4), (convb[:], conv_b4),
                             (dtw[:], dtw4), (dtb[:], dtb4), (d_col[:], d4),
                             (ident[:], ident_h)]:
                p.op("sync", lambda e, d=dst, s=src: e.dma_start(out=d, in_=s),
                     inc="ld", inck=16)

            def stage_cast(dst_h, src, nelem):
                st32 = big_a[:, :nelem]
                p.op("sync", lambda e, s=src, dd=st32: e.dma_start(out=dd, in_=s),
                     inc="ld", inck=16, wait=("dve",))
                p.op("vector", lambda e, dd=dst_h, s=st32: e.tensor_copy(
                    out=dd, in_=s), inc="dve", wait=("ld",))
            stage_cast(wxs_h[:].rearrange("p a b -> p (a b)"), w_in_xs, 8 * DSL)
            stage_cast(wz_h[:].rearrange("p a b -> p (a b)"), w_in_z, 8 * DSL)
            stage_cast(wxp_h[:].rearrange("p a b -> p (a b)"), w_xp, NDB * 129)
            stage_cast(wout_h[:].rearrange("p a b c -> p (a b c)"), w_out_sl,
                       NDB * 8 * 128)
            p.op("vector", lambda e: e.tensor_copy(out=ident_f[:],
                                                   in_=ident[:]),
                 inc="dve", wait=("ld",))
            asl_f = a_sl[:].rearrange("p a b -> p (a b)")
            p.op("sync", lambda e: e.dma_start(out=asl_f, in_=alog4),
                 inc="ld", inck=16, wait=("dve",))
            p.op("scalar", lambda e: e.activation(
                out=big_a[:, :NDB * DS], in_=asl_f, func=AF.Exp),
                inc="act", wait=("ld",))
            p.op("vector", lambda e: e.tensor_scalar_mul(
                out=asl_f, in0=big_a[:, :NDB * DS], scalar1=-1.0),
                inc="dve", wait=("act",))

            crow_t = xt[0:1, :]
            p.op("sync", lambda e: e.dma_start(out=crow_t, in_=c_row),
                 inc="ld", inck=16)
            csil = bct[0:1, :D]
            p.op("scalar", lambda e: e.activation(out=csil, in_=crow_t,
                                                  func=AF.Silu),
                 inc="act", wait=("ld",))
            for kc in range(8):
                p.op("tensor", lambda e, kc=kc: e.transpose(
                    out=ps_t[:, 0:1], in_=csil[:, kc * 128:(kc + 1) * 128],
                    identity=ident_f[0:1, 0:1]), inc="te",
                    wait=("dve", "ld", "act"))
                p.op("vector", lambda e, kc=kc: e.tensor_copy(
                    out=silc_f[:, kc:kc + 1], in_=ps_t[:, 0:1]),
                    inc="dve", wait=("te",))
            sa_r = big_a[0:1, 0 * D:1 * D]; ca_r = big_a[0:1, 1 * D:2 * D]
            ga_r = big_a[0:1, 2 * D:3 * D]; sb_r = big_a[0:1, 3 * D:4 * D]
            cb_r = big_c[0:1, 0 * D:1 * D]; gb_r = big_c[0:1, 1 * D:2 * D]
            segs = [sa_r, ca_r, ga_r, sb_r, cb_r, gb_r]
            for nch in range(12):
                st32 = big_b[:, :8 * 512]
                p.op("sync", lambda e, n=nch, dd=st32: e.dma_start(
                    out=dd.rearrange("p (k c) -> p k c", k=8),
                    in_=wmod_r[:, :, n * 512:(n + 1) * 512]),
                    inc="ld", inck=16, wait=("dve", "te"))
                for kc in range(8):
                    p.op("tensor", lambda e, kc=kc, s=st32: e.matmul(
                        out=ps_r[0:1, :], lhsT=silc_f[:, kc:kc + 1],
                        rhs=s.rearrange("p (k c) -> p k c", k=8)[:, kc, :],
                        start=(kc == 0), stop=(kc == 7)),
                        inc="te" if kc == 7 else None,
                        wait=("dve", "ld") if kc == 0 else ())
                tgt = segs[nch // 2][:, (nch % 2) * 512:(nch % 2) * 512 + 512]
                p.op("vector", lambda e, t=tgt: e.tensor_copy(
                    out=t, in_=ps_r[0:1, :]), inc="dve", wait=("te",))
            t1 = big_c[0:1, 2 * D:3 * D]
            t2 = big_c[0:1, 3 * D:4 * D]
            for g_r in (ga_r, gb_r):
                p.op("vector", lambda e, g=g_r: e.tensor_scalar(
                    out=t1, in0=g, scalar1=0.0, scalar2=2.0,
                    op0=ALU.is_ge, op1=ALU.mult), inc="dve", wait=("te",))
                p.op("vector", lambda e: e.tensor_single_scalar(
                    out=t1, in_=t1, scalar=1.0, op=ALU.subtract), inc="dve")
                p.op("vector", lambda e, g=g_r: e.tensor_scalar_mul(
                    out=t2, in0=g, scalar1=-1.0), inc="dve")
                p.op("vector", lambda e, g=g_r: e.tensor_tensor(
                    out=t2, in0=t2, in1=g, op=ALU.max), inc="dve")
                p.op("vector", lambda e: e.tensor_single_scalar(
                    out=t2, in_=t2, scalar=0.25, op=ALU.max), inc="dve")
                p.op("vector", lambda e, g=g_r: e.tensor_tensor(
                    out=g, in0=t1, in1=t2, op=ALU.mult), inc="dve")
            for c_r in (ca_r, cb_r):
                p.op("vector", lambda e, cr=c_r: e.tensor_single_scalar(
                    out=cr, in_=cr, scalar=1.0, op=ALU.add), inc="dve")
            for nmm, r in (("sa", sa_r), ("ca1", ca_r), ("ga", ga_r),
                           ("sbm", sb_r), ("cb1", cb_r), ("gb", gb_r)):
                for h in range(2):
                    p.op("tensor", lambda e, r=r, h=h: e.matmul(
                        out=ps_r, lhsT=ones_f[:],
                        rhs=r[:, h * 512:(h + 1) * 512], start=True, stop=True),
                        inc="te", wait=("dve",))
                    p.op("vector", lambda e, nm=nmm, h=h: e.tensor_copy(
                        out=reps[nm][:, h * 512:(h + 1) * 512], in_=ps_r),
                        inc="dve", wait=("te",))

        def ln1(c):
            for tt in range(QT):
                r0 = c * Q + tt * 128
                p.op("sync", lambda e, r0=r0: e.dma_start(
                    out=xt[:], in_=x_b[r0:r0 + 128, :]),
                    inc="ld", inck=16, wait=("dve", "te"))
                c0, c1, c2, c3, c4, c5 = cols
                p.op("vector", lambda e: e.reduce_sum(
                    c0[:], xt[:], axis=AX.X), inc="dve", wait=("ld",))
                p.op("vector", lambda e: e.tensor_scalar_mul(
                    out=c1[:], in0=c0[:], scalar1=1.0 / D), inc="dve")
                p.op("vector", lambda e: e.tensor_tensor(
                    out=big_c[:, :D], in0=xt[:], in1=xt[:], op=ALU.mult),
                    inc="dve", wait=("gp",))
                p.op("vector", lambda e: e.reduce_sum(
                    c2[:], big_c[:, :D], axis=AX.X), inc="dve")
                p.op("vector", lambda e: e.tensor_scalar_mul(
                    out=c3[:], in0=c2[:], scalar1=1.0 / D), inc="dve")
                p.op("vector", lambda e: e.tensor_tensor(
                    out=c4[:], in0=c1[:], in1=c1[:], op=ALU.mult), inc="dve")
                p.op("vector", lambda e: e.tensor_tensor(
                    out=c5[:], in0=c3[:], in1=c4[:], op=ALU.subtract),
                    inc="dve")
                p.op("scalar", lambda e: e.activation(
                    out=c0[:], in_=c5[:], func=AF.Sqrt, bias=cst_eps[:]),
                    inc="act", wait=("dve",))
                p.op("vector", lambda e: e.reciprocal(c2[:], c0[:]),
                     inc="dve", wait=("act",))
                hf32 = big_d[:, 0:D]
                p.op("vector", lambda e, h=hf32: e.tensor_scalar(
                    out=h, in0=xt[:], scalar1=c1[:], scalar2=c2[:],
                    op0=ALU.subtract, op1=ALU.mult), inc="dve")
                p.op("vector", lambda e, h=hf32: e.tensor_tensor(
                    out=h, in0=h, in1=reps["ca1"][:],
                    op=ALU.mult), inc="dve")
                p.op("vector", lambda e, h=hf32: e.tensor_tensor(
                    out=h, in0=h, in1=reps["sa"][:], op=ALU.add),
                    inc="dve")
                for kc in range(8):
                    p.op("tensor", lambda e, kc=kc, h=hf32: e.transpose(
                        out=ps_t[:, 0:128], in_=h[:, kc * 128:(kc + 1) * 128],
                        identity=ident_f[:]), inc="te", wait=("dve",))
                    p.op("vector", lambda e, kc=kc, tt=tt: e.tensor_copy(
                        out=hT3[:, kc, tt * 128:(tt + 1) * 128],
                        in_=ps_t[:, 0:128]),
                        inc="dve", wait=("te",))

        def in_proj_conv(c):
            for wsel, wtile in ((0, wxs_h), (1, wz_h)):
                for db in range(NDB):
                    for hf in range(2):
                        psd = ps_a if (db + hf) % 2 == 0 else ps_b
                        for kc in range(8):
                            p.op("tensor",
                                 lambda e, db=db, hf=hf, kc=kc, ps=psd, w=wtile:
                                 e.matmul(
                                     out=ps[:],
                                     lhsT=w[:, kc, db * 128:(db + 1) * 128],
                                     rhs=hT3[:, kc, hf * 512:(hf + 1) * 512],
                                     start=(kc == 0), stop=(kc == 7)),
                                 inc="te" if kc == 7 else None,
                                 wait=("dve",) if kc == 0 else ())
                        if wsel == 0:
                            p.op("vector",
                                 lambda e, db=db, hf=hf, ps=psd: e.tensor_copy(
                                     out=xsr[:, db,
                                             KC + hf * 512:KC + (hf + 1) * 512],
                                     in_=ps[:]), inc="dve", wait=("te",))
                        else:
                            p.op("vector",
                                 lambda e, db=db, hf=hf, ps=psd: e.tensor_copy(
                                     out=z3[:, db, hf * 512:(hf + 1) * 512],
                                     in_=ps[:]), inc="dve", wait=("te",))
            p.op("vector", lambda e: e.tensor_copy(
                out=xsr[:, :, 1:KC], in_=halo[:]), inc="dve")
            p.op("vector", lambda e: e.tensor_copy(
                out=halo[:], in_=xsr[:, :, KC + Q - (KC - 1):KC + Q]),
                inc="dve")
            for db in range(NDB):
                p.op("vector", lambda e, db=db: e.tensor_scalar(
                    out=tmp_a[:, db, :], in0=xsr[:, db, 1:1 + Q],
                    scalar1=convw[:, db, 0:1], scalar2=None, op0=ALU.mult),
                    inc="dve", wait=("gp",) if db == 0 else ())
                for k in range(1, KC):
                    p.op("vector", lambda e, db=db, k=k: e.scalar_tensor_tensor(
                        out=tmp_a[:, db, :], in0=xsr[:, db, 1 + k:1 + k + Q],
                        scalar=convw[:, db, k:k + 1], in1=tmp_a[:, db, :],
                        op0=ALU.mult, op1=ALU.add), inc="dve")
                p.op("scalar", lambda e, db=db: e.activation(
                    out=xssi3[:, db, :], in_=tmp_a[:, db, :], func=AF.Silu,
                    bias=convb[:, db:db + 1]), inc="act", wait=("dve",))

        def x_proj_reduce(c):
            for hf in range(2):
                for db in range(NDB):
                    p.op("tensor", lambda e, db=db, hf=hf: e.matmul(
                        out=ps_a, lhsT=wxp_h[:, db, 0:128],
                        rhs=xssi3[:, db, hf * 512:(hf + 1) * 512],
                        start=(db == 0), stop=(db == NDB - 1)),
                        inc="te" if db == NDB - 1 else None,
                        wait=("act", "dve") if db == 0 else ())
                p.op("vector", lambda e, hf=hf: e.tensor_copy(
                    out=bct[:, hf * 512:(hf + 1) * 512], in_=ps_a),
                    inc="dve", wait=("te",))
                for db in range(NDB):
                    p.op("tensor", lambda e, db=db, hf=hf: e.matmul(
                        out=ps_b[0:1, :], lhsT=wxp_h[:, db, 128:129],
                        rhs=xssi3[:, db, hf * 512:(hf + 1) * 512],
                        start=(db == 0), stop=(db == NDB - 1)),
                        inc="te" if db == NDB - 1 else None)
                p.op("vector", lambda e, hf=hf: e.tensor_copy(
                    out=dtr_t[:, hf * 512:(hf + 1) * 512], in_=ps_b[0:1, :]),
                    inc="dve", wait=("te",))
            p.op("gpsimd", lambda e, c=c: e.dma_start(
                out=xp_part[c][0:128, :], in_=bct[:]), inc="gp", inck=16,
                wait=("dve",))
            p.op("gpsimd", lambda e, c=c: e.dma_start(
                out=xp_part[c][128:129, :], in_=dtr_t[:]),
                inc="gp", inck=16)
            ccn = new_cc()
            p.op("gpsimd", lambda e, c=c: e.collective_compute(
                "AllReduce", ALU.add, replica_groups=G2,
                ins=[xp_part[c]], outs=[xp_red[c]]), inc=ccn, wait=("gp",))
            p.op("sync", lambda e, c=c: e.dma_start(
                out=bct[:], in_=xp_red[c][0:128, :]), inc="ld", inck=16,
                wait=((ccn, 1), "dve"))
            p.op("sync", lambda e, c=c: e.dma_start(
                out=dtr_t[:], in_=xp_red[c][128:129, :]),
                inc="ld", inck=16, wait=("te",))
            p.op("vector", lambda e: e.tensor_copy(out=bc_h, in_=bct[:]),
                 inc="dve", wait=("ld", "gp"))

        def dt_u_build(c):
            for hf in range(2):
                p.op("tensor", lambda e, hf=hf: e.matmul(
                    out=ps_r, lhsT=ones_f[:],
                    rhs=dtr_t[:, hf * 512:(hf + 1) * 512], start=True,
                    stop=True), inc="te", wait=("dve", "ld"))
                for db in range(NDB):
                    p.op("vector", lambda e, db=db, hf=hf: e.tensor_scalar(
                        out=tmp_a[:, db, hf * 512:(hf + 1) * 512], in0=ps_r,
                        scalar1=dtw[:, db:db + 1], scalar2=dtb[:, db:db + 1],
                        op0=ALU.mult, op1=ALU.add), inc="dve",
                        wait=("te", "gp") if db == 0 else ("te",))
            p.op("vector", lambda e: e.tensor_scalar(
                out=dt_t[:], in0=tmp_a[:], scalar1=0.0, scalar2=None,
                op0=ALU.max), inc="dve", sw=False, wait=("gp",))
            p.op("scalar", lambda e: e.activation(
                out=dAv[:], in_=tmp_a[:], func=AF.Abs), inc="act",
                wait=("dve",))
            p.op("scalar", lambda e: e.activation(
                out=tmp_a[:], in_=dAv[:], func=AF.Exp, scale=cst_neg1[:]),
                inc="act")
            p.op("scalar", lambda e: e.activation(
                out=dAv[:], in_=tmp_a[:], func=AF.Ln, bias=cst_one[:]), inc="act")
            p.op("vector", lambda e: e.tensor_tensor(
                out=dt_t[:], in0=dt_t[:], in1=dAv[:], op=ALU.add),
                inc="dve", sw=False, wait=("act",))
            p.op("vector", lambda e: e.tensor_tensor(
                out=u_h, in0=big_a[:], in1=xssi, op=ALU.mult),
                inc="dve", sw=False)

        def scan(c):
            dabuf = [dAv, yacc]
            NB3 = 4
            gp_marks = {}
            dve_marks = {}

            def emit_bcast(n):
                nb = n % NB3
                p.op("gpsimd", lambda e, n=n, nb=nb: e.dma_start(
                    out=bb[nb],
                    in_=bc_h[n:n + 1, :].unsqueeze(1)
                    .broadcast_to([1, 128, Q])), inc="gp", inck=16,
                    wait=("dve",))
                p.op("gpsimd", lambda e, n=n, nb=nb: e.dma_start(
                    out=cbb[nb],
                    in_=bc_h[64 + n:65 + n, :].unsqueeze(1)
                    .broadcast_to([1, 128, Q])), inc="gp", inck=16)
                gp_marks[n] = p.cnt["gp"]
            emit_bcast(0)
            emit_bcast(1)
            emit_bcast(2)
            dve_marks[-2] = dve_marks[-1] = p.cnt["dve"]
            for n in range(DS):
                nb = n % NB3
                da = dabuf[n % 2]
                for db in range(NDB):
                    p.op("scalar", lambda e, db=db, n=n, da=da: e.activation(
                        out=da[:, db, :], in_=dt_t[:, db, :], func=AF.Exp,
                        scale=a_sl[:, db, n:n + 1]),
                        inc="act" if db == NDB - 1 else None,
                        wait=(("dve", dve_marks[n - 2]),) if db == 0 else ())
                if n + 3 < DS:
                    emit_bcast(n + 3)
                for db in range(NDB):
                    p.op("vector", lambda e, db=db, nb=nb: e.tensor_tensor(
                        out=inj3[:, db, :], in0=u3[:, db, :], in1=bb[nb],
                        op=ALU.mult), inc="dve", sw=False,
                        wait=(("act", p.cnt["act"]), ("gp", gp_marks[n]))
                        if db == 0 else ())
                for db in range(NDB):
                    p.op("vector", lambda e, db=db, n=n, da=da:
                         e.tensor_tensor_scan(
                             out=da[:, db, :], data0=da[:, db, :],
                             data1=inj3[:, db, :],
                             initial=carry[:, db, n:n + 1],
                             op0=ALU.mult, op1=ALU.add), inc="dve", sw=False)
                p.op("vector", lambda e, n=n, da=da: e.tensor_copy(
                    out=carry[:, :, n:n + 1], in_=da[:, :, Q - 1:Q]),
                    inc="dve")
                p.op("vector", lambda e, nb=nb, da=da: e.tensor_tensor(
                    out=tmp_a[:], in0=da[:],
                    in1=cbb[nb].unsqueeze(1).broadcast_to([128, NDB, Q]),
                    op=ALU.mult), inc="dve", sw=False, wait=("te",))
                for db in range(NDB):
                    for tb in range(8):
                        blk = (db * 8 + tb) * 128
                        p.op("tensor",
                             lambda e, db=db, tb=tb, blk=blk, n=n: e.matmul(
                                 out=ps_all[:, blk:blk + 128],
                                 lhsT=tmp_a[:, db, tb * 128:(tb + 1) * 128],
                                 rhs=ident_f[:], is_transpose=True,
                                 start=(n == 0 and tb % 4 == 0),
                                 stop=(n == DS - 1 and tb % 4 == 3)),
                             inc="te" if (db == NDB - 1 and tb == 7) else None,
                             wait=("dve",) if (db == 0 and tb == 0) else ())
                dve_marks[n] = p.cnt["dve"]
            p.op("vector", lambda e: e.tensor_copy(
                out=big_c[:], in_=ps_all[:]), inc="dve", wait=("te",))
            for db in range(NDB):
                for tb in range(8):
                    blk = (db * 8 + tb) * 128
                    p.op("tensor", lambda e, blk=blk, tb=tb: e.matmul(
                        out=ps_all[:, blk:blk + 128],
                        lhsT=big_c[:, blk:blk + 128],
                        rhs=ident_f[:], is_transpose=True,
                        start=(tb % 4 == 0), stop=(tb % 4 == 3)),
                        inc="te" if blk == (NDB * 8 - 1) * 128 else None,
                        wait=("dve",) if blk == 0 else ())
            p.op("vector", lambda e: e.tensor_copy(
                out=big_d[:], in_=ps_all[:]), inc="dve", wait=("te",))
            for db in range(NDB):
                p.op("vector", lambda e, db=db: e.scalar_tensor_tensor(
                    out=yacc[:, db, :], in0=xssi3[:, db, :],
                    scalar=d_col[:, db:db + 1], in1=yacc[:, db, :],
                    op0=ALU.mult, op1=ALU.add), inc="dve")
            p.op("scalar", lambda e: e.activation(
                out=u_h, in_=z_h, func=AF.Silu), inc="act",
                wait=("dve",))
            p.op("vector", lambda e: e.tensor_tensor(
                out=z_h, in0=big_d[:], in1=u_h, op=ALU.mult),
                inc="dve", wait=("act",))

        def out_proj(c):
            po_lo = dt_t
            po_hi = tmp_a
            for mt in range(8):
                dst = po_lo if mt < 4 else po_hi
                for hf in range(2):
                    psd = ps_a if (mt + hf) % 2 == 0 else ps_b
                    for db in range(NDB):
                        p.op("tensor", lambda e, mt=mt, hf=hf, db=db, ps=psd:
                             e.matmul(out=ps[:],
                                      lhsT=wout_h[:, db, mt, :],
                                      rhs=z3[:, db, hf * 512:(hf + 1) * 512],
                                      start=(db == 0), stop=(db == NDB - 1)),
                             inc="te" if db == NDB - 1 else None,
                             wait=("dve",) if db == 0 else ())
                    p.op("vector", lambda e, mt=mt, hf=hf, dd=dst, ps=psd:
                         e.tensor_copy(
                             out=dd[:, mt % 4, hf * 512:(hf + 1) * 512],
                             in_=ps[:]), inc="dve",
                         wait=("te", "ld") if (DBG and mt == 0 and hf == 0)
                         else ("te",))
            for sq in range(4):
                p.op("gpsimd", lambda e, c=c, sq=sq: e.dma_start(
                    out=po_d[c][sq * 1024:sq * 1024 + 512, :]
                    .rearrange("(a p) q -> p a q", p=128),
                    in_=po_lo[:, :, sq * NQ4:(sq + 1) * NQ4]),
                    inc="gp", inck=16, wait=("dve",) if sq == 0 else ())
                p.op("gpsimd", lambda e, c=c, sq=sq: e.dma_start(
                    out=po_d[c][sq * 1024 + 512:(sq + 1) * 1024, :]
                    .rearrange("(a p) q -> p a q", p=128),
                    in_=po_hi[:, :, sq * NQ4:(sq + 1) * NQ4]),
                    inc="gp", inck=16)
            ccn = new_cc()
            p.op("gpsimd", lambda e, c=c: e.collective_compute(
                "ReduceScatter", ALU.add, replica_groups=G2,
                ins=[po_d[c]], outs=[attn_q[c]]), inc=ccn, wait=("gp",))
            return ccn

        def ffn_weight_quant():
            mats = [(wg_t, wq_g_d, 0), (wu_t, wq_u_d, 1), (wd_t, wq_d_d, 2)]
            for src, dst, mi in mats:
                srcf = src.rearrange("p a b c -> p (a b c)")
                dstf = dst.rearrange("p a b c -> p (a b c)")
                acc, tmpc = cols[0], cols[1]
                p.op("vector", lambda e, a=acc: e.memset(a[:], 0.0), inc="dve")
                for ch in range(8):
                    st32 = big_a[:, :4096]
                    p.op("sync", lambda e, s=srcf, ch=ch, dd=st32: e.dma_start(
                        out=dd, in_=s[:, ch * 4096:(ch + 1) * 4096]),
                        inc="ld", inck=16, wait=("dve", "gp", "te"))
                    p.op("vector", lambda e, dd=st32, t=tmpc: e.reduce_sum(
                        t[:], dd, axis=AX.X, apply_absolute_value=True),
                        inc="dve", wait=("ld",))
                    p.op("vector", lambda e, a=acc, t=tmpc: e.tensor_tensor(
                        out=a[:], in0=a[:], in1=t[:], op=ALU.add), inc="dve")
                p.op("tensor", lambda e, a=acc: e.matmul(
                    out=ps_r[0:1, 0:1], lhsT=onesc_f[:], rhs=a[:],
                    start=True, stop=True), inc="te", wait=("dve",))
                p.op("vector", lambda e: e.tensor_scalar(
                    out=rows4[:, 0:1], in0=ps_r[0:1, 0:1],
                    scalar1=1.0 / (4096.0 * 1024.0), scalar2=1e-5,
                    op0=ALU.mult, op1=ALU.max), inc="dve", wait=("te",))
                p.op("vector", lambda e: e.reciprocal(
                    rows4[:, 1:2], rows4[:, 0:1]), inc="dve")
                p.op("gpsimd", lambda e: e.dma_start(
                    out=swb_d[0:1, 0:2], in_=rows4[:, 0:2]),
                    inc="gp", inck=16, wait=("dve",))
                p.op("sync", lambda e, mi=mi: e.dma_start(
                    out=sw3[:, mi:mi + 1],
                    in_=swb_d[0:1, 0:1].broadcast_to([128, 1])),
                    inc="ld", inck=16, wait=("gp",))
                p.op("sync", lambda e: e.dma_start(
                    out=swq[:], in_=swb_d[0:1, 1:2].broadcast_to([128, 1])),
                    inc="ld", inck=16)
                for ch in range(8):
                    st32 = big_a[:, :4096]
                    qv = big_b[:, :4096]
                    g2v = big_c[:, :4096]
                    st16 = wstr16[:, :4096]
                    p.op("sync", lambda e, s=srcf, ch=ch, dd=st32: e.dma_start(
                        out=dd, in_=s[:, ch * 4096:(ch + 1) * 4096]),
                        inc="ld", inck=16, wait=("dve",))
                    p.op("vector", lambda e, dd=st32, q=qv: e.tensor_scalar_mul(
                        out=q, in0=dd, scalar1=swq[:]), inc="dve",
                        wait=("ld", "gp"))
                    p.op("vector", lambda e, q=qv, g=g2v: e.tensor_single_scalar(
                        out=g, in_=q, scalar=-0.5, op=ALU.is_lt), inc="dve")
                    p.op("vector", lambda e, q=qv: e.tensor_single_scalar(
                        out=q, in_=q, scalar=0.5, op=ALU.is_gt), inc="dve")
                    p.op("vector", lambda e, q=qv, g=g2v: e.tensor_tensor(
                        out=q, in0=q, in1=g, op=ALU.subtract), inc="dve")
                    p.op("vector", lambda e, q=qv, dd=st16: e.tensor_copy(
                        out=dd, in_=q), inc="dve")
                    p.op("gpsimd", lambda e, dd=dstf, ch=ch, s=st16:
                         e.dma_start(out=dd[:, ch * 4096:(ch + 1) * 4096],
                                     in_=s), inc="gp", inck=16, wait=("dve",))

        def ffn_quant1(rs_sems):
            ntile = 0
            for c in range(NCH):
                aq32 = big_b[:, :8 * NQ4].rearrange("p (k t) -> p k t", k=8)
                p.op("sync", lambda e, c=c, a=aq32: e.dma_start(
                    out=a, in_=attn_q[c].rearrange("(k p) t -> p k t", p=128)),
                    inc="ld", inck=16, wait=((rs_sems[c], 1), "dve", "gp"))
                for tt in range(NQ4 // 128):
                    r0 = c * NQ4 + tt * 128
                    p.op("sync", lambda e, r0=r0: e.dma_start(
                        out=xt[:], in_=x_own[r0:r0 + 128, :]),
                        inc="ld", inck=16, wait=("dve", "te"))
                    x2t = big_d[:, :D]
                    for kc in range(8):
                        p.op("tensor", lambda e, kc=kc, tt=tt, a=aq32:
                             e.transpose(
                                 out=ps_t,
                                 in_=a[:, kc, tt * 128:(tt + 1) * 128],
                                 identity=ident_f[:]),
                             inc="te", wait=("ld", "dve"))
                        p.op("vector", lambda e, kc=kc: e.tensor_copy(
                            out=big_c[:, kc * 128:(kc + 1) * 128],
                            in_=ps_t), inc="dve", wait=("te",))
                    p.op("vector", lambda e: e.tensor_tensor(
                        out=x2t, in0=big_c[:, :D], in1=reps["ga"][:],
                        op=ALU.mult), inc="dve", wait=("ld",))
                    p.op("vector", lambda e: e.tensor_tensor(
                        out=x2t, in0=x2t, in1=xt[:], op=ALU.add), inc="dve")
                    p.op("gpsimd", lambda e, r0=r0: e.dma_start(
                        out=x2_d[r0:r0 + 128, :], in_=x2t),
                        inc="gp", inck=16, wait=("dve",))
                    c0, c1, c2, c3, c4, c5 = cols
                    h2 = big_c[:, :D]
                    p.op("vector", lambda e: e.reduce_sum(
                        c0[:], x2t, axis=AX.X), inc="dve")
                    p.op("vector", lambda e: e.tensor_scalar_mul(
                        out=c1[:], in0=c0[:], scalar1=1.0 / D), inc="dve")
                    p.op("vector", lambda e: e.tensor_tensor(
                        out=h2, in0=x2t, in1=x2t, op=ALU.mult), inc="dve")
                    p.op("vector", lambda e: e.reduce_sum(
                        c2[:], h2, axis=AX.X), inc="dve")
                    p.op("vector", lambda e: e.tensor_scalar_mul(
                        out=c3[:], in0=c2[:], scalar1=1.0 / D), inc="dve")
                    p.op("vector", lambda e: e.tensor_tensor(
                        out=c4[:], in0=c1[:], in1=c1[:], op=ALU.mult),
                        inc="dve")
                    p.op("vector", lambda e: e.tensor_tensor(
                        out=c5[:], in0=c3[:], in1=c4[:], op=ALU.subtract),
                        inc="dve")
                    p.op("scalar", lambda e: e.activation(
                        out=c0[:], in_=c5[:], func=AF.Sqrt, bias=cst_eps[:]),
                        inc="act", wait=("dve",))
                    p.op("vector", lambda e: e.reciprocal(c2[:], c0[:]),
                         inc="dve", wait=("act",))
                    p.op("vector", lambda e: e.tensor_scalar(
                        out=h2, in0=x2t, scalar1=c1[:], scalar2=c2[:],
                        op0=ALU.subtract, op1=ALU.mult), inc="dve")
                    p.op("vector", lambda e: e.tensor_tensor(
                        out=h2, in0=h2, in1=reps["cb1"][:], op=ALU.mult),
                        inc="dve")
                    p.op("vector", lambda e: e.tensor_tensor(
                        out=h2, in0=h2, in1=reps["sbm"][:], op=ALU.add),
                        inc="dve")
                    mx, sc, iv = cols[3], cols[4], cols[5]
                    p.op("vector", lambda e: e.reduce_max(
                        mx[:], h2, axis=AX.X, apply_absolute_value=True),
                        inc="dve")
                    p.op("vector", lambda e: e.tensor_single_scalar(
                        out=mx[:], in_=mx[:], scalar=1e-5, op=ALU.max),
                        inc="dve")
                    p.op("vector", lambda e: e.reciprocal(sc[:], mx[:]),
                         inc="dve")
                    p.op("vector", lambda e: e.tensor_scalar_mul(
                        out=sc[:], in0=sc[:], scalar1=127.0), inc="dve")
                    p.op("vector", lambda e: e.tensor_scalar_mul(
                        out=iv[:], in0=mx[:], scalar1=1.0 / 127.0), inc="dve")
                    p.op("vector", lambda e, nt=ntile: e.tensor_copy(
                        out=big_a[:, nt:nt + 1], in_=iv[:]), inc="dve")
                    q = big_d[:, D:2 * D]
                    p.op("vector", lambda e: e.tensor_scalar_mul(
                        out=q, in0=h2, scalar1=sc[:]), inc="dve")
                    p.op("vector", lambda e: e.tensor_scalar(
                        out=q, in0=q, scalar1=MAGIC, scalar2=MAGIC,
                        op0=ALU.add, op1=ALU.subtract), inc="dve")
                    p.op("vector", lambda e: e.tensor_scalar(
                        out=q, in0=q, scalar1=127.0, scalar2=-128.0,
                        op0=ALU.min, op1=ALU.max), inc="dve")
                    p.op("vector", lambda e: e.tensor_copy(
                        out=ht_h[:], in_=q), inc="dve")
                    p.op("gpsimd", lambda e, r0=r0: e.dma_start(
                        out=q1l_in[r0:r0 + 128, :], in_=ht_h[:]),
                        inc="gp", inck=16, wait=("dve",))
                    ntile += 1
            ccn = new_cc()
            p.op("gpsimd", lambda e: e.collective_compute(
                "AllGather", ALU.bypass, replica_groups=G1,
                ins=[q1l_in], outs=[q1l_out]), inc=ccn, wait=("gp",))
            return ccn

        def ffn_main(q1cc):
            invcols = big_a
            q2ccs = []
            for blk in range(NB4):
                xqT = hT3
                for tt in range(4):
                    r0 = blk * 512 + tt * 128
                    p.op("sync", lambda e, r0=r0: e.dma_start(
                        out=ht_h[:], in_=q1l_out[r0:r0 + 128, :]),
                        inc="ld", inck=16, wait=((q1cc, 1), "dve", "te"))
                    p.op("vector", lambda e: e.tensor_copy(
                        out=xt[:], in_=ht_h[:]), inc="dve", wait=("ld",))
                    for kc in range(8):
                        p.op("tensor", lambda e, kc=kc: e.transpose(
                            out=ps_t[:, 0:128],
                            in_=xt[:, kc * 128:(kc + 1) * 128],
                            identity=ident_f[:]), inc="te", wait=("dve",))
                        p.op("vector", lambda e, kc=kc, tt=tt: e.tensor_copy(
                            out=xqT[:, kc, tt * 128:(tt + 1) * 128],
                            in_=ps_t[:, 0:128]), inc="dve", wait=("te",))
                er = xt[0:1, 0:512]
                for tt in range(4):
                    cidx = blk * 4 + tt
                    p.op("tensor", lambda e, cidx=cidx: e.transpose(
                        out=ps_t[0:1, 0:128], in_=invcols[:, cidx:cidx + 1],
                        identity=ident_f[:]), inc="te", wait=("dve",))
                    p.op("vector", lambda e, tt=tt: e.tensor_copy(
                        out=er[:, tt * 128:(tt + 1) * 128],
                        in_=ps_t[0:1, 0:128]), inc="dve", wait=("te",))
                epi_g = big_c[:, 0:512]
                epi_u = big_c[:, 512:1024]
                p.op("tensor", lambda e: e.matmul(
                    out=ps_r, lhsT=ones_f[:], rhs=er, start=True,
                    stop=True), inc="te", wait=("dve",))
                p.op("vector", lambda e: e.tensor_scalar_mul(
                    out=epi_g, in0=ps_r, scalar1=sw3[:, 0:1]),
                    inc="dve", wait=("te",))
                p.op("vector", lambda e: e.tensor_scalar_mul(
                    out=epi_u, in0=ps_r, scalar1=sw3[:, 1:2]), inc="dve")
                ld_marks = {}
                te0 = p.cnt["te"]
                te_marks = {}

                def wq_load(mt):
                    sl = (mt % 2) * 16
                    p.op("sync", lambda e, mt=mt, sl=sl: e.dma_start(
                        out=wst3[:, sl:sl + 8, :], in_=wq_g_d[:, mt, :, :]),
                        inc="ld", inck=16,
                        wait=(("te", te_marks.get(mt - 2, te0)), "dve"))
                    p.op("sync", lambda e, mt=mt, sl=sl: e.dma_start(
                        out=wst3[:, sl + 8:sl + 16, :],
                        in_=wq_u_d[:, mt, :, :]), inc="ld", inck=16)
                    ld_marks[mt] = p.cnt["ld"]
                wq_load(0)
                for mt in range(32):
                    sl = (mt % 2) * 16
                    if mt + 1 < 32:
                        wq_load(mt + 1)
                    for kcc in range(8):
                        p.op("tensor", lambda e, kcc=kcc, sl=sl: e.matmul(
                            out=ps_a, lhsT=wst3[:, sl + kcc, :],
                            rhs=xqT[:, kcc, 0:512], start=(kcc == 0),
                            stop=(kcc == 7)),
                            inc="te" if kcc == 7 else None,
                            wait=(("ld", ld_marks[mt]), "dve")
                            if kcc == 0 else ())
                    for kcc in range(8):
                        p.op("tensor", lambda e, kcc=kcc, sl=sl: e.matmul(
                            out=ps_b, lhsT=wst3[:, sl + 8 + kcc, :],
                            rhs=xqT[:, kcc, 0:512], start=(kcc == 0),
                            stop=(kcc == 7)),
                            inc="te" if kcc == 7 else None,
                            wait=("dve",) if kcc == 0 else ())
                    te_marks[mt] = p.cnt["te"]
                    p.op("vector", lambda e: e.tensor_tensor(
                        out=gtl, in0=ps_a, in1=epi_g, op=ALU.mult),
                        inc="dve", wait=("te",))
                    p.op("scalar", lambda e: e.activation(
                        out=gtl2, in_=gtl, func=AF.Silu), inc="act",
                        wait=("dve",))
                    p.op("vector", lambda e: e.tensor_tensor(
                        out=utl, in0=ps_b, in1=epi_u, op=ALU.mult),
                        inc="dve")
                    p.op("vector", lambda e, mt=mt: e.tensor_tensor(
                        out=gu3[:, mt, :], in0=gtl2, in1=utl,
                        op=ALU.mult), inc="dve",
                        wait=("act", "gp") if mt == 0 else ("act",))
                    if mt == 0:
                        p.op("vector", lambda e: e.tensor_copy(
                            out=rmx, in_=gu3[:, 0, :]), inc="dve")
                        p.op("vector", lambda e: e.tensor_copy(
                            out=rmn, in_=gu3[:, 0, :]), inc="dve")
                    else:
                        p.op("vector", lambda e, mt=mt: e.tensor_tensor(
                            out=rmx, in0=rmx, in1=gu3[:, mt, :],
                            op=ALU.max), inc="dve")
                        p.op("vector", lambda e, mt=mt: e.tensor_tensor(
                            out=rmn, in0=rmn, in1=gu3[:, mt, :],
                            op=ALU.min), inc="dve")
                p.op("vector", lambda e: e.tensor_scalar_mul(
                    out=rmn, in0=rmn, scalar1=-1.0), inc="dve")
                p.op("gpsimd", lambda e: e.tensor_reduce(
                    out=big_b[0:1, 0:512], in_=rmx,
                    axis=AX.C, op=ALU.max), inc="gpc", wait=("dve",))
                p.op("gpsimd", lambda e: e.tensor_reduce(
                    out=big_b[0:1, 512:1024], in_=rmn,
                    axis=AX.C, op=ALU.max), inc="gpc")
                p.op("vector", lambda e: e.tensor_tensor(
                    out=xt[0:1, 0:512], in0=big_b[0:1, 0:512],
                    in1=big_b[0:1, 512:1024], op=ALU.max),
                    inc="dve", wait=("gpc", "te"))
                p.op("vector", lambda e: e.tensor_single_scalar(
                    out=xt[0:1, 0:512], in_=xt[0:1, 0:512], scalar=1e-5,
                    op=ALU.max), inc="dve")
                p.op("vector", lambda e: e.tensor_scalar_mul(
                    out=xt[0:1, 512:1024], in0=xt[0:1, 0:512],
                    scalar1=1.0 / 127.0), inc="dve")
                p.op("vector", lambda e, blk=blk: e.tensor_copy(
                    out=invcols[0:1, 2048 + blk * 512:2048 + (blk + 1) * 512],
                    in_=xt[0:1, 512:1024]), inc="dve")
                p.op("vector", lambda e: e.reciprocal(
                    xt[0:1, 0:512], xt[0:1, 0:512]), inc="dve")
                p.op("vector", lambda e: e.tensor_scalar_mul(
                    out=xt[0:1, 0:512], in0=xt[0:1, 0:512], scalar1=127.0),
                    inc="dve")
                p.op("tensor", lambda e: e.matmul(
                    out=ps_r, lhsT=ones_f[:], rhs=xt[0:1, 0:512],
                    start=True, stop=True), inc="te", wait=("dve",))
                s2rep = big_c[:, 1024:1536]
                p.op("vector", lambda e: e.tensor_copy(
                    out=s2rep, in_=ps_r), inc="dve", wait=("te",))
                for mt in range(32):
                    qq = big_b[:, 0:512]
                    p.op("vector", lambda e, mt=mt: e.tensor_tensor(
                        out=qq, in0=gu3[:, mt, :], in1=s2rep, op=ALU.mult),
                        inc="dve")
                    p.op("vector", lambda e: e.tensor_scalar(
                        out=qq, in0=qq, scalar1=MAGIC, scalar2=MAGIC,
                        op0=ALU.add, op1=ALU.subtract), inc="dve")
                    p.op("vector", lambda e: e.tensor_scalar(
                        out=qq, in0=qq, scalar1=127.0, scalar2=-128.0,
                        op0=ALU.min, op1=ALU.max), inc="dve")
                    p.op("vector", lambda e, mt=mt: e.tensor_copy(
                        out=gu3[:, mt, :], in_=qq), inc="dve")
                p.op("gpsimd", lambda e, blk=blk: e.dma_start(
                    out=q2l_in[blk].rearrange("(k p) t -> p k t", p=128),
                    in_=gu3[:]), inc="gp", inck=16, wait=("dve",))
                ccn = new_cc()
                p.op("gpsimd", lambda e, blk=blk: e.collective_compute(
                    "AllGather", ALU.bypass, replica_groups=G1,
                    ins=[q2l_in[blk]], outs=[q2l_out[blk]]), inc=ccn,
                    wait=("gp",))
                q2ccs.append(ccn)
            for blk in range(NB4):
                p.op("sync", lambda e, blk=blk: e.dma_start(
                    out=gu3[:],
                    in_=q2l_out[blk].rearrange("(k p) t -> p k t", p=128)),
                    inc="ld", inck=16,
                    wait=((q2ccs[blk], 1), "dve", "gp", "te"))
                p.op("tensor", lambda e, blk=blk: e.matmul(
                    out=ps_r, lhsT=ones_f[:],
                    rhs=invcols[0:1,
                                2048 + blk * 512:2048 + (blk + 1) * 512],
                    start=True, stop=True), inc="te", wait=("dve", "ld"))
                epi_d = big_c[:, 0:512]
                p.op("vector", lambda e: e.tensor_scalar_mul(
                    out=epi_d, in0=ps_r, scalar1=sw3[:, 2:3]),
                    inc="dve", wait=("te",))
                dl_marks = {}
                te_d0 = p.cnt["te"]
                dt_marks = {}

                def wd_load(h):
                    sl = (h % 2) * 16
                    mt_, hh = divmod(h, 2)
                    p.op("sync", lambda e, mt_=mt_, hh=hh, sl=sl: e.dma_start(
                        out=wst3[:, sl:sl + 16, :],
                        in_=wq_d_d[:, mt_, hh * 16:(hh + 1) * 16, :]),
                        inc="ld", inck=16,
                        wait=(("te", dt_marks.get(h - 2, te_d0)), "dve"))
                    dl_marks[h] = p.cnt["ld"]
                wd_load(0)
                wd_load(1)
                for mt in range(8):
                    for hh in range(2):
                        h = mt * 2 + hh
                        sl = (h % 2) * 16
                        for kc2 in range(16):
                            kcc = hh * 16 + kc2
                            p.op("tensor",
                                 lambda e, kcc=kcc, kc2=kc2, sl=sl: e.matmul(
                                     out=ps_a, lhsT=wst3[:, sl + kc2, :],
                                     rhs=gu3[:, kcc, :], start=(kcc == 0),
                                     stop=(kcc == 31)),
                                 inc="te" if kc2 == 15 else None,
                                 wait=(("ld", dl_marks[h]), "dve")
                                 if kc2 == 0 else ())
                        dt_marks[h] = p.cnt["te"]
                        if h + 2 < 16:
                            wd_load(h + 2)
                    p.op("vector", lambda e, mt=mt: e.tensor_tensor(
                        out=ffn8[:, mt, :], in0=ps_a, in1=epi_d,
                        op=ALU.mult), inc="dve", wait=("te",))
                for tt in range(4):
                    r0 = blk * 512 + tt * 128
                    p.op("sync", lambda e, r0=r0: e.dma_start(
                        out=xt[:], in_=x2_d[r0:r0 + 128, :]),
                        inc="ld", inck=16, wait=("dve", "te", "gp"))
                    fft = big_c[:, 0:D]
                    for kc in range(8):
                        p.op("tensor", lambda e, kc=kc, tt=tt: e.transpose(
                            out=ps_t,
                            in_=ffn8[:, kc, tt * 128:(tt + 1) * 128],
                            identity=ident_f[:]), inc="te", wait=("dve",))
                        p.op("vector", lambda e, kc=kc: e.tensor_copy(
                            out=fft[:, kc * 128:(kc + 1) * 128], in_=ps_t),
                            inc="dve", wait=("te",))
                    x3 = big_c[:, D:2 * D]
                    p.op("vector", lambda e: e.tensor_tensor(
                        out=x3, in0=fft, in1=reps["gb"][:], op=ALU.mult),
                        inc="dve", wait=("ld",))
                    p.op("vector", lambda e: e.tensor_tensor(
                        out=x3, in0=x3, in1=xt[:], op=ALU.add), inc="dve")
                    p.op("gpsimd", lambda e, r0=r0: e.dma_start(
                        out=y_out[r0:r0 + 128, :], in_=x3),
                        inc="gp", inck=16, wait=("dve",))

        def dump(name, src):
            p.op("sync", lambda e, nm=name, s2=src: e.dma_start(
                out=dbg[nm], in_=s2),
                inc="ld", inck=16, wait=("dve", "act", "te", "gp"))

        setup()
        rs_sems = []
        ln1(0)
        for c in range(NCH):
            in_proj_conv(c)
            x_proj_reduce(c)
            if c + 1 < NCH:
                ln1(c + 1)
            dt_u_build(c)
            scan(c)
            if DBG and c == 0:
                dump("dbg_xt", xt[:])
                for ci in range(6):
                    p.op("sync", lambda e, ci=ci: e.dma_start(
                        out=dbg["dbg_cols"][:, ci:ci + 1], in_=cols[ci][:],
                        allow_slow_non_contiguous=True),
                        inc="ld", inck=16, wait=("dve", "act", "te", "gp"))
                dump("dbg_hT", hT[:])
                dump("dbg_xsr", xsr[:].rearrange("p a q -> p (a q)"))
                dump("dbg_xssi", xssi)
                dump("dbg_bct", bct[:])
                dump("dbg_dtr", xt[0:1, :Q])
                dump("dbg_dt", big_a[:])
                dump("dbg_u", u_h)
                dump("dbg_y", big_d[:])
                dump("dbg_yz", z_h)
                dump("dbg_carry", carry[:].rearrange("p a q -> p (a q)"))
                dump("dbg_ca1", reps["ca1"][:])
                dump("dbg_sa", reps["sa"][:])
                dump("dbg_ga", reps["ga"][:])
                dump("dbg_z", u_h)
                dump("dbg_asl", a_sl[:].rearrange("p a q -> p (a q)"))
            rs_sems.append(out_proj(c))
            if DBG and c == 0:
                p.op("sync", lambda e: e.dma_start(
                    out=dbg["dbg_aq"], in_=attn_q[0]),
                    inc="ld", inck=16, wait=((rs_sems[0], 1),))
        ffn_weight_quant()
        q1cc = ffn_quant1(rs_sems)
        ffn_main(q1cc)
        p.op("gpsimd", lambda e: e.engine_nop(), wait=("gp", "ld"))

        sem_names = ["ld", "gp", "te", "dve", "act", "gpc"] + cc_names
        with ExitStack() as sctx:
            H = {sn: sctx.enter_context(nc.semaphore(sn)) for sn in sem_names}
            with nc.Block() as block:
                @block.sync
                def _(e):
                    for fn in p.seq["sync"]:
                        fn(e, H)

                @block.scalar
                def _(e):
                    for fn in p.seq["scalar"]:
                        fn(e, H)

                @block.vector
                def _(e):
                    for fn in p.seq["vector"]:
                        fn(e, H)

                @block.tensor
                def _(e):
                    for fn in p.seq["tensor"]:
                        fn(e, H)

                @block.gpsimd
                def _(e):
                    for fn in p.seq["gpsimd"]:
                        fn(e, H)
    return nc


def make_inputs(core, L, Q, ins):
    import ml_dtypes
    g, j = core // 4, core % 4
    NCH = L // Q
    NQ4 = Q // 4
    sl = slice(j * DSL, (j + 1) * DSL)
    x = np.ascontiguousarray(np.asarray(ins["L0_attn_x"])[g])
    x_own = np.concatenate(
        [x[c * Q + j * NQ4: c * Q + (j + 1) * NQ4] for c in range(NCH)],
        axis=0)
    W = np.asarray(ins["L0_attn_in_proj_w"])
    d = {
        "x_b": x,
        "x_own": x_own,
        "w_in_xs": W.reshape(8, 128, 2 * DI)[:, :, sl].transpose(1, 0, 2),
        "w_in_z": W.reshape(8, 128, 2 * DI)
        [:, :, DI + j * DSL: DI + (j + 1) * DSL].transpose(1, 0, 2),
        "w_xp": np.asarray(ins["L0_attn_x_proj_w"])[sl]
        .reshape(NDB, 128, 129).transpose(1, 0, 2),
        "conv_w4": np.asarray(ins["L0_attn_conv_w"])[sl, 0, :]
        .reshape(NDB, 128, KC).transpose(1, 0, 2),
        "conv_b4": np.asarray(ins["L0_attn_conv_b"])[sl].reshape(NDB, 128).T,
        "dtw4": np.asarray(ins["L0_attn_dt_w"])[0, sl].reshape(NDB, 128).T,
        "dtb4": np.asarray(ins["L0_attn_dt_b"])[sl].reshape(NDB, 128).T,
        "alog4": np.asarray(ins["L0_attn_A_log"])[sl]
        .reshape(NDB, 128, DS).transpose(1, 0, 2),
        "d4": np.asarray(ins["L0_attn_D"])[sl].reshape(NDB, 128).T,
        "w_out_sl": np.asarray(ins["L0_attn_out_proj_w"])[sl]
        .reshape(NDB, 128, 8, 128).transpose(1, 0, 2, 3),
        "wg_t": np.asarray(ins["L0_ffn_w_gate"]).T
        .reshape(8, 128, 32, 128).transpose(1, 2, 0, 3),
        "wu_t": np.asarray(ins["L0_ffn_w_up"]).T
        .reshape(8, 128, 32, 128).transpose(1, 2, 0, 3),
        "wd_t": np.asarray(ins["L0_ffn_w_down"]).T
        .reshape(32, 128, 8, 128).transpose(1, 2, 0, 3),
        "c_row": np.asarray(ins["L0_c"])[g:g + 1],
        "wmod_r": np.asarray(ins["L0_wmod"]).reshape(8, 128, 6 * D)
        .transpose(1, 0, 2),
        "ident_h": np.eye(128, dtype=ml_dtypes.bfloat16),
    }
    return {k: np.ascontiguousarray(np.asarray(v, dtype=np.float32))
            if k != "ident_h" else v for k, v in d.items()}


def assemble(L, Q, results):
    NCH = L // Q
    NQ4 = Q // 4
    y = np.empty((2, L, D), np.float32)
    for core in range(8):
        g, j = core // 4, core % 4
        yo = results[core]["y_out"]
        for c in range(NCH):
            y[g, c * Q + j * NQ4: c * Q + (j + 1) * NQ4] = \
                yo[c * NQ4:(c + 1) * NQ4]
    return y


def kernel(**inputs):
    L, Q = 8192, 1024
    nc = build_nc(L, Q)
    in_maps = [make_inputs(c, L, Q, inputs) for c in range(8)]
    res = run_bass_kernel_spmd(nc, in_maps, core_ids=list(range(8)))
    return assemble(L, Q, res.results)


# version 07
# speedup vs baseline: 1.3706x; 1.0130x over previous
import numpy as np
from contextlib import ExitStack

import concourse.bass as bass
import concourse.mybir as mybir
from concourse.bass_utils import run_bass_kernel_spmd

F32 = mybir.dt.float32
BF16 = mybir.dt.bfloat16
AF = mybir.ActivationFunctionType
ALU = mybir.AluOpType
AX = mybir.AxisListType

MAGIC = float(np.float32(1.5 * 2.0 ** 23))
LN_EPS = 1e-5
D = 1024
DI = 2048
DSL = 512
NDB = 4
DS = 64
KC = 4
G2 = [[0, 1, 2, 3], [4, 5, 6, 7]]
G1 = [[c] for c in range(8)]

ENGS = ("sync", "scalar", "vector", "tensor", "gpsimd")


class Plan:
    def __init__(self):
        self.seq = {e: [] for e in ENGS}
        self.cnt = {}

    def sem(self, name):
        self.cnt.setdefault(name, 0)

    SELF_SEM = {"vector": "dve", "scalar": "act"}

    def op(self, eng, build, inc=None, inck=1, wait=(), sw=True):
        waits = []
        if sw and eng in self.SELF_SEM:
            sn = self.SELF_SEM[eng]
            waits.append((sn, self.cnt[sn]))
        for w in wait:
            if isinstance(w, tuple):
                waits.append(w)
            else:
                waits.append((w, self.cnt[w]))
        incs = None
        if inc is not None:
            self.cnt[inc] += inck
            incs = (inc, inck)

        def fn(e, H, waits=tuple(waits), build=build, incs=incs):
            for s, v in waits:
                if v > 0:
                    e.wait_ge(H[s], v)
            ins = build(e)
            if incs is not None:
                ins.then_inc(H[incs[0]], incs[1])
        self.seq[eng].append(fn)


DBG = False


def build_nc(L, Q):
    NCH = L // Q
    QT = Q // 128
    NQ4 = Q // 4
    OWN = NCH * NQ4
    NB4 = OWN // 512
    assert L % Q == 0 and Q == 1024 and NQ4 % 128 == 0 and OWN % 512 == 0

    nc = bass.Bass("TRN2", target_bir_lowering=False, debug=False,
                   detect_race_conditions=False)

    def din(name, shape, dt=F32):
        return nc.dram_tensor(name, shape, dt, kind="ExternalInput").ap()

    x_b = din("x_b", [L, D])
    x_own = din("x_own", [OWN, D])
    w_in_xs = din("w_in_xs", [128, 8, DSL])
    w_in_z = din("w_in_z", [128, 8, DSL])
    w_xp = din("w_xp", [128, NDB, 129])
    conv_w4 = din("conv_w4", [128, NDB, KC])
    conv_b4 = din("conv_b4", [128, NDB])
    dtw4 = din("dtw4", [128, NDB])
    dtb4 = din("dtb4", [128, NDB])
    alog4 = din("alog4", [128, NDB, DS])
    d4 = din("d4", [128, NDB])
    w_out_sl = din("w_out_sl", [128, NDB, 8, 128])
    wg_t = din("wg_t", [128, 32, 8, 128])
    wu_t = din("wu_t", [128, 32, 8, 128])
    wd_t = din("wd_t", [128, 8, 32, 128])
    c_row = din("c_row", [1, D])
    wmod_r = din("wmod_r", [128, 8, 6 * D])
    ident_h = din("ident_h", [128, 128], BF16)

    y_out = nc.dram_tensor("y_out", [OWN, D], F32, kind="ExternalOutput").ap()
    dbg = {}
    if DBG:
        for nm, shape, dt in [
                ("dbg_hT", [128, 8 * Q], BF16), ("dbg_xsr", [128, NDB * (KC + Q)], BF16),
                ("dbg_xssi", [128, NDB * Q], BF16), ("dbg_bct", [128, Q], F32),
                ("dbg_dtr", [1, Q], F32), ("dbg_dt", [128, NDB * Q], F32),
                ("dbg_u", [128, NDB * Q], BF16), ("dbg_y", [128, NDB * Q], F32),
                ("dbg_yz", [128, NDB * Q], BF16), ("dbg_carry", [128, NDB * DS], F32),
                ("dbg_ca1", [128, D], BF16), ("dbg_sa", [128, D], BF16),
                ("dbg_ga", [128, D], BF16), ("dbg_aq", [1024, NQ4], F32),
                ("dbg_z", [128, NDB * Q], BF16), ("dbg_asl", [128, NDB * DS], F32),
                ("dbg_xt", [128, D], F32), ("dbg_cols", [128, 6], F32),
                ("dbg_hth", [128, D], BF16), ("dbg_hth2", [128, D], BF16)]:
            dbg[nm] = nc.dram_tensor(nm, shape, dt, kind="ExternalOutput").ap()

    dscr = lambda n, s, dt=F32: nc.dram_tensor(n, s, dt).ap()
    xp_part = [dscr(f"xp_part{c}", [129, Q]) for c in range(NCH)]
    xp_red = [dscr(f"xp_red{c}", [129, Q]) for c in range(NCH)]
    po_d = [dscr(f"po_d{c}", [4 * 1024, NQ4]) for c in range(NCH)]
    attn_q = [dscr(f"attn_q{c}", [1024, NQ4]) for c in range(NCH)]
    wq_gu_d = dscr("wq_gu_d", [128, 32, 16, 128], BF16)
    wq_gu_f = wq_gu_d.rearrange("p a b c -> p (a b) c")
    wq_gu_3 = wq_gu_d.rearrange("p a b c -> p a (b c)")
    wq_d_d = dscr("wq_d_d", [128, 8, 32, 128], BF16)
    x2_d = dscr("x2_d", [OWN, D])
    q1l_in = dscr("q1l_in", [OWN, D], BF16)
    q1l_out = dscr("q1l_out", [OWN, D], BF16)
    q2l_in = [dscr(f"q2l_in{b}", [4096, 512], BF16) for b in range(NB4)]
    q2l_out = [dscr(f"q2l_out{b}", [4096, 512], BF16) for b in range(NB4)]
    swb_d = dscr("swb_d", [1, 4])

    p = Plan()
    for s in ("ld", "ld2", "gp", "te", "dve", "act", "gpc"):
        p.sem(s)
    cc_names = []

    def new_cc():
        n = f"cc{len(cc_names)}"
        cc_names.append(n)
        p.sem(n)
        return n

    with ExitStack() as ctx:
        sb = lambda n, s, dt=F32: ctx.enter_context(nc.sbuf_tensor(n, s, dt))
        psum = lambda n, s, dt=F32: ctx.enter_context(nc.psum_tensor(n, s, dt))

        wxs_h = sb("wxs_h", [128, 8, DSL], BF16)
        wz_h = sb("wz_h", [128, 8, DSL], BF16)
        wxp_h = sb("wxp_h", [128, NDB, 129], BF16)
        wout_h = sb("wout_h", [128, NDB, 8, 128], BF16)
        convw = sb("convw", [128, NDB, KC])
        convb = sb("convb", [128, NDB])
        dtw = sb("dtw", [128, NDB])
        dtb = sb("dtb", [128, NDB])
        a_sl = sb("a_sl", [128, NDB, DS])
        d_col = sb("d_col", [128, NDB])
        ident = sb("ident", [128, 128], BF16)
        ident_f = sb("ident_f", [128, 128])
        ones_f = sb("ones_f", [1, 128])
        onesc_f = sb("onesc_f", [128, 1])
        reps = {}
        for nmm in ("ca1", "sa", "ga", "cb1", "sbm", "gb"):
            reps[nmm] = sb("rep_" + nmm, [128, D], BF16)
        carry = sb("carry", [128, NDB, DS])
        halo = sb("halo", [128, NDB, KC - 1], BF16)
        silc_f = sb("silc_f", [128, 8])
        sw3 = sb("sw3", [128, 4])
        swq = sb("swq", [128, 1])
        cols = [sb(f"col{i}", [128, 1]) for i in range(6)]
        rows4 = sb("rows4", [1, 16])
        cst_eps = sb("cst_eps", [128, 1])
        cst_neg1 = sb("cst_neg1", [128, 1])
        cst_one = sb("cst_one", [128, 1])

        big_a = sb("big_a", [128, NDB * Q])
        big_b = sb("big_b", [128, NDB * Q])
        big_c = sb("big_c", [128, NDB * Q])
        big_d = sb("big_d", [128, NDB * Q])
        xt = sb("xt", [128, D])
        bct = sb("bct", [128, Q])
        dtr_t = sb("dtr_t", [1, Q])
        hT = sb("hT", [128, 8 * Q], BF16)
        NXS = NDB * (KC + Q)
        bfB = sb("bfB", [128, NXS + 4 * NDB * Q + 9 * Q], BF16)
        o = 0
        def carve(n):
            nonlocal o
            v = bfB[:, o:o + n]
            o += n
            return v
        xsr = carve(NXS).rearrange("p (a q) -> p a q", a=NDB)
        xssi = carve(NDB * Q)
        z_h = carve(NDB * Q)
        u_h = carve(NDB * Q)
        inj = carve(NDB * Q)
        bb = [carve(Q) for i in range(4)]
        cbb = [carve(Q) for i in range(4)]
        bc_h = carve(Q)
        ht_h = sb("ht_h", [128, D], BF16)
        ht_h2 = sb("ht_h2", [128, D], BF16)
        gu16 = bfB[:, 0:32 * 512]
        wstr16 = bfB[:, 32 * 512:32 * 512 + 32 * 128]
        gtl = bfB[:, 20480:20992]
        gtl2 = bfB[:, 20992:21504]
        utl = bfB[:, 21504:22016]
        rmx = bfB[:, 22016:22528]
        rmn = bfB[:, 22528:23040]
        ps_all = psum("ps_all", [128, 4096])
        ps_a = ps_all[:, 0:512]
        ps_b = ps_all[:, 512:1024]
        ps_r = ps_all[:, 1024:1536]
        ps_t = ps_all[:, 1536:1664]
        ps_t4 = ps_all[:, 2048:2176]

        dt_t = big_a[:].rearrange("p (a q) -> p a q", a=NDB)
        tmp_a = big_b[:].rearrange("p (a q) -> p a q", a=NDB)
        dAv = big_c[:].rearrange("p (a q) -> p a q", a=NDB)
        yacc = big_d[:].rearrange("p (a q) -> p a q", a=NDB)
        xssi3 = xssi.rearrange("p (a q) -> p a q", a=NDB)
        z3 = z_h.rearrange("p (a q) -> p a q", a=NDB)
        u3 = u_h.rearrange("p (a q) -> p a q", a=NDB)
        inj3 = inj.rearrange("p (a q) -> p a q", a=NDB)
        hT3 = hT[:].rearrange("p (k q) -> p k q", k=8)
        gu3 = gu16.rearrange("p (m t) -> p m t", m=32)
        wst3 = wstr16.rearrange("p (k m) -> p k m", m=128)
        wstB3 = bfB[:, 23040:27136].rearrange("p (k m) -> p k m", m=128)
        ffn8 = big_d[:].rearrange("p (m t) -> p m t", m=8)

        def bcast3(t, n):
            return t.unsqueeze(1).broadcast_to([128, n, t.shape[-1]])

        def setup():
            p.op("vector", lambda e: e.memset(ones_f[:], 1.0), inc="dve")
            p.op("vector", lambda e: e.memset(cst_eps[:], LN_EPS), inc="dve")
            p.op("vector", lambda e: e.memset(cst_neg1[:], -1.0), inc="dve")
            p.op("vector", lambda e: e.memset(cst_one[:], 1.0), inc="dve")
            p.op("vector", lambda e: e.memset(onesc_f[:], 1.0), inc="dve")
            p.op("vector", lambda e: e.memset(carry[:], 0.0), inc="dve")
            p.op("vector", lambda e: e.memset(xsr[:, :, 0:KC], 0.0), inc="dve")
            p.op("vector", lambda e: e.memset(halo[:], 0.0), inc="dve")
            for dst, src in [(convw[:], conv_w4), (convb[:], conv_b4),
                             (dtw[:], dtw4), (dtb[:], dtb4), (d_col[:], d4),
                             (ident[:], ident_h)]:
                p.op("sync", lambda e, d=dst, s=src: e.dma_start(out=d, in_=s),
                     inc="ld", inck=16)

            def stage_cast(dst_h, src, nelem):
                st32 = big_a[:, :nelem]
                p.op("sync", lambda e, s=src, dd=st32: e.dma_start(out=dd, in_=s),
                     inc="ld", inck=16, wait=("dve",))
                p.op("vector", lambda e, dd=dst_h, s=st32: e.tensor_copy(
                    out=dd, in_=s), inc="dve", wait=("ld",))
            stage_cast(wxs_h[:].rearrange("p a b -> p (a b)"), w_in_xs, 8 * DSL)
            stage_cast(wz_h[:].rearrange("p a b -> p (a b)"), w_in_z, 8 * DSL)
            stage_cast(wxp_h[:].rearrange("p a b -> p (a b)"), w_xp, NDB * 129)
            stage_cast(wout_h[:].rearrange("p a b c -> p (a b c)"), w_out_sl,
                       NDB * 8 * 128)
            p.op("vector", lambda e: e.tensor_copy(out=ident_f[:],
                                                   in_=ident[:]),
                 inc="dve", wait=("ld",))
            asl_f = a_sl[:].rearrange("p a b -> p (a b)")
            p.op("sync", lambda e: e.dma_start(out=asl_f, in_=alog4),
                 inc="ld", inck=16, wait=("dve",))
            p.op("scalar", lambda e: e.activation(
                out=big_a[:, :NDB * DS], in_=asl_f, func=AF.Exp),
                inc="act", wait=("ld",))
            p.op("vector", lambda e: e.tensor_scalar_mul(
                out=asl_f, in0=big_a[:, :NDB * DS], scalar1=-1.0),
                inc="dve", wait=("act",))

            crow_t = xt[0:1, :]
            p.op("sync", lambda e: e.dma_start(out=crow_t, in_=c_row),
                 inc="ld", inck=16)
            csil = bct[0:1, :D]
            p.op("scalar", lambda e: e.activation(out=csil, in_=crow_t,
                                                  func=AF.Silu),
                 inc="act", wait=("ld",))
            for kc in range(8):
                p.op("tensor", lambda e, kc=kc: e.transpose(
                    out=ps_t[:, 0:1], in_=csil[:, kc * 128:(kc + 1) * 128],
                    identity=ident_f[0:1, 0:1]), inc="te",
                    wait=("dve", "ld", "act"))
                p.op("vector", lambda e, kc=kc: e.tensor_copy(
                    out=silc_f[:, kc:kc + 1], in_=ps_t[:, 0:1]),
                    inc="dve", wait=("te",))
            sa_r = big_a[0:1, 0 * D:1 * D]; ca_r = big_a[0:1, 1 * D:2 * D]
            ga_r = big_a[0:1, 2 * D:3 * D]; sb_r = big_a[0:1, 3 * D:4 * D]
            cb_r = big_c[0:1, 0 * D:1 * D]; gb_r = big_c[0:1, 1 * D:2 * D]
            segs = [sa_r, ca_r, ga_r, sb_r, cb_r, gb_r]
            for nch in range(12):
                st32 = big_b[:, :8 * 512]
                p.op("sync", lambda e, n=nch, dd=st32: e.dma_start(
                    out=dd.rearrange("p (k c) -> p k c", k=8),
                    in_=wmod_r[:, :, n * 512:(n + 1) * 512]),
                    inc="ld", inck=16, wait=("dve", "te"))
                for kc in range(8):
                    p.op("tensor", lambda e, kc=kc, s=st32: e.matmul(
                        out=ps_r[0:1, :], lhsT=silc_f[:, kc:kc + 1],
                        rhs=s.rearrange("p (k c) -> p k c", k=8)[:, kc, :],
                        start=(kc == 0), stop=(kc == 7)),
                        inc="te" if kc == 7 else None,
                        wait=("dve", "ld") if kc == 0 else ())
                tgt = segs[nch // 2][:, (nch % 2) * 512:(nch % 2) * 512 + 512]
                p.op("vector", lambda e, t=tgt: e.tensor_copy(
                    out=t, in_=ps_r[0:1, :]), inc="dve", wait=("te",))
            t1 = big_c[0:1, 2 * D:3 * D]
            t2 = big_c[0:1, 3 * D:4 * D]
            for g_r in (ga_r, gb_r):
                p.op("vector", lambda e, g=g_r: e.tensor_scalar(
                    out=t1, in0=g, scalar1=0.0, scalar2=2.0,
                    op0=ALU.is_ge, op1=ALU.mult), inc="dve", wait=("te",))
                p.op("vector", lambda e: e.tensor_single_scalar(
                    out=t1, in_=t1, scalar=1.0, op=ALU.subtract), inc="dve")
                p.op("vector", lambda e, g=g_r: e.tensor_scalar_mul(
                    out=t2, in0=g, scalar1=-1.0), inc="dve")
                p.op("vector", lambda e, g=g_r: e.tensor_tensor(
                    out=t2, in0=t2, in1=g, op=ALU.max), inc="dve")
                p.op("vector", lambda e: e.tensor_single_scalar(
                    out=t2, in_=t2, scalar=0.25, op=ALU.max), inc="dve")
                p.op("vector", lambda e, g=g_r: e.tensor_tensor(
                    out=g, in0=t1, in1=t2, op=ALU.mult), inc="dve")
            for c_r in (ca_r, cb_r):
                p.op("vector", lambda e, cr=c_r: e.tensor_single_scalar(
                    out=cr, in_=cr, scalar=1.0, op=ALU.add), inc="dve")
            for nmm, r in (("sa", sa_r), ("ca1", ca_r), ("ga", ga_r),
                           ("sbm", sb_r), ("cb1", cb_r), ("gb", gb_r)):
                for h in range(2):
                    p.op("tensor", lambda e, r=r, h=h: e.matmul(
                        out=ps_r, lhsT=ones_f[:],
                        rhs=r[:, h * 512:(h + 1) * 512], start=True, stop=True),
                        inc="te", wait=("dve",))
                    p.op("vector", lambda e, nm=nmm, h=h: e.tensor_copy(
                        out=reps[nm][:, h * 512:(h + 1) * 512], in_=ps_r),
                        inc="dve", wait=("te",))

        def ln1(c):
            for tt in range(QT):
                r0 = c * Q + tt * 128
                p.op("sync", lambda e, r0=r0: e.dma_start(
                    out=xt[:], in_=x_b[r0:r0 + 128, :]),
                    inc="ld", inck=16, wait=("dve", "te"))
                c0, c1, c2, c3, c4, c5 = cols
                p.op("vector", lambda e: e.reduce_sum(
                    c0[:], xt[:], axis=AX.X), inc="dve", wait=("ld",))
                p.op("vector", lambda e: e.tensor_scalar_mul(
                    out=c1[:], in0=c0[:], scalar1=1.0 / D), inc="dve")
                p.op("vector", lambda e: e.tensor_tensor(
                    out=big_c[:, :D], in0=xt[:], in1=xt[:], op=ALU.mult),
                    inc="dve", wait=("gp",))
                p.op("vector", lambda e: e.reduce_sum(
                    c2[:], big_c[:, :D], axis=AX.X), inc="dve")
                p.op("vector", lambda e: e.tensor_scalar_mul(
                    out=c3[:], in0=c2[:], scalar1=1.0 / D), inc="dve")
                p.op("vector", lambda e: e.tensor_tensor(
                    out=c4[:], in0=c1[:], in1=c1[:], op=ALU.mult), inc="dve")
                p.op("vector", lambda e: e.tensor_tensor(
                    out=c5[:], in0=c3[:], in1=c4[:], op=ALU.subtract),
                    inc="dve")
                p.op("scalar", lambda e: e.activation(
                    out=c0[:], in_=c5[:], func=AF.Sqrt, bias=cst_eps[:]),
                    inc="act", wait=("dve",))
                p.op("vector", lambda e: e.reciprocal(c2[:], c0[:]),
                     inc="dve", wait=("act",))
                hf32 = big_d[:, 0:D]
                p.op("vector", lambda e, h=hf32: e.tensor_scalar(
                    out=h, in0=xt[:], scalar1=c1[:], scalar2=c2[:],
                    op0=ALU.subtract, op1=ALU.mult), inc="dve")
                p.op("vector", lambda e, h=hf32: e.tensor_tensor(
                    out=h, in0=h, in1=reps["ca1"][:],
                    op=ALU.mult), inc="dve")
                p.op("vector", lambda e, h=hf32: e.tensor_tensor(
                    out=h, in0=h, in1=reps["sa"][:], op=ALU.add),
                    inc="dve")
                for kc in range(8):
                    p.op("tensor", lambda e, kc=kc, h=hf32: e.transpose(
                        out=ps_t[:, 0:128], in_=h[:, kc * 128:(kc + 1) * 128],
                        identity=ident_f[:]), inc="te", wait=("dve",))
                    p.op("vector", lambda e, kc=kc, tt=tt: e.tensor_copy(
                        out=hT3[:, kc, tt * 128:(tt + 1) * 128],
                        in_=ps_t[:, 0:128]),
                        inc="dve", wait=("te",))

        def in_proj_conv(c):
            for wsel, wtile in ((0, wxs_h), (1, wz_h)):
                for db in range(NDB):
                    for hf in range(2):
                        psd = ps_a if (db + hf) % 2 == 0 else ps_b
                        for kc in range(8):
                            p.op("tensor",
                                 lambda e, db=db, hf=hf, kc=kc, ps=psd, w=wtile:
                                 e.matmul(
                                     out=ps[:],
                                     lhsT=w[:, kc, db * 128:(db + 1) * 128],
                                     rhs=hT3[:, kc, hf * 512:(hf + 1) * 512],
                                     start=(kc == 0), stop=(kc == 7)),
                                 inc="te" if kc == 7 else None,
                                 wait=("dve",) if kc == 0 else ())
                        if wsel == 0:
                            p.op("vector",
                                 lambda e, db=db, hf=hf, ps=psd: e.tensor_copy(
                                     out=xsr[:, db,
                                             KC + hf * 512:KC + (hf + 1) * 512],
                                     in_=ps[:]), inc="dve", wait=("te",))
                        else:
                            p.op("vector",
                                 lambda e, db=db, hf=hf, ps=psd: e.tensor_copy(
                                     out=z3[:, db, hf * 512:(hf + 1) * 512],
                                     in_=ps[:]), inc="dve", wait=("te",))
            p.op("vector", lambda e: e.tensor_copy(
                out=xsr[:, :, 1:KC], in_=halo[:]), inc="dve")
            p.op("vector", lambda e: e.tensor_copy(
                out=halo[:], in_=xsr[:, :, KC + Q - (KC - 1):KC + Q]),
                inc="dve")
            for db in range(NDB):
                p.op("vector", lambda e, db=db: e.tensor_scalar(
                    out=tmp_a[:, db, :], in0=xsr[:, db, 1:1 + Q],
                    scalar1=convw[:, db, 0:1], scalar2=None, op0=ALU.mult),
                    inc="dve", wait=("gp",) if db == 0 else ())
                for k in range(1, KC):
                    p.op("vector", lambda e, db=db, k=k: e.scalar_tensor_tensor(
                        out=tmp_a[:, db, :], in0=xsr[:, db, 1 + k:1 + k + Q],
                        scalar=convw[:, db, k:k + 1], in1=tmp_a[:, db, :],
                        op0=ALU.mult, op1=ALU.add), inc="dve")
                p.op("scalar", lambda e, db=db: e.activation(
                    out=xssi3[:, db, :], in_=tmp_a[:, db, :], func=AF.Silu,
                    bias=convb[:, db:db + 1]), inc="act", wait=("dve",))

        def x_proj_reduce(c):
            for hf in range(2):
                for db in range(NDB):
                    p.op("tensor", lambda e, db=db, hf=hf: e.matmul(
                        out=ps_a, lhsT=wxp_h[:, db, 0:128],
                        rhs=xssi3[:, db, hf * 512:(hf + 1) * 512],
                        start=(db == 0), stop=(db == NDB - 1)),
                        inc="te" if db == NDB - 1 else None,
                        wait=("act", "dve") if db == 0 else ())
                p.op("vector", lambda e, hf=hf: e.tensor_copy(
                    out=bct[:, hf * 512:(hf + 1) * 512], in_=ps_a),
                    inc="dve", wait=("te",))
                for db in range(NDB):
                    p.op("tensor", lambda e, db=db, hf=hf: e.matmul(
                        out=ps_b[0:1, :], lhsT=wxp_h[:, db, 128:129],
                        rhs=xssi3[:, db, hf * 512:(hf + 1) * 512],
                        start=(db == 0), stop=(db == NDB - 1)),
                        inc="te" if db == NDB - 1 else None)
                p.op("vector", lambda e, hf=hf: e.tensor_copy(
                    out=dtr_t[:, hf * 512:(hf + 1) * 512], in_=ps_b[0:1, :]),
                    inc="dve", wait=("te",))
            p.op("gpsimd", lambda e, c=c: e.dma_start(
                out=xp_part[c][0:128, :], in_=bct[:]), inc="gp", inck=16,
                wait=("dve",))
            p.op("gpsimd", lambda e, c=c: e.dma_start(
                out=xp_part[c][128:129, :], in_=dtr_t[:]),
                inc="gp", inck=16)
            ccn = new_cc()
            p.op("gpsimd", lambda e, c=c: e.collective_compute(
                "AllReduce", ALU.add, replica_groups=G2,
                ins=[xp_part[c]], outs=[xp_red[c]]), inc=ccn, wait=("gp",))
            p.op("sync", lambda e, c=c: e.dma_start(
                out=bct[:], in_=xp_red[c][0:128, :]), inc="ld", inck=16,
                wait=((ccn, 1), "dve"))
            p.op("sync", lambda e, c=c: e.dma_start(
                out=dtr_t[:], in_=xp_red[c][128:129, :]),
                inc="ld", inck=16, wait=("te",))
            p.op("vector", lambda e: e.tensor_copy(out=bc_h, in_=bct[:]),
                 inc="dve", wait=("ld", "gp"))

        def dt_u_build(c):
            for hf in range(2):
                p.op("tensor", lambda e, hf=hf: e.matmul(
                    out=ps_r, lhsT=ones_f[:],
                    rhs=dtr_t[:, hf * 512:(hf + 1) * 512], start=True,
                    stop=True), inc="te", wait=("dve", "ld"))
                for db in range(NDB):
                    p.op("vector", lambda e, db=db, hf=hf: e.tensor_scalar(
                        out=tmp_a[:, db, hf * 512:(hf + 1) * 512], in0=ps_r,
                        scalar1=dtw[:, db:db + 1], scalar2=dtb[:, db:db + 1],
                        op0=ALU.mult, op1=ALU.add), inc="dve",
                        wait=("te", "gp") if db == 0 else ("te",))
            p.op("vector", lambda e: e.tensor_scalar(
                out=dt_t[:], in0=tmp_a[:], scalar1=0.0, scalar2=None,
                op0=ALU.max), inc="dve", sw=False, wait=("gp",))
            p.op("scalar", lambda e: e.activation(
                out=dAv[:], in_=tmp_a[:], func=AF.Abs), inc="act",
                wait=("dve",))
            p.op("scalar", lambda e: e.activation(
                out=tmp_a[:], in_=dAv[:], func=AF.Exp, scale=cst_neg1[:]),
                inc="act")
            p.op("scalar", lambda e: e.activation(
                out=dAv[:], in_=tmp_a[:], func=AF.Ln, bias=cst_one[:]), inc="act")
            p.op("vector", lambda e: e.tensor_tensor(
                out=dt_t[:], in0=dt_t[:], in1=dAv[:], op=ALU.add),
                inc="dve", sw=False, wait=("act",))
            p.op("vector", lambda e: e.tensor_tensor(
                out=u_h, in0=big_a[:], in1=xssi, op=ALU.mult),
                inc="dve", sw=False)

        def scan(c):
            dabuf = [dAv, yacc]
            NB3 = 4
            gp_marks = {}
            dve_marks = {}

            def emit_bcast(n):
                nb = n % NB3
                p.op("gpsimd", lambda e, n=n, nb=nb: e.dma_start(
                    out=bb[nb],
                    in_=bc_h[n:n + 1, :].unsqueeze(1)
                    .broadcast_to([1, 128, Q])), inc="gp", inck=16,
                    wait=("dve",))
                p.op("gpsimd", lambda e, n=n, nb=nb: e.dma_start(
                    out=cbb[nb],
                    in_=bc_h[64 + n:65 + n, :].unsqueeze(1)
                    .broadcast_to([1, 128, Q])), inc="gp", inck=16)
                gp_marks[n] = p.cnt["gp"]
            emit_bcast(0)
            emit_bcast(1)
            emit_bcast(2)
            dve_marks[-2] = dve_marks[-1] = p.cnt["dve"]
            for n in range(DS):
                nb = n % NB3
                da = dabuf[n % 2]
                for db in range(NDB):
                    p.op("scalar", lambda e, db=db, n=n, da=da: e.activation(
                        out=da[:, db, :], in_=dt_t[:, db, :], func=AF.Exp,
                        scale=a_sl[:, db, n:n + 1]),
                        inc="act" if db == NDB - 1 else None,
                        wait=(("dve", dve_marks[n - 2]),) if db == 0 else ())
                if n + 3 < DS:
                    emit_bcast(n + 3)
                for db in range(NDB):
                    p.op("vector", lambda e, db=db, nb=nb: e.tensor_tensor(
                        out=inj3[:, db, :], in0=u3[:, db, :], in1=bb[nb],
                        op=ALU.mult), inc="dve", sw=False,
                        wait=(("act", p.cnt["act"]), ("gp", gp_marks[n]))
                        if db == 0 else ())
                for db in range(NDB):
                    p.op("vector", lambda e, db=db, n=n, da=da:
                         e.tensor_tensor_scan(
                             out=da[:, db, :], data0=da[:, db, :],
                             data1=inj3[:, db, :],
                             initial=carry[:, db, n:n + 1],
                             op0=ALU.mult, op1=ALU.add), inc="dve", sw=False)
                p.op("vector", lambda e, n=n, da=da: e.tensor_copy(
                    out=carry[:, :, n:n + 1], in_=da[:, :, Q - 1:Q]),
                    inc="dve")
                p.op("vector", lambda e, nb=nb, da=da: e.tensor_tensor(
                    out=tmp_a[:], in0=da[:],
                    in1=cbb[nb].unsqueeze(1).broadcast_to([128, NDB, Q]),
                    op=ALU.mult), inc="dve", sw=False, wait=("te",))
                for db in range(NDB):
                    for tb in range(8):
                        blk = (db * 8 + tb) * 128
                        p.op("tensor",
                             lambda e, db=db, tb=tb, blk=blk, n=n: e.matmul(
                                 out=ps_all[:, blk:blk + 128],
                                 lhsT=tmp_a[:, db, tb * 128:(tb + 1) * 128],
                                 rhs=ident_f[:], is_transpose=True,
                                 start=(n == 0 and tb % 4 == 0),
                                 stop=(n == DS - 1 and tb % 4 == 3)),
                             inc="te" if (db == NDB - 1 and tb == 7) else None,
                             wait=("dve",) if (db == 0 and tb == 0) else ())
                dve_marks[n] = p.cnt["dve"]
            p.op("vector", lambda e: e.tensor_copy(
                out=big_c[:], in_=ps_all[:]), inc="dve", wait=("te",))
            for db in range(NDB):
                for tb in range(8):
                    blk = (db * 8 + tb) * 128
                    p.op("tensor", lambda e, blk=blk, tb=tb: e.matmul(
                        out=ps_all[:, blk:blk + 128],
                        lhsT=big_c[:, blk:blk + 128],
                        rhs=ident_f[:], is_transpose=True,
                        start=(tb % 4 == 0), stop=(tb % 4 == 3)),
                        inc="te" if blk == (NDB * 8 - 1) * 128 else None,
                        wait=("dve",) if blk == 0 else ())
            p.op("vector", lambda e: e.tensor_copy(
                out=big_d[:], in_=ps_all[:]), inc="dve", wait=("te",))
            for db in range(NDB):
                p.op("vector", lambda e, db=db: e.scalar_tensor_tensor(
                    out=yacc[:, db, :], in0=xssi3[:, db, :],
                    scalar=d_col[:, db:db + 1], in1=yacc[:, db, :],
                    op0=ALU.mult, op1=ALU.add), inc="dve")
            p.op("scalar", lambda e: e.activation(
                out=u_h, in_=z_h, func=AF.Silu), inc="act",
                wait=("dve",))
            p.op("vector", lambda e: e.tensor_tensor(
                out=z_h, in0=big_d[:], in1=u_h, op=ALU.mult),
                inc="dve", wait=("act",))

        def out_proj(c):
            po_lo = dt_t
            po_hi = tmp_a
            for mt in range(8):
                dst = po_lo if mt < 4 else po_hi
                for hf in range(2):
                    psd = ps_a if (mt + hf) % 2 == 0 else ps_b
                    for db in range(NDB):
                        p.op("tensor", lambda e, mt=mt, hf=hf, db=db, ps=psd:
                             e.matmul(out=ps[:],
                                      lhsT=wout_h[:, db, mt, :],
                                      rhs=z3[:, db, hf * 512:(hf + 1) * 512],
                                      start=(db == 0), stop=(db == NDB - 1)),
                             inc="te" if db == NDB - 1 else None,
                             wait=("dve",) if db == 0 else ())
                    p.op("vector", lambda e, mt=mt, hf=hf, dd=dst, ps=psd:
                         e.tensor_copy(
                             out=dd[:, mt % 4, hf * 512:(hf + 1) * 512],
                             in_=ps[:]), inc="dve",
                         wait=("te", "ld") if (DBG and mt == 0 and hf == 0)
                         else ("te",))
            for sq in range(4):
                p.op("gpsimd", lambda e, c=c, sq=sq: e.dma_start(
                    out=po_d[c][sq * 1024:sq * 1024 + 512, :]
                    .rearrange("(a p) q -> p a q", p=128),
                    in_=po_lo[:, :, sq * NQ4:(sq + 1) * NQ4]),
                    inc="gp", inck=16, wait=("dve",) if sq == 0 else ())
                p.op("gpsimd", lambda e, c=c, sq=sq: e.dma_start(
                    out=po_d[c][sq * 1024 + 512:(sq + 1) * 1024, :]
                    .rearrange("(a p) q -> p a q", p=128),
                    in_=po_hi[:, :, sq * NQ4:(sq + 1) * NQ4]),
                    inc="gp", inck=16)
            ccn = new_cc()
            p.op("gpsimd", lambda e, c=c: e.collective_compute(
                "ReduceScatter", ALU.add, replica_groups=G2,
                ins=[po_d[c]], outs=[attn_q[c]]), inc=ccn, wait=("gp",))
            return ccn

        def ffn_weight_quant():
            dstget = {
                0: lambda ch: wq_gu_3[:, 4 * ch:4 * ch + 4, 0:1024],
                1: lambda ch: wq_gu_3[:, 4 * ch:4 * ch + 4, 1024:2048],
                2: lambda ch: wq_d_d.rearrange("p a b c -> p (a b c)")[
                    :, ch * 4096:(ch + 1) * 4096],
            }
            mats = [(wg_t, 0), (wu_t, 1), (wd_t, 2)]
            for src, mi in mats:
                srcf = src.rearrange("p a b c -> p (a b c)")
                acc, tmpc = cols[0], cols[1]
                p.op("vector", lambda e, a=acc: e.memset(a[:], 0.0), inc="dve")
                for ch in range(8):
                    st32 = big_a[:, :4096]
                    p.op("sync", lambda e, s=srcf, ch=ch, dd=st32: e.dma_start(
                        out=dd, in_=s[:, ch * 4096:(ch + 1) * 4096]),
                        inc="ld", inck=16, wait=("dve", "gp", "te"))
                    p.op("vector", lambda e, dd=st32, t=tmpc: e.reduce_sum(
                        t[:], dd, axis=AX.X, apply_absolute_value=True),
                        inc="dve", wait=("ld",))
                    p.op("vector", lambda e, a=acc, t=tmpc: e.tensor_tensor(
                        out=a[:], in0=a[:], in1=t[:], op=ALU.add), inc="dve")
                p.op("tensor", lambda e, a=acc: e.matmul(
                    out=ps_r[0:1, 0:1], lhsT=onesc_f[:], rhs=a[:],
                    start=True, stop=True), inc="te", wait=("dve",))
                p.op("vector", lambda e: e.tensor_scalar(
                    out=rows4[:, 0:1], in0=ps_r[0:1, 0:1],
                    scalar1=1.0 / (4096.0 * 1024.0), scalar2=1e-5,
                    op0=ALU.mult, op1=ALU.max), inc="dve", wait=("te",))
                p.op("vector", lambda e: e.reciprocal(
                    rows4[:, 1:2], rows4[:, 0:1]), inc="dve")
                p.op("gpsimd", lambda e: e.dma_start(
                    out=swb_d[0:1, 0:2], in_=rows4[:, 0:2]),
                    inc="gp", inck=16, wait=("dve",))
                p.op("sync", lambda e, mi=mi: e.dma_start(
                    out=sw3[:, mi:mi + 1],
                    in_=swb_d[0:1, 0:1].broadcast_to([128, 1])),
                    inc="ld", inck=16, wait=("gp",))
                p.op("sync", lambda e: e.dma_start(
                    out=swq[:], in_=swb_d[0:1, 1:2].broadcast_to([128, 1])),
                    inc="ld", inck=16)
                for ch in range(8):
                    st32 = big_a[:, :4096]
                    qv = big_b[:, :4096]
                    g2v = big_c[:, :4096]
                    st16 = wstr16[:, :4096]
                    p.op("sync", lambda e, s=srcf, ch=ch, dd=st32: e.dma_start(
                        out=dd, in_=s[:, ch * 4096:(ch + 1) * 4096]),
                        inc="ld", inck=16, wait=("dve",))
                    p.op("vector", lambda e, dd=st32, q=qv: e.tensor_scalar_mul(
                        out=q, in0=dd, scalar1=swq[:]), inc="dve",
                        wait=("ld", "gp"))
                    p.op("vector", lambda e, q=qv, g=g2v: e.tensor_single_scalar(
                        out=g, in_=q, scalar=-0.5, op=ALU.is_lt), inc="dve")
                    p.op("vector", lambda e, q=qv: e.tensor_single_scalar(
                        out=q, in_=q, scalar=0.5, op=ALU.is_gt), inc="dve")
                    p.op("vector", lambda e, q=qv, g=g2v: e.tensor_tensor(
                        out=q, in0=q, in1=g, op=ALU.subtract), inc="dve")
                    p.op("vector", lambda e, q=qv, dd=st16: e.tensor_copy(
                        out=dd, in_=q), inc="dve")
                    if mi == 2:
                        p.op("gpsimd", lambda e, ch=ch, s=st16: e.dma_start(
                            out=dstget[2](ch), in_=s),
                            inc="gp", inck=16, wait=("dve",))
                    else:
                        p.op("gpsimd", lambda e, ch=ch, s=st16, mi=mi:
                             e.dma_start(
                                 out=dstget[mi](ch),
                                 in_=s.rearrange("p (a q) -> p a q", a=4)),
                             inc="gp", inck=16, wait=("dve",))

        def ffn_quant1(rs_sems):
            ntile = 0
            dve0 = p.cnt["dve"]
            gp0 = p.cnt["gp"]
            pst_mark = {}
            xadd_mark = {}
            x2st_mark = {}
            ht_mark = 0
            aqte_mark = 0
            nt_per = NQ4 // 128
            for c in range(NCH):
                aq32 = big_b[:, :8 * NQ4].rearrange("p (k t) -> p k t", k=8)
                p.op("sync", lambda e, c=c, a=aq32: e.dma_start(
                    out=a, in_=attn_q[c].rearrange("(k p) t -> p k t", p=128)),
                    inc="ld", inck=16,
                    wait=((rs_sems[c], 1), ("te", aqte_mark),
                          ("dve", dve0), ("gp", gp0)))
                aq_mark = p.cnt["ld"]
                for tt in range(nt_per):
                    r0 = c * NQ4 + tt * 128
                    par = ntile % 2
                    hbuf = big_c[:, par * 2048:par * 2048 + D]
                    x2t = big_d[:, par * 2048:par * 2048 + D]
                    q = big_d[:, par * 2048 + D:par * 2048 + 2 * D]
                    xr = big_b[:, 2048 + par * 1024:2048 + (par + 1) * 1024]
                    p.op("sync", lambda e, r0=r0, xr=xr: e.dma_start(
                        out=xr, in_=x_own[r0:r0 + 128, :]),
                        inc="ld", inck=16,
                        wait=(("dve", max(xadd_mark.get(par, 0), dve0)),))
                    xr_mark = p.cnt["ld"]
                    for kc in range(8):
                        slot = kc % 2
                        pst = (ps_t, ps_t4)[slot]
                        w = [("dve", pst_mark.get(slot, 0))]
                        if kc == 0:
                            w.append(("ld", aq_mark))
                        p.op("tensor", lambda e, kc=kc, tt=tt, a=aq32,
                             pst=pst: e.transpose(
                                 out=pst,
                                 in_=a[:, kc, tt * 128:(tt + 1) * 128],
                                 identity=ident_f[:]),
                             inc="te", wait=tuple(w))
                        tmk = p.cnt["te"]
                        p.op("vector", lambda e, kc=kc, pst=pst,
                             hbuf=hbuf: e.tensor_copy(
                                 out=hbuf[:, kc * 128:(kc + 1) * 128],
                                 in_=pst), inc="dve", wait=(("te", tmk),))
                        pst_mark[slot] = p.cnt["dve"]
                    if tt == nt_per - 1:
                        aqte_mark = p.cnt["te"]
                    p.op("vector", lambda e, x2t=x2t, hbuf=hbuf:
                         e.tensor_tensor(
                             out=x2t, in0=hbuf, in1=reps["ga"][:],
                             op=ALU.mult), inc="dve",
                         wait=(("gp", x2st_mark.get(par, 0)),))
                    p.op("vector", lambda e, x2t=x2t, xr=xr: e.tensor_tensor(
                        out=x2t, in0=x2t, in1=xr, op=ALU.add), inc="dve",
                        wait=(("ld", xr_mark),))
                    xadd_mark[par] = p.cnt["dve"]
                    p.op("gpsimd", lambda e, r0=r0, x2t=x2t: e.dma_start(
                        out=x2_d[r0:r0 + 128, :], in_=x2t),
                        inc="gp", inck=16, wait=("dve",))
                    x2st_mark[par] = p.cnt["gp"]
                    c0, c1, c2, c3, c4, c5 = cols
                    h2 = hbuf
                    p.op("vector", lambda e, x2t=x2t: e.reduce_sum(
                        c0[:], x2t, axis=AX.X), inc="dve")
                    p.op("vector", lambda e: e.tensor_scalar_mul(
                        out=c1[:], in0=c0[:], scalar1=1.0 / D), inc="dve")
                    p.op("vector", lambda e, x2t=x2t, h2=h2: e.tensor_tensor(
                        out=h2, in0=x2t, in1=x2t, op=ALU.mult), inc="dve")
                    p.op("vector", lambda e, h2=h2: e.reduce_sum(
                        c2[:], h2, axis=AX.X), inc="dve")
                    p.op("vector", lambda e: e.tensor_scalar_mul(
                        out=c3[:], in0=c2[:], scalar1=1.0 / D), inc="dve")
                    p.op("vector", lambda e: e.tensor_tensor(
                        out=c4[:], in0=c1[:], in1=c1[:], op=ALU.mult),
                        inc="dve")
                    p.op("vector", lambda e: e.tensor_tensor(
                        out=c5[:], in0=c3[:], in1=c4[:], op=ALU.subtract),
                        inc="dve")
                    p.op("scalar", lambda e: e.activation(
                        out=c0[:], in_=c5[:], func=AF.Sqrt, bias=cst_eps[:]),
                        inc="act", wait=("dve",))
                    p.op("vector", lambda e: e.reciprocal(c2[:], c0[:]),
                         inc="dve", wait=("act",))
                    p.op("vector", lambda e, x2t=x2t, h2=h2: e.tensor_scalar(
                        out=h2, in0=x2t, scalar1=c1[:], scalar2=c2[:],
                        op0=ALU.subtract, op1=ALU.mult), inc="dve")
                    p.op("vector", lambda e, h2=h2: e.tensor_tensor(
                        out=h2, in0=h2, in1=reps["cb1"][:], op=ALU.mult),
                        inc="dve")
                    p.op("vector", lambda e, h2=h2: e.tensor_tensor(
                        out=h2, in0=h2, in1=reps["sbm"][:], op=ALU.add),
                        inc="dve")
                    mx, sc, iv = cols[3], cols[4], cols[5]
                    p.op("vector", lambda e, h2=h2: e.reduce_max(
                        mx[:], h2, axis=AX.X, apply_absolute_value=True),
                        inc="dve")
                    p.op("vector", lambda e: e.tensor_single_scalar(
                        out=mx[:], in_=mx[:], scalar=1e-5, op=ALU.max),
                        inc="dve")
                    p.op("vector", lambda e: e.reciprocal(sc[:], mx[:]),
                         inc="dve")
                    p.op("vector", lambda e: e.tensor_scalar_mul(
                        out=sc[:], in0=sc[:], scalar1=127.0), inc="dve")
                    p.op("vector", lambda e: e.tensor_scalar_mul(
                        out=iv[:], in0=mx[:], scalar1=1.0 / 127.0), inc="dve")
                    p.op("vector", lambda e, nt=ntile: e.tensor_copy(
                        out=big_a[:, nt:nt + 1], in_=iv[:]), inc="dve")
                    p.op("vector", lambda e, q=q, h2=h2: e.tensor_scalar_mul(
                        out=q, in0=h2, scalar1=sc[:]), inc="dve")
                    p.op("vector", lambda e, q=q: e.tensor_scalar(
                        out=q, in0=q, scalar1=MAGIC, scalar2=MAGIC,
                        op0=ALU.add, op1=ALU.subtract), inc="dve")
                    p.op("vector", lambda e, q=q: e.tensor_scalar(
                        out=q, in0=q, scalar1=127.0, scalar2=-128.0,
                        op0=ALU.min, op1=ALU.max), inc="dve")
                    p.op("vector", lambda e, q=q: e.tensor_copy(
                        out=ht_h[:], in_=q), inc="dve",
                        wait=(("gp", ht_mark),))
                    p.op("gpsimd", lambda e, r0=r0: e.dma_start(
                        out=q1l_in[r0:r0 + 128, :], in_=ht_h[:]),
                        inc="gp", inck=16, wait=("dve",))
                    ht_mark = p.cnt["gp"]
                    ntile += 1
            ccn = new_cc()
            p.op("gpsimd", lambda e: e.collective_compute(
                "AllGather", ALU.bypass, replica_groups=G1,
                ins=[q1l_in], outs=[q1l_out]), inc=ccn, wait=("gp",))
            return ccn

        def ffn_main(q1cc):
            invcols = big_a
            q2ccs = []
            for blk in range(NB4):
                xqT = hT3
                for tt in range(4):
                    r0 = blk * 512 + tt * 128
                    p.op("sync", lambda e, r0=r0: e.dma_start(
                        out=ht_h[:], in_=q1l_out[r0:r0 + 128, :]),
                        inc="ld", inck=16, wait=((q1cc, 1), "dve", "te"))
                    p.op("vector", lambda e: e.tensor_copy(
                        out=xt[:], in_=ht_h[:]), inc="dve", wait=("ld",))
                    for kc in range(8):
                        p.op("tensor", lambda e, kc=kc: e.transpose(
                            out=ps_t[:, 0:128],
                            in_=xt[:, kc * 128:(kc + 1) * 128],
                            identity=ident_f[:]), inc="te", wait=("dve",))
                        p.op("vector", lambda e, kc=kc, tt=tt: e.tensor_copy(
                            out=xqT[:, kc, tt * 128:(tt + 1) * 128],
                            in_=ps_t[:, 0:128]), inc="dve", wait=("te",))
                er = xt[0:1, 0:512]
                for tt in range(4):
                    cidx = blk * 4 + tt
                    p.op("tensor", lambda e, cidx=cidx: e.transpose(
                        out=ps_t[0:1, 0:128], in_=invcols[:, cidx:cidx + 1],
                        identity=ident_f[:]), inc="te", wait=("dve",))
                    p.op("vector", lambda e, tt=tt: e.tensor_copy(
                        out=er[:, tt * 128:(tt + 1) * 128],
                        in_=ps_t[0:1, 0:128]), inc="dve", wait=("te",))
                epi_g = big_c[:, 0:512]
                epi_u = big_c[:, 512:1024]
                p.op("tensor", lambda e: e.matmul(
                    out=ps_r, lhsT=ones_f[:], rhs=er, start=True,
                    stop=True), inc="te", wait=("dve",))
                p.op("vector", lambda e: e.tensor_scalar_mul(
                    out=epi_g, in0=ps_r, scalar1=sw3[:, 0:1]),
                    inc="dve", wait=("te",))
                p.op("vector", lambda e: e.tensor_scalar_mul(
                    out=epi_u, in0=ps_r, scalar1=sw3[:, 1:2]), inc="dve")
                regs = (wst3, wstB3)
                rings = (("sync", "ld"), ("scalar", "ld2"))
                ld_marks = {}
                te0 = p.cnt["te"]
                pair_te = {}

                def wq_load(pi):
                    eng, sem = rings[pi % 2]
                    reg = regs[pi % 2]
                    p.op(eng, lambda e, pi=pi, reg=reg: e.dma_start(
                        out=reg[:, 0:32, :],
                        in_=wq_gu_f[:, 32 * pi:32 * pi + 32, :]),
                        inc=sem, inck=16,
                        wait=(("te", pair_te.get(pi - 2, te0)), "dve"),
                        sw=False)
                    ld_marks[pi] = (sem, p.cnt[sem])
                wq_load(0)
                wq_load(1)
                for mt in range(32):
                    pi = mt // 2
                    reg = regs[pi % 2]
                    base = (mt % 2) * 16
                    for kcc in range(8):
                        p.op("tensor", lambda e, kcc=kcc, base=base,
                             reg=reg: e.matmul(
                                 out=ps_a, lhsT=reg[:, base + kcc, :],
                                 rhs=xqT[:, kcc, 0:512], start=(kcc == 0),
                                 stop=(kcc == 7)),
                             inc="te" if kcc == 7 else None,
                             wait=(ld_marks[pi], "dve")
                             if kcc == 0 else ())
                    for kcc in range(8):
                        p.op("tensor", lambda e, kcc=kcc, base=base,
                             reg=reg: e.matmul(
                                 out=ps_b, lhsT=reg[:, base + 8 + kcc, :],
                                 rhs=xqT[:, kcc, 0:512], start=(kcc == 0),
                                 stop=(kcc == 7)),
                             inc="te" if kcc == 7 else None,
                             wait=("dve",) if kcc == 0 else ())
                    if mt % 2 == 1:
                        pair_te[pi] = p.cnt["te"]
                        if pi + 2 < 16:
                            wq_load(pi + 2)
                    p.op("vector", lambda e: e.tensor_tensor(
                        out=gtl, in0=ps_a, in1=epi_g, op=ALU.mult),
                        inc="dve", wait=("te",))
                    p.op("scalar", lambda e: e.activation(
                        out=gtl2, in_=gtl, func=AF.Silu), inc="act",
                        wait=("dve",))
                    p.op("vector", lambda e: e.tensor_tensor(
                        out=utl, in0=ps_b, in1=epi_u, op=ALU.mult),
                        inc="dve")
                    p.op("vector", lambda e, mt=mt: e.tensor_tensor(
                        out=gu3[:, mt, :], in0=gtl2, in1=utl,
                        op=ALU.mult), inc="dve",
                        wait=("act", "gp") if mt == 0 else ("act",))
                    if mt == 0:
                        p.op("vector", lambda e: e.tensor_copy(
                            out=rmx, in_=gu3[:, 0, :]), inc="dve")
                        p.op("vector", lambda e: e.tensor_copy(
                            out=rmn, in_=gu3[:, 0, :]), inc="dve")
                    else:
                        p.op("vector", lambda e, mt=mt: e.tensor_tensor(
                            out=rmx, in0=rmx, in1=gu3[:, mt, :],
                            op=ALU.max), inc="dve")
                        p.op("vector", lambda e, mt=mt: e.tensor_tensor(
                            out=rmn, in0=rmn, in1=gu3[:, mt, :],
                            op=ALU.min), inc="dve")
                p.op("vector", lambda e: e.tensor_scalar_mul(
                    out=rmn, in0=rmn, scalar1=-1.0), inc="dve")
                p.op("gpsimd", lambda e: e.tensor_reduce(
                    out=big_b[0:1, 0:512], in_=rmx,
                    axis=AX.C, op=ALU.max), inc="gpc", wait=("dve",))
                p.op("gpsimd", lambda e: e.tensor_reduce(
                    out=big_b[0:1, 512:1024], in_=rmn,
                    axis=AX.C, op=ALU.max), inc="gpc")
                p.op("vector", lambda e: e.tensor_tensor(
                    out=xt[0:1, 0:512], in0=big_b[0:1, 0:512],
                    in1=big_b[0:1, 512:1024], op=ALU.max),
                    inc="dve", wait=("gpc", "te"))
                p.op("vector", lambda e: e.tensor_single_scalar(
                    out=xt[0:1, 0:512], in_=xt[0:1, 0:512], scalar=1e-5,
                    op=ALU.max), inc="dve")
                p.op("vector", lambda e: e.tensor_scalar_mul(
                    out=xt[0:1, 512:1024], in0=xt[0:1, 0:512],
                    scalar1=1.0 / 127.0), inc="dve")
                p.op("vector", lambda e, blk=blk: e.tensor_copy(
                    out=invcols[0:1, 2048 + blk * 512:2048 + (blk + 1) * 512],
                    in_=xt[0:1, 512:1024]), inc="dve")
                p.op("vector", lambda e: e.reciprocal(
                    xt[0:1, 0:512], xt[0:1, 0:512]), inc="dve")
                p.op("vector", lambda e: e.tensor_scalar_mul(
                    out=xt[0:1, 0:512], in0=xt[0:1, 0:512], scalar1=127.0),
                    inc="dve")
                p.op("tensor", lambda e: e.matmul(
                    out=ps_r, lhsT=ones_f[:], rhs=xt[0:1, 0:512],
                    start=True, stop=True), inc="te", wait=("dve",))
                s2rep = big_c[:, 1024:1536]
                p.op("vector", lambda e: e.tensor_copy(
                    out=s2rep, in_=ps_r), inc="dve", wait=("te",))
                for mt in range(32):
                    qq = big_b[:, 0:512]
                    p.op("vector", lambda e, mt=mt: e.tensor_tensor(
                        out=qq, in0=gu3[:, mt, :], in1=s2rep, op=ALU.mult),
                        inc="dve")
                    p.op("vector", lambda e: e.tensor_scalar(
                        out=qq, in0=qq, scalar1=MAGIC, scalar2=MAGIC,
                        op0=ALU.add, op1=ALU.subtract), inc="dve")
                    p.op("vector", lambda e: e.tensor_scalar(
                        out=qq, in0=qq, scalar1=127.0, scalar2=-128.0,
                        op0=ALU.min, op1=ALU.max), inc="dve")
                    p.op("vector", lambda e, mt=mt: e.tensor_copy(
                        out=gu3[:, mt, :], in_=qq), inc="dve")
                p.op("gpsimd", lambda e, blk=blk: e.dma_start(
                    out=q2l_in[blk].rearrange("(p k) t -> p k t", k=32),
                    in_=gu3[:]), inc="gp", inck=16, wait=("dve",))
                ccn = new_cc()
                p.op("gpsimd", lambda e, blk=blk: e.collective_compute(
                    "AllGather", ALU.bypass, replica_groups=G1,
                    ins=[q2l_in[blk]], outs=[q2l_out[blk]]), inc=ccn,
                    wait=("gp",))
                q2ccs.append(ccn)
            for blk in range(NB4):
                dl_marks = {}
                te_d0 = p.cnt["te"]
                dve_d0 = p.cnt["dve"]
                dt_marks = {}

                def wd_load(h):
                    sl = (h % 2) * 16
                    mt_, hh = divmod(h, 2)
                    eng, sem = (("scalar", "ld2") if h < 2 or h % 2 == 1
                                else ("sync", "ld"))
                    p.op(eng, lambda e, mt_=mt_, hh=hh, sl=sl: e.dma_start(
                        out=wst3[:, sl:sl + 16, :],
                        in_=wq_d_d[:, mt_, hh * 16:(hh + 1) * 16, :]),
                        inc=sem, inck=16,
                        wait=(("te", dt_marks.get(h - 2, te_d0)),
                              ("dve", dve_d0)),
                        sw=False)
                    dl_marks[h] = (sem, p.cnt[sem])
                wd_load(0)
                wd_load(1)
                p.op("sync", lambda e, blk=blk: e.dma_start(
                    out=gu3[:],
                    in_=q2l_out[blk].rearrange("(p k) t -> p k t", k=32)),
                    inc="ld", inck=16,
                    wait=((q2ccs[blk], 1), "dve", "gp", "te"))
                p.op("tensor", lambda e, blk=blk: e.matmul(
                    out=ps_r, lhsT=ones_f[:],
                    rhs=invcols[0:1,
                                2048 + blk * 512:2048 + (blk + 1) * 512],
                    start=True, stop=True), inc="te", wait=("dve", "ld"))
                epi_d = big_c[:, 0:512]
                p.op("vector", lambda e: e.tensor_scalar_mul(
                    out=epi_d, in0=ps_r, scalar1=sw3[:, 2:3]),
                    inc="dve", wait=("te",))
                for mt in range(8):
                    for hh in range(2):
                        h = mt * 2 + hh
                        sl = (h % 2) * 16
                        for kc2 in range(16):
                            kcc = hh * 16 + kc2
                            p.op("tensor",
                                 lambda e, kcc=kcc, kc2=kc2, sl=sl: e.matmul(
                                     out=ps_a, lhsT=wst3[:, sl + kc2, :],
                                     rhs=gu3[:, kcc, :], start=(kcc == 0),
                                     stop=(kcc == 31)),
                                 inc="te" if kc2 == 15 else None,
                                 wait=(dl_marks[h], "dve")
                                 if kc2 == 0 else ())
                        dt_marks[h] = p.cnt["te"]
                        if h + 2 < 16:
                            wd_load(h + 2)
                    p.op("vector", lambda e, mt=mt: e.tensor_tensor(
                        out=ffn8[:, mt, :], in0=ps_a, in1=epi_d,
                        op=ALU.mult), inc="dve", wait=("te",))
                for tt in range(4):
                    r0 = blk * 512 + tt * 128
                    p.op("sync", lambda e, r0=r0: e.dma_start(
                        out=xt[:], in_=x2_d[r0:r0 + 128, :]),
                        inc="ld", inck=16, wait=("dve", "te", "gp"))
                    fft = big_c[:, 0:D]
                    for kc in range(8):
                        p.op("tensor", lambda e, kc=kc, tt=tt: e.transpose(
                            out=ps_t,
                            in_=ffn8[:, kc, tt * 128:(tt + 1) * 128],
                            identity=ident_f[:]), inc="te", wait=("dve",))
                        p.op("vector", lambda e, kc=kc: e.tensor_copy(
                            out=fft[:, kc * 128:(kc + 1) * 128], in_=ps_t),
                            inc="dve", wait=("te",))
                    x3 = big_c[:, D:2 * D]
                    p.op("vector", lambda e: e.tensor_tensor(
                        out=x3, in0=fft, in1=reps["gb"][:], op=ALU.mult),
                        inc="dve", wait=("ld",))
                    p.op("vector", lambda e: e.tensor_tensor(
                        out=x3, in0=x3, in1=xt[:], op=ALU.add), inc="dve")
                    p.op("gpsimd", lambda e, r0=r0: e.dma_start(
                        out=y_out[r0:r0 + 128, :], in_=x3),
                        inc="gp", inck=16, wait=("dve",))

        def dump(name, src):
            p.op("sync", lambda e, nm=name, s2=src: e.dma_start(
                out=dbg[nm], in_=s2),
                inc="ld", inck=16, wait=("dve", "act", "te", "gp"))

        setup()
        rs_sems = []
        ln1(0)
        for c in range(NCH):
            in_proj_conv(c)
            x_proj_reduce(c)
            if c + 1 < NCH:
                ln1(c + 1)
            dt_u_build(c)
            scan(c)
            if DBG and c == 0:
                dump("dbg_xt", xt[:])
                for ci in range(6):
                    p.op("sync", lambda e, ci=ci: e.dma_start(
                        out=dbg["dbg_cols"][:, ci:ci + 1], in_=cols[ci][:],
                        allow_slow_non_contiguous=True),
                        inc="ld", inck=16, wait=("dve", "act", "te", "gp"))
                dump("dbg_hT", hT[:])
                dump("dbg_xsr", xsr[:].rearrange("p a q -> p (a q)"))
                dump("dbg_xssi", xssi)
                dump("dbg_bct", bct[:])
                dump("dbg_dtr", xt[0:1, :Q])
                dump("dbg_dt", big_a[:])
                dump("dbg_u", u_h)
                dump("dbg_y", big_d[:])
                dump("dbg_yz", z_h)
                dump("dbg_carry", carry[:].rearrange("p a q -> p (a q)"))
                dump("dbg_ca1", reps["ca1"][:])
                dump("dbg_sa", reps["sa"][:])
                dump("dbg_ga", reps["ga"][:])
                dump("dbg_z", u_h)
                dump("dbg_asl", a_sl[:].rearrange("p a q -> p (a q)"))
            rs_sems.append(out_proj(c))
            if DBG and c == 0:
                p.op("sync", lambda e: e.dma_start(
                    out=dbg["dbg_aq"], in_=attn_q[0]),
                    inc="ld", inck=16, wait=((rs_sems[0], 1),))
        ffn_weight_quant()
        q1cc = ffn_quant1(rs_sems)
        ffn_main(q1cc)
        p.op("gpsimd", lambda e: e.engine_nop(), wait=("gp", "ld", "ld2"))

        sem_names = ["ld", "ld2", "gp", "te", "dve", "act", "gpc"] + cc_names
        with ExitStack() as sctx:
            H = {sn: sctx.enter_context(nc.semaphore(sn)) for sn in sem_names}
            with nc.Block() as block:
                @block.sync
                def _(e):
                    for fn in p.seq["sync"]:
                        fn(e, H)

                @block.scalar
                def _(e):
                    for fn in p.seq["scalar"]:
                        fn(e, H)

                @block.vector
                def _(e):
                    for fn in p.seq["vector"]:
                        fn(e, H)

                @block.tensor
                def _(e):
                    for fn in p.seq["tensor"]:
                        fn(e, H)

                @block.gpsimd
                def _(e):
                    for fn in p.seq["gpsimd"]:
                        fn(e, H)
    return nc


def make_inputs(core, L, Q, ins):
    import ml_dtypes
    g, j = core // 4, core % 4
    NCH = L // Q
    NQ4 = Q // 4
    sl = slice(j * DSL, (j + 1) * DSL)
    x = np.ascontiguousarray(np.asarray(ins["L0_attn_x"])[g])
    x_own = np.concatenate(
        [x[c * Q + j * NQ4: c * Q + (j + 1) * NQ4] for c in range(NCH)],
        axis=0)
    W = np.asarray(ins["L0_attn_in_proj_w"])
    d = {
        "x_b": x,
        "x_own": x_own,
        "w_in_xs": W.reshape(8, 128, 2 * DI)[:, :, sl].transpose(1, 0, 2),
        "w_in_z": W.reshape(8, 128, 2 * DI)
        [:, :, DI + j * DSL: DI + (j + 1) * DSL].transpose(1, 0, 2),
        "w_xp": np.asarray(ins["L0_attn_x_proj_w"])[sl]
        .reshape(NDB, 128, 129).transpose(1, 0, 2),
        "conv_w4": np.asarray(ins["L0_attn_conv_w"])[sl, 0, :]
        .reshape(NDB, 128, KC).transpose(1, 0, 2),
        "conv_b4": np.asarray(ins["L0_attn_conv_b"])[sl].reshape(NDB, 128).T,
        "dtw4": np.asarray(ins["L0_attn_dt_w"])[0, sl].reshape(NDB, 128).T,
        "dtb4": np.asarray(ins["L0_attn_dt_b"])[sl].reshape(NDB, 128).T,
        "alog4": np.asarray(ins["L0_attn_A_log"])[sl]
        .reshape(NDB, 128, DS).transpose(1, 0, 2),
        "d4": np.asarray(ins["L0_attn_D"])[sl].reshape(NDB, 128).T,
        "w_out_sl": np.asarray(ins["L0_attn_out_proj_w"])[sl]
        .reshape(NDB, 128, 8, 128).transpose(1, 0, 2, 3),
        "wg_t": np.asarray(ins["L0_ffn_w_gate"]).T
        .reshape(8, 128, 32, 128).transpose(1, 2, 0, 3),
        "wu_t": np.asarray(ins["L0_ffn_w_up"]).T
        .reshape(8, 128, 32, 128).transpose(1, 2, 0, 3),
        "wd_t": np.asarray(ins["L0_ffn_w_down"]).T
        .reshape(32, 128, 8, 128).transpose(1, 2, 0, 3),
        "c_row": np.asarray(ins["L0_c"])[g:g + 1],
        "wmod_r": np.asarray(ins["L0_wmod"]).reshape(8, 128, 6 * D)
        .transpose(1, 0, 2),
        "ident_h": np.eye(128, dtype=ml_dtypes.bfloat16),
    }
    return {k: np.ascontiguousarray(np.asarray(v, dtype=np.float32))
            if k != "ident_h" else v for k, v in d.items()}


def assemble(L, Q, results):
    NCH = L // Q
    NQ4 = Q // 4
    y = np.empty((2, L, D), np.float32)
    for core in range(8):
        g, j = core // 4, core % 4
        yo = results[core]["y_out"]
        for c in range(NCH):
            y[g, c * Q + j * NQ4: c * Q + (j + 1) * NQ4] = \
                yo[c * NQ4:(c + 1) * NQ4]
    return y


def kernel(**inputs):
    L, Q = 8192, 1024
    nc = build_nc(L, Q)
    in_maps = [make_inputs(c, L, Q, inputs) for c in range(8)]
    res = run_bass_kernel_spmd(nc, in_maps, core_ids=list(range(8)))
    return assemble(L, Q, res.results)
